# version 06
# baseline (speedup 1.0000x reference)
import math

import numpy as np
import ml_dtypes

import concourse.bass as bass
import concourse.mybir as mybir
from concourse.bass_utils import run_bass_kernel_spmd

F32 = mybir.dt.float32
BF16 = mybir.dt.bfloat16
ALU = mybir.AluOpType
ACT = mybir.ActivationFunctionType
AX = mybir.AxisListType

BF = ml_dtypes.bfloat16

S = 2048
D = 1024
NT = S // 128
NCH = D // 128
NEG = -1.0e30

N_CORES = 8



ENGINES = ("sync", "tensor", "vector", "scalar", "gpsimd")


class Em:

    def __init__(self, nc):
        self.nc = nc
        self.progs = {e: [] for e in ENGINES}
        self.counts = {}
        self.sems = {}
        self.sem_names = []

    def sem(self, name):
        if name not in self.counts:
            self.counts[name] = 0
            self.sem_names.append(name)
        return name

    def op(self, eng, fn, wait=(), inc=None, inc_by=1):
        tgt = None
        if inc is not None:
            self.sem(inc)
            self.counts[inc] += inc_by
            tgt = self.counts[inc]
        wait = tuple(wait)
        for s, _v in wait:
            self.sem(s)

        def thunk(e, fn=fn, wait=wait, inc=inc, inc_by=inc_by):
            for s, v in wait:
                e.wait_ge(self.sems[s], v)
            last = fn(e)
            if inc is not None:
                assert last is not None, "inc set but fn returned no instruction"
                last.then_inc(self.sems[inc], inc_by)

        self.progs[eng].append(thunk)
        return tgt

    def dma(self, eng, pairs, inc, wait=()):
        self.sem(inc)
        wait = tuple(wait)
        for s, _v in wait:
            self.sem(s)
        self.counts[inc] += 16 * len(pairs)
        tgt = self.counts[inc]

        def thunk(e, pairs=tuple(pairs), wait=wait, inc=inc):
            for s, v in wait:
                e.wait_ge(self.sems[s], v)
            for out_ap, in_ap in pairs:
                e.dma_start(out=out_ap, in_=in_ap).then_inc(self.sems[inc], 16)

        self.progs[eng].append(thunk)
        return tgt

    def count(self, sem):
        return self.counts.get(sem, 0)

    def tap_barrier(self, tgt):
        self.sem("ST")
        for e in ENGINES:
            self.progs[e].append(
                lambda eng, tgt=tgt: eng.wait_ge(self.sems["ST"], tgt))

    def replay(self):
        nc = self.nc
        import contextlib

        with contextlib.ExitStack() as stack:
            for name in self.sem_names:
                self.sems[name] = stack.enter_context(nc.semaphore(name))
            with nc.Block() as block:
                @block.sync
                def _(sync):
                    for t in self.progs["sync"]:
                        t(sync)

                @block.tensor
                def _(tensor):
                    for t in self.progs["tensor"]:
                        t(tensor)

                @block.vector
                def _(vector):
                    for t in self.progs["vector"]:
                        t(vector)

                @block.scalar
                def _(scalar):
                    for t in self.progs["scalar"]:
                        t(scalar)

                @block.gpsimd
                def _(gpsimd):
                    for t in self.progs["gpsimd"]:
                        t(gpsimd)



def make_constants():
    c = {}
    c["ident"] = np.eye(128, dtype=BF)
    c["ident_f32"] = np.eye(128, dtype=np.float32)
    c["ones_bf"] = np.ones((1, 128), dtype=BF)

    for r in range(2):
        j = np.arange(128)[:, None]
        i = np.arange(256)[None, :]
        m = (128 * r + j > i).astype(np.float32) * NEG
        c[f"mask_l0a_{r}"] = m.astype(BF)

    for rel in (-4, -3, 0, 1):
        j = np.arange(128)[:, None]
        i = np.arange(256)[None, :]
        qmk = (i) - (128 * rel + j)
        m = ((qmk < 0) | (qmk > 512)).astype(np.float32) * NEG
        c[f"mask_l0f_{rel}"] = m.astype(BF)

    for rel in (-1, 1):
        j = np.arange(128)[:, None]
        i = np.arange(128)[None, :]
        diff = (i) - (128 * rel + j)
        m = (np.abs(diff) > 128).astype(np.float32) * NEG
        c[f"mask_l1a_{rel}"] = m.astype(BF)

    DH = 64
    half = np.arange(0, DH, 2, dtype=np.float64)
    inv_freq = 1.0 / 10000.0 ** (half / DH)
    t = np.arange(S, dtype=np.float64)
    freqs = t[:, None] * inv_freq[None, :]
    freqs = np.concatenate([freqs, freqs], axis=1)
    base = (half + 0.4 * DH) / (1.4 * DH)
    power = (t - S // 2) / 256.0
    scale = base[None, :] ** power[:, None]
    scale = np.concatenate([scale, scale], axis=1)
    cos, sin = np.cos(freqs), np.sin(freqs)
    sgn = np.concatenate([-np.ones(32), np.ones(32)])[None, :]

    sm = np.zeros((128, 128), dtype=np.float32)
    for m in range(1, 128):
        sm[m - 1, m] = 1.0
    c["shiftmat"] = sm
    se = np.zeros((128, 128), dtype=np.float32)
    se[127, 0] = 1.0
    c["sel127"] = se

    def tokmaj(a):
        return np.ascontiguousarray(
            a.reshape(NT, 128, 64).transpose(1, 0, 2).reshape(128, NT * 64))
    c["rotq_cos"] = tokmaj(cos * scale * 8.0).astype(BF)
    c["rotq_sin"] = tokmaj(sin * scale * sgn * 8.0).astype(BF)
    c["rotk_cos"] = tokmaj(cos / scale).astype(np.float32)
    c["rotk_sin"] = tokmaj(sin / scale * sgn).astype(np.float32)
    return c



class TruncSchedule(Exception):
    pass


class Prog:
    def __init__(self, debug_taps=()):
        self.trunc = False
        nc = bass.Bass("TRN2", target_bir_lowering=False)
        self.nc = nc
        self.em = Em(nc)
        self.debug_taps = set(debug_taps)
        self.dbg_tensors = {}

        di = {}
        def din(name, shape):
            di[name] = nc.dram_tensor(name, list(shape), F32, kind="ExternalInput").ap()
        din("x_in", (S, D))
        din("L0_attn_Wk", (D, D)); din("L0_attn_bk", (D,))
        din("L0_attn_Wv", (D, 4096)); din("L0_attn_bv", (4096,))
        din("L0_attn_Wf", (4096, D)); din("L0_attn_bf", (D,))
        din("L0_ffn_wqkv", (2, D, 1536)); din("L0_ffn_q_scale", (2, 64))
        din("L0_ffn_k_scale", (2, 64)); din("L0_ffn_wo", (2, 512, D))
        din("L0_ffn_ln_g", (2, D)); din("L0_ffn_ln_b", (2, D))
        din("L0_ffn_w1", (2, D, 5460)); din("L0_ffn_b1", (2, 5460))
        din("L0_ffn_w2", (2, 2730, D)); din("L0_ffn_b2", (2, D))
        din("L1_attn_Wq", (D, D)); din("L1_attn_Wk", (D, D))
        din("L1_attn_Wv", (D, D)); din("L1_attn_Wo", (D, D))
        din("L1_ffn_w_gate", (4096, D)); din("L1_ffn_w_up", (4096, D))
        din("L1_ffn_w_down", (D, 4096))
        self.di = di

        self.consts = make_constants()
        self.dc = {}
        for k, v in self.consts.items():
            dt = BF16 if v.dtype == BF else F32
            self.dc[k] = nc.dram_tensor("c_" + k, list(v.shape), dt,
                                        kind="ExternalInput").ap()

        self.dr = {}
        for name, shape in (("bkT", (128, 8)), ("b1aT", (128, 44)),
                            ("b1gT", (128, 44)), ("lngT", (128, 16)),
                            ("lnbT", (128, 16)), ("b2T", (128, 16))):
            self.dr[name] = nc.dram_tensor("r_" + name, list(shape), F32,
                                           kind="ExternalInput").ap()

        self.d_out = nc.dram_tensor("x_out", [S, D], F32, kind="ExternalOutput").ap()

        self.d_xspill = nc.dram_tensor("xspill", [S, D], F32, kind="Internal").ap()
        self.d_zspill = nc.dram_tensor("zspill", [S, 4096], F32, kind="Internal").ap()
        self.d_kspill = nc.dram_tensor("kspill", [D, S], BF16, kind="Internal").ap()
        self.d_wgT = nc.dram_tensor("wgT", [D, 4096], BF16, kind="Internal").ap()
        self.d_wuT = nc.dram_tensor("wuT", [D, 4096], BF16, kind="Internal").ap()
        self.d_wdT = nc.dram_tensor("wdT", [4096, D], BF16, kind="Internal").ap()

        A = nc.alloc_sbuf_tensor
        self.xs = A("xs", [128, NT * D], F32)
        self.xbf = A("xbf", [128, 2 * D], BF16)
        self.ident = A("ident", [128, 128], BF16)
        self.ident_f32 = A("ident_f32", [128, 128], F32)
        self.ones_bf = A("ones_bf", [1, 128], BF16)
        self.mask = {}
        for k in self.consts:
            if k.startswith("mask_"):
                sh = self.consts[k].shape
                self.mask[k] = A(k, list(sh), BF16)

        self.bkT = A("bkT", [128, 8], F32)
        self.bv_row = A("bv_row", [1, 4096], BF16)
        self.bf_row = A("bf_row", [1, D], BF16)
        self.b2T = A("b2T", [128, 16], F32)
        self.b1aT = A("b1aT", [128, 2 * 22], F32)
        self.b1gT = A("b1gT", [128, 2 * 22], F32)
        self.lngT = A("lngT", [128, 2 * 8], F32)
        self.lnbT = A("lnbT", [128, 2 * 8], F32)
        self.qs_row = A("qs_row", [1, 128], F32)
        self.ks_row = A("ks_row", [1, 128], F32)
        self.shiftmat = A("shiftmat", [128, 128], F32)
        self.sel127 = A("sel127", [128, 128], F32)
        self.ones_f32 = A("ones_f32", [1, 128], F32)
        self.ones_col = A("ones_col", [128, 1], BF16)
        self.onescol_f32 = A("onescol_f32", [128, 1], F32)

        self.xT = A("xT", [128, NCH * S], BF16)
        self.arnA = A("arnA", [128, 3072], BF16)
        self.arnB = A("arnB", [128, 8224], BF16)
        self.arnC = A("arnC", [128, 2048], BF16)
        self.arnD = A("arnD", [128, 8192], BF16)
        self.wstage = A("wstage", [128, 2 * 2048], F32)
        self.wbf = A("wbf", [128, 12288], BF16)
        self.scr = A("scr", [128, 1088], F32)
        self.stat = A("stat", [128, 64], F32)

        P = nc.alloc_psum_tensor
        self.pb = [P(f"pb{i}", [128, 512], F32) for i in range(6)]
        self.pt = [P(f"pt{i}", [128, 1024], BF16) for i in range(2)]

        self.ew = self.em

    def dbg(self, name, shape, dtype=F32):
        if name in self.debug_taps and name not in self.dbg_tensors:
            self.dbg_tensors[name] = self.nc.dram_tensor(
                "dbg_" + name, list(shape), dtype, kind="ExternalOutput").ap()
        return self.dbg_tensors.get(name)


def bf(a):
    return np.ascontiguousarray(a, dtype=BF)


def f32(a):
    return np.ascontiguousarray(a, dtype=np.float32)



def emit_preamble(P: Prog):
    em = P.em
    pairs = [(P.xs[:].rearrange("p (t d) -> p t d", t=NT),
              P.di["x_in"].rearrange("(t p) d -> p t d", p=128))]
    pairs.append((P.ident[:], P.dc["ident"]))
    pairs.append((P.ident_f32[:], P.dc["ident_f32"]))
    pairs.append((P.ones_bf[:], P.dc["ones_bf"]))
    for k in P.mask:
        pairs.append((P.mask[k][:], P.dc[k]))
    pairs.append((P.bkT[:], P.dr["bkT"]))
    pairs.append((P.b1aT[:], P.dr["b1aT"]))
    pairs.append((P.b1gT[:], P.dr["b1gT"]))
    pairs.append((P.lngT[:], P.dr["lngT"]))
    pairs.append((P.lnbT[:], P.dr["lnbT"]))
    pairs.append((P.qs_row[:], P.di["L0_ffn_q_scale"].rearrange("l d -> (l d)")[None, :]))
    pairs.append((P.ks_row[:], P.di["L0_ffn_k_scale"].rearrange("l d -> (l d)")[None, :]))
    pairs.append((P.shiftmat[:], P.dc["shiftmat"]))
    pairs.append((P.sel127[:], P.dc["sel127"]))
    pairs.append((P.wstage[0:1, 0:4096], P.di["L0_attn_bv"][None, :]))
    pairs.append((P.scr[0:1, 0:1024], P.di["L0_attn_bf"][None, :]))
    pairs.append((P.b2T[:], P.dr["b2T"]))
    lw = em.dma("sync", pairs, inc="LW")

    def casts(v):
        v.tensor_copy(P.bv_row[:], P.wstage[0:1, 0:4096])
        v.tensor_copy(P.bf_row[:], P.scr[0:1, 0:1024])
        v.memset(P.ones_col[:], 1.0)
        v.memset(P.onescol_f32[:], 1.0)
        return v.memset(P.ones_f32[:], 1.0)
    em.op("vector", casts, wait=[("LW", lw)], inc="V")
    P.pre_LW = em.count("LW")
    P.pre_V = em.count("V")


def emit_xT(P: Prog, src_f32=None, dst=None, shift_cols=None, wait_extra=(),
            tag=""):
    em = P.em
    nc = P.nc
    src = src_f32 if src_f32 is not None else P.xs
    dst = dst if dst is not None else P.xT
    if shift_cols:
        def zf(v):
            last = None
            for c in shift_cols:
                last = v.memset(dst[:, 2048 * c: 2048 * c + 1], 0.0)
            return last
        em.op("vector", zf, wait=wait_extra, inc="V")
    for t in range(NT):
        half = t % 2
        def cast(v, t=t, half=half):
            return v.tensor_copy(P.xbf[:, half * D:(half + 1) * D],
                                 src[:, t * D:(t + 1) * D])
        w = list(wait_extra) if t == 0 else []
        if t >= 2:
            pass
        em.op("vector", cast, wait=w, inc="V")
        vt = em.count("V")

        def trans(te, t=t, half=half):
            last = None
            for c in range(NCH):
                last = te.transpose(
                    P.pt[half][:, 128 * c:128 * (c + 1)],
                    P.xbf[:, half * D + 128 * c: half * D + 128 * (c + 1)],
                    P.ident[:])
            return last
        em.op("tensor", trans, wait=[("V", vt)], inc="T")
        tt = em.count("T")

        def evac(v, t=t, half=half):
            last = None
            for c in range(NCH):
                off = 1 if (shift_cols and c in shift_cols) else 0
                if t == NT - 1 and off == 1:
                    last = v.tensor_copy(
                        dst[:, 2048 * c + 128 * t + 1: 2048 * c + 128 * t + 128],
                        P.pt[half][:, 128 * c: 128 * c + 127])
                else:
                    last = v.tensor_copy(
                        dst[:, 2048 * c + 128 * t + off:
                            2048 * c + 128 * t + 128 + off],
                        P.pt[half][:, 128 * c:128 * (c + 1)])
            return last
        em.op("vector", evac, wait=[("T", tt)], inc="V")
    return em.count("V"), em.count("T")


def emit_ln(P: Prog, wait_v=None, wait_t=None, wait_s=None, out_bf_cb=None,
            gamma_col=None, beta_col=None, src=None):
    em = P.em
    xs = src if src is not None else P.xs
    for t in range(NT):
        w = []
        if t == 0:
            if wait_v is not None:
                w.append(("V", wait_v))
            if wait_t is not None:
                w.append(("T", wait_t))
            if wait_s is not None:
                w.append(("SC", wait_s))
        em.op("vector",
              lambda v, t=t: v.reduce_sum(P.stat[:, t:t + 1],
                                          xs[:, t * D:(t + 1) * D], axis=AX.X),
              wait=w, inc="V")
        em.op("scalar",
              lambda s, t=t: s.activation(out=P.scr[:, 0:D],
                                          in_=xs[:, t * D:(t + 1) * D],
                                          func=ACT.Square,
                                          accum_out=P.stat[:, 16 + t:16 + t + 1]),
              wait=w, inc="SC")
    vstats = em.count("V")
    sstats = em.count("SC")

    def stats_math(v):
        v.drain()
        v.tensor_scalar_mul(P.stat[:, 32:48], P.stat[:, 0:16], 1.0 / D)
        v.tensor_scalar_mul(P.stat[:, 48:64], P.stat[:, 16:32], 1.0 / D)
        v.drain()
        v.tensor_tensor(P.scr[:, 0:16], P.stat[:, 32:48], P.stat[:, 32:48],
                        op=ALU.mult)
        v.drain()
        v.tensor_sub(P.stat[:, 48:64], P.stat[:, 48:64], P.scr[:, 0:16])
        v.drain()
        return v.tensor_scalar_add(P.stat[:, 48:64], P.stat[:, 48:64], 1e-5)
    em.op("vector", stats_math, wait=[("SC", sstats)], inc="V")
    vsm = em.count("V")
    if P.dbg("STAT2", (128, 64), F32) is not None:
        em.tap_barrier(em.dma("sync", [(P.dbg_tensors["STAT2"], P.stat[:])],
                       inc="ST", wait=[("V", vsm)]))

    def rsq(s):
        return s.activation(out=P.scr[:, 0:16], in_=P.stat[:, 48:64],
                            func=ACT.Sqrt)
    em.op("scalar", rsq, wait=[("V", vsm)], inc="SC")
    ssq = em.count("SC")

    def recip(v):
        v.reciprocal(P.stat[:, 48:64], P.scr[:, 0:16])
        v.drain()
        v.tensor_tensor(P.stat[:, 32:48], P.stat[:, 32:48], P.stat[:, 48:64],
                        op=ALU.mult)
        v.drain()
        return v.tensor_scalar_mul(P.stat[:, 32:48], P.stat[:, 32:48], -1.0)
    em.op("vector", recip, wait=[("SC", ssq)], inc="V")
    vrec = em.count("V")
    if P.dbg("STAT", (128, 64), F32) is not None:
        em.tap_barrier(em.dma("sync", [(P.dbg_tensors["STAT"], P.stat[:])],
                       inc="ST", wait=[("V", vrec)]))

    for t in range(NT):
        w = [("V", vrec)] if t == 0 else []

        def apply_f32(s, t=t):
            return s.activation(out=xs[:, t * D:(t + 1) * D],
                                in_=xs[:, t * D:(t + 1) * D], func=ACT.Identity,
                                bias=P.stat[:, 32 + t:33 + t],
                                scale=P.stat[:, 48 + t:49 + t])
        em.op("scalar", apply_f32, wait=w, inc="SC")
        st = em.count("SC")
        if out_bf_cb is not None:
            em.op("vector",
                  lambda v, t=t: v.tensor_copy(out_bf_cb(t),
                                               xs[:, t * D:(t + 1) * D]),
                  wait=[("SC", st)], inc="V")
    if gamma_col is not None:
        raise NotImplementedError("inner LN gamma/beta handled separately")
    return em.count("V"), em.count("SC")


def emit_l0_attn(P: Prog):
    em = P.em
    em.op("vector", lambda v: v.tensor_scalar_mul(P.xs[:], P.xs[:], 2.0),
          inc="V")
    nc = P.nc
    KT = P.arnA
    ptil_off = 2048
    Vb = P.arnB
    Gb = P.arnC
    GT = P.arnD
    wk_bf = P.wbf[:, 0:1024]
    wv_bf = P.wbf[:, 1024:1024 + 4096]
    wf_bf = P.wbf[:, 5120:5120 + 4096]

    x_in = P.di

    for p in range(8):
        wk_dram = P.di["L0_attn_Wk"].rearrange("(ic q) o -> q ic o", q=128)
        lw0 = em.dma("sync", [
            (P.wstage[:, 0:1024].rearrange("q (ic o) -> q ic o", ic=8),
             wk_dram[:, :, 128 * p:128 * (p + 1)]),
        ], inc="LW", wait=[("V", em.count("V"))])
        em.op("vector", lambda v: v.tensor_copy(wk_bf, P.wstage[:, 0:1024]),
              wait=[("LW", lw0), ("T", em.count("T"))], inc="V")
        vk = em.count("V")

        wv_dram = P.di["L0_attn_Wv"].rearrange("(ic q) o -> q ic o", q=128)
        for hhalf in range(2):
            lwv = em.dma("sync", [
                (P.wstage[:, 2048:2048 + 2048].rearrange("q (ic o) -> q ic o", ic=4),
                 wv_dram[:, 4 * hhalf:4 * hhalf + 4, 512 * p:512 * (p + 1)]),
            ], inc="LW", wait=[("V", em.count("V"))])
            em.op("vector",
                  lambda v, hh=hhalf: v.tensor_copy(
                      wv_bf[:, 2048 * hh:2048 * (hh + 1)],
                      P.wstage[:, 2048:2048 + 2048]),
                  wait=[("LW", lwv), ("T", em.count("T"))], inc="V")
        vv = em.count("V")

        wf_dram = P.di["L0_attn_Wf"].rearrange("(ic q) o -> q ic o", q=128)
        lwf = em.dma("sync", [
            (P.wstage[:, 0:2048].rearrange("q (ic o) -> q ic o", ic=2),
             wf_dram[:, 4 * p: 4 * p + 2, :]),
            (P.wstage[:, 2048:4096].rearrange("q (ic o) -> q ic o", ic=2),
             wf_dram[:, 4 * p + 2: 4 * p + 4, :]),
        ], inc="LW", wait=[("V", vk)])
        em.op("vector", lambda v: v.tensor_copy(wf_bf, P.wstage[:, 0:4096]),
              wait=[("LW", lwf), ("T", em.count("T"))], inc="V")
        vwf = em.count("V")

        if p == 0:
            em.op("vector",
                  lambda v: v.memset(
                      Vb[:, 0:NT * 514]
                      .rearrange("q (t h c) -> q t h c", t=NT, h=2)[:, :, :, 256:257],
                      1.0),
                  inc="V")

        tprev = em.count("T")

        def kproj(te, vk=vk):
            last = None
            for ic in range(NCH):
                for ns in range(4):
                    last = te.matmul(
                        P.pb[ns][:, 0:512],
                        wk_bf[:, 128 * ic:128 * (ic + 1)],
                        P.xT[:, 2048 * ic + 512 * ns: 2048 * ic + 512 * (ns + 1)],
                        start=(ic == 0), stop=(ic == NCH - 1))
            return last
        em.op("tensor", kproj, wait=[("V", vk)], inc="T")
        tk = em.count("T")
        for ns in range(4):
            em.op("scalar",
                  lambda s, ns=ns, p=p: s.activation(
                      out=KT[:, 512 * ns:512 * (ns + 1)],
                      in_=P.pb[ns][:, 0:512], func=ACT.Identity,
                      bias=P.bkT[:, p:p + 1]),
                  wait=[("T", tk)] if ns == 0 else [], inc="SC")
        sck = em.count("SC")

        def vproj_t(te, t, vwait, p=p):
            def f(te_, t=t, p=p):
                last = None
                bank = P.pb[4 + (t % 2)]
                for ic in range(NCH):
                    last = te_.matmul(
                        bank[:, 0:512],
                        P.xT[:, 2048 * ic + 128 * t: 2048 * ic + 128 * (t + 1)],
                        wv_bf[:, 512 * ic:512 * (ic + 1)],
                        start=(ic == 0), stop=False)
                last = te_.matmul(bank[:, 0:512], P.ones_bf[:],
                                  P.bv_row[:, 512 * p:512 * (p + 1)],
                                  start=False, stop=True)
                return last
            return f

        for t in range(NT):
            w = [("V", vv)] if t == 0 else []
            if t >= 2:
                w.append(("SC", em.count("SC")))
            em.op("tensor", vproj_t(None, t, None), wait=w, inc="T")
            tv = em.count("T")
            em.op("scalar",
                  lambda s, t=t: s.copy(
                      Vb[:, 514 * t:514 * t + 514]
                      .rearrange("q (h c) -> q h c", h=2)[:, :, 0:256],
                      P.pb[4 + (t % 2)][:, 0:512]
                      .rearrange("q (h c) -> q h c", h=2)),
                  wait=[("T", tv)], inc="SC")
        scv = em.count("SC")

        if p == 0:
            if P.dbg("KT", (128, 2048), BF16) is not None:
                em.tap_barrier(em.dma("sync", [(P.dbg_tensors["KT"], KT[:, 0:2048])],
                       inc="ST", wait=[("SC", sck)]))
            if P.dbg("V0", (128, 8224), BF16) is not None:
                em.tap_barrier(em.dma("sync", [(P.dbg_tensors["V0"], Vb[:, 0:8224])],
                       inc="ST", wait=[("SC", scv)]))

        def obank(i):
            if i < 2:
                return P.pb[4 + i][:]
            return P.pt[i - 2][:].bitcast(F32)
        def score_mm(te, qp, kb, slot, p=p):
            rel = kb - 2 * qp
            last = None
            for h in range(2):
                bank = P.pb[2 * slot + h]
                lhsT = KT[64 * h:64 * (h + 1), 128 * kb:128 * (kb + 1)]
                rhs = P.xT[64 * h:64 * (h + 1),
                           2048 * p + 256 * qp: 2048 * p + 256 * (qp + 1)]
                last = te.matmul(bank[:, 0:256], lhsT, rhs,
                                 start=True, stop=(rel < 0))
                if rel >= 0:
                    last = te.matmul(bank[:, 0:256], P.ident[:],
                                     P.mask[f"mask_l0a_{rel}"][:],
                                     start=False, stop=True)
            return last

        for qp in range(8):
            nkb = 2 * qp + 2
            sc_exp = {}
            t_pv_done_for_slot = {}
            for kb in range(nkb):
                slot = kb % 2
                w = []
                if kb >= 2:
                    w.append(("SC", sc_exp[kb - 2]))
                elif qp > 0:
                    w.append(("SC", em.count("SC")))
                    w.append(("V", em.count("V")))
                else:
                    w.append(("SC", scv))
                em.op("tensor", lambda te, qp=qp, kb=kb, slot=slot,
                      f=score_mm: f(te, qp, kb, slot), wait=w, inc="T")
                ts = em.count("T")
                if p == 0 and qp == 0 and kb == 0 and \
                        P.dbg("SC00", (128, 768), F32) is not None:
                    def ctl(te):
                        te.matmul(P.pb[2][:, 0:256],
                                  P.ident[:], P.mask["mask_l0a_0"][:],
                                  start=True, stop=True)
                        return te.matmul(P.pb[3][:, 0:256],
                                         KT[:, 0:128], P.xT[:, 0:256],
                                         start=True, stop=True)
                    em.op("tensor", ctl, wait=[("T", ts)], inc="T")
                    tc = em.count("T")
                    em.op("vector", lambda v: (
                        v.tensor_copy(P.scr[:, 0:256], P.pb[0][:, 0:256]),
                        v.tensor_copy(P.scr[:, 256:512], P.pb[2][:, 0:256]),
                        v.tensor_copy(P.scr[:, 512:768], P.pb[3][:, 0:256]))[-1],
                        wait=[("T", tc)], inc="V")
                    em.tap_barrier(em.dma("sync",
                        [(P.dbg_tensors["SC00"], P.scr[:, 0:768])],
                        inc="ST", wait=[("V", em.count("V"))]))
                    if P.trunc == 1:
                        raise TruncSchedule()
                def do_exp(s, slot=slot, kb=kb):
                    last = None
                    for h in range(2):
                        last = s.activation(
                            out=P.arnA[:, ptil_off + 256 * (2 * h + slot):
                                       ptil_off + 256 * (2 * h + slot + 1)],
                            in_=P.pb[2 * slot + h][:, 0:256], func=ACT.Exp)
                    return last
                we = [("T", ts)]
                if kb >= 2:
                    we = [("T", max(ts, t_pv_done_for_slot.get(slot, 0)))]
                em.op("scalar", do_exp, wait=we, inc="SC")
                sc_exp[kb] = em.count("SC")
                if p == 0 and qp == 0 and kb == 0 and \
                        P.dbg("PT00", (128, 1024), BF16) is not None:
                    em.tap_barrier(em.dma("sync", [(P.dbg_tensors["PT00"],
                                     P.arnA[:, 2048:3072])],
                           inc="ST", wait=[("SC", sc_exp[kb])]))
                def do_pv(te, qp=qp, kb=kb, slot=slot):
                    last = None
                    for h in range(2):
                        ptile = P.arnA[:, ptil_off + 256 * (2 * h + slot):
                                       ptil_off + 256 * (2 * h + slot + 1)]
                        for qq in range(2):
                            qb = 2 * qp + qq
                            if kb > qb:
                                continue
                            last = te.matmul(
                                obank(2 * h + qq)[:, 0:257],
                                ptile[:, 128 * qq:128 * (qq + 1)],
                                Vb[:, 514 * kb + 257 * h:
                                   514 * kb + 257 * (h + 1)],
                                start=(kb == 0), stop=(kb == qb))
                    return last
                em.op("tensor", do_pv, wait=[("SC", sc_exp[kb])], inc="T")
                t_pv_done_for_slot[slot] = em.count("T")
            tpv = em.count("T")
            def recips(v, qp=qp):
                last = None
                for h in range(2):
                    for qq in range(2):
                        last = v.reciprocal(P.stat[:, 40 + 2 * h + qq:
                                                   41 + 2 * h + qq],
                                            obank(2 * h + qq)[:, 256:257])
                return last
            em.op("vector", recips,
                  wait=[("T", tpv), ("SC", em.count("SC"))], inc="V")
            vr = em.count("V")

            def gelu_evac(s, qp=qp):
                last = None
                for h in range(2):
                    for qq in range(2):
                        qb = 2 * qp + qq
                        last = s.activation(
                            out=Gb[:, 512 * (qb % 4) + 256 * h:
                                   512 * (qb % 4) + 256 * (h + 1)],
                            in_=obank(2 * h + qq)[:, 0:256],
                            func=ACT.Gelu,
                            scale=P.stat[:, 40 + 2 * h + qq:41 + 2 * h + qq])
                return last
            em.op("scalar", gelu_evac, wait=[("V", vr)], inc="SC")
            sc_g = em.count("SC")
            if p == 0 and qp == 1 and \
                    P.dbg("G03", (128, 2048), BF16) is not None:
                em.tap_barrier(em.dma("sync", [(P.dbg_tensors["G03"], P.arnC[:, 0:2048])],
                       inc="ST", wait=[("SC", sc_g)]))

            for qq in range(2):
                qb = 2 * qp + qq

                def trans(te, qb=qb, qq=qq):
                    last = None
                    for c in range(4):
                        last = te.transpose(
                            P.pb[4 + qq][:].bitcast(BF16)[:, 128 * c:128 * (c + 1)],
                            Gb[:, 512 * (qb % 4) + 128 * c:
                               512 * (qb % 4) + 128 * (c + 1)],
                            P.ident[:])
                    return last
                em.op("tensor", trans, wait=[("SC", sc_g)], inc="T")
                tt = em.count("T")
                em.op("vector",
                      lambda v, qb=qb, qq=qq: v.tensor_copy(
                          GT[:].rearrange("q (c s) -> q c s", c=4)
                          [:, :, 128 * qb:128 * (qb + 1)],
                          P.pb[4 + qq][:].bitcast(BF16)[:, 0:512]
                          .rearrange("q (c s) -> q c s", c=4)),
                      wait=[("T", tt)], inc="V")
        vgt = em.count("V")

        if p == 0 and P.dbg("GT0", (128, 8192), BF16) is not None:
            em.tap_barrier(em.dma("sync", [(P.dbg_tensors["GT0"], GT[:, 0:8192])],
                   inc="ST", wait=[("V", vgt)]))

        for t in range(NT):
            w = [("V", max(vgt, vwf))] if t == 0 else []
            if t >= 2:
                w.append(("V", em.count("V")))

            def wfmm(te, t=t, p=p):
                last = None
                for c in range(4):
                    lhsT = GT[:, 2048 * c + 128 * t: 2048 * c + 128 * (t + 1)]
                    for half in range(2):
                        last = te.matmul(
                            P.pb[2 * (t % 2) + half][:, 0:512],
                            lhsT,
                            wf_bf[:, 1024 * c + 512 * half:
                                  1024 * c + 512 * (half + 1)],
                            start=(c == 0), stop=(c == 3 and p != 7))
                if p == 7:
                    for half in range(2):
                        last = te.matmul(
                            P.pb[2 * (t % 2) + half][:, 0:512],
                            P.ones_bf[:],
                            P.bf_row[:, 512 * half:512 * (half + 1)],
                            start=False, stop=True)
                return last
            em.op("tensor", wfmm, wait=w, inc="T")
            tw = em.count("T")

            def add_evac(v, t=t):
                v.tensor_add(P.xs[:, D * t: D * t + 512],
                             P.xs[:, D * t: D * t + 512],
                             P.pb[2 * (t % 2)][:, 0:512])
                return v.tensor_add(P.xs[:, D * t + 512: D * (t + 1)],
                                    P.xs[:, D * t + 512: D * (t + 1)],
                                    P.pb[2 * (t % 2) + 1][:, 0:512])
            em.op("vector", add_evac, wait=[("T", tw)], inc="V")
        if P.trunc == 3 and p == 0:
            raise TruncSchedule()
        if P.trunc == 4 and p == 1:
            raise TruncSchedule()
    return em.count("V")


def emit_store(P: Prog):
    em = P.em
    wv, wsc = em.count("V"), em.count("SC")
    st = em.dma("sync", [
        (P.d_out.rearrange("(t p) d -> p t d", p=128),
         P.xs[:].rearrange("p (t d) -> p t d", t=NT)),
    ], inc="ST", wait=[("V", wv), ("SC", wsc), ("T", em.count("T"))])

    em.progs["sync"].append(lambda e, st=st: e.wait_ge(em.sems["ST"], st))


def build_program(n_stages=99, debug_taps=()):
    import os
    P = Prog(debug_taps)
    P.trunc = int(os.environ.get("KTRUNC", "0"))
    emit_preamble(P)
    vxt, txt = emit_xT(P, wait_extra=[("LW", P.pre_LW), ("V", P.pre_V)])
    if P.dbg("XT", (128, NCH * S), BF16) is not None:
        P.em.dma("sync", [(P.dbg_tensors["XT"], P.xT[:])],
                 inc="ST", wait=[("V", vxt)])
    try:
        emit_l0_attn(P)
    except TruncSchedule:
        emit_store(P)
        P.em.replay()
        return P
    if P.dbg("Y", (128, NT * D), F32) is not None:
        P.em.tap_barrier(P.em.dma("sync", [(P.dbg_tensors["Y"], P.xs[:])],
                         inc="ST", wait=[("V", P.em.count("V"))]))
    kbar = os.environ.get("KBAR", "")
    if kbar:
        em = P.em
        em.op("vector", lambda v: v.engine_nop(),
              wait=[("SC", em.count("SC")), ("T", em.count("T"))], inc="V")
        vb = em.count("V")
        for e in kbar.split(","):
            if e and e != "vector":
                em.op(e, lambda x: None, wait=[("V", vb)])
    emit_ln(P, wait_v=P.em.count("V"))
    if n_stages >= 2 and "emit_l0_ffn" in globals():
        emit_l0_ffn(P)
    if n_stages >= 3 and "emit_l1_attn" in globals():
        emit_l1_attn(P)
    if n_stages >= 4 and "emit_l1_ffn" in globals():
        emit_l1_ffn(P)
    emit_store(P)
    P.em.replay()
    return P



_CACHE = {}


def _repack_params(inputs):
    r = {}
    def cols(v, pad_to=None):
        v = np.asarray(v, dtype=np.float32).reshape(-1)
        n = v.shape[0]
        ncol = -(-n // 128)
        if pad_to is not None:
            ncol = pad_to
        out = np.zeros((128, ncol), dtype=np.float32)
        full = v[: (n // 128) * 128].reshape(-1, 128).T
        out[:, : full.shape[1]] = full
        rem = n - (n // 128) * 128
        if rem:
            out[:rem, full.shape[1]] = v[(n // 128) * 128:]
        return out
    r["bkT"] = cols(inputs["L0_attn_bk"])
    b1 = np.asarray(inputs["L0_ffn_b1"], dtype=np.float32)
    r["b1aT"] = np.concatenate([cols(b1[l][:2730], 22) for l in range(2)], axis=1)
    r["b1gT"] = np.concatenate([cols(b1[l][2730:], 22) for l in range(2)], axis=1)
    r["lngT"] = np.concatenate(
        [cols(np.asarray(inputs["L0_ffn_ln_g"])[l]) for l in range(2)], axis=1)
    r["lnbT"] = np.concatenate(
        [cols(np.asarray(inputs["L0_ffn_ln_b"])[l]) for l in range(2)], axis=1)
    r["b2T"] = np.concatenate(
        [cols(np.asarray(inputs["L0_ffn_b2"])[l]) for l in range(2)], axis=1)
    return r


def _get_prog(n_stages=99, debug_taps=()):
    key = (n_stages, tuple(sorted(debug_taps)))
    if key not in _CACHE:
        _CACHE[key] = build_program(n_stages, debug_taps)
    return _CACHE[key]


def run_cores(inputs, n_stages=99, debug_taps=(), core_ids=None):
    P = _get_prog(n_stages, debug_taps)
    x = np.ascontiguousarray(np.asarray(inputs["L0_attn_x"], dtype=np.float32))
    reps = _repack_params(inputs)
    consts = P.consts
    base = {}
    for k, v in inputs.items():
        if k == "L0_attn_x":
            continue
        base[k] = np.ascontiguousarray(np.asarray(v, dtype=np.float32))
    in_maps = []
    cores = core_ids if core_ids is not None else list(range(N_CORES))
    for b in range(len(cores)):
        m = {"x_in": x[cores[b]]}
        m.update(base)
        for k, v in reps.items():
            m["r_" + k] = v
        for k, v in consts.items():
            m["c_" + k] = np.ascontiguousarray(v)
        in_maps.append(m)
    res = run_bass_kernel_spmd(P.nc, in_maps, core_ids=cores)
    return P, res


def kernel(**inputs):
    P, res = run_cores(inputs)
    out = np.stack([np.asarray(r["x_out"], dtype=np.float32)
                    for r in res.results], axis=0)
    return out


def emit_l0_ffn(P: Prog):
    em = P.em
    nc = P.nc

    sp = em.dma("sync", [(P.d_xspill.rearrange("(t p) d -> p t d", p=128),
                          P.xs[:].rearrange("p (t d) -> p t d", t=NT))],
                inc="SP", wait=[("V", em.count("V")), ("SC", em.count("SC"))])

    for l in range(2):
        emit_l0_ffn_layer(P, l, sp)

    for t in range(NT):
        lw = em.dma("sync",
                    [(P.wstage[:, 1024 * (t % 2):1024 * (t % 2) + 1024],
                      P.d_xspill[128 * t:128 * (t + 1), :])],
                    inc="LW", wait=[("SP", sp), ("V", em.count("V"))])
        em.op("vector",
              lambda v, t=t: v.tensor_add(
                  P.xs[:, D * t:D * (t + 1)], P.xs[:, D * t:D * (t + 1)],
                  P.wstage[:, 1024 * (t % 2):1024 * (t % 2) + 1024]),
              wait=[("LW", lw)], inc="V")
    emit_ln(P, wait_v=em.count("V"))


def emit_l0_ffn_layer(P: Prog, l: int, spill_tgt):
    em = P.em
    F = 64 * NT

    emit_xT(P, src_f32=P.xs, dst=P.xT, shift_cols=(4, 5, 6, 7),
            wait_extra=[("V", em.count("V")), ("SC", em.count("SC")),
                        ("T", em.count("T"))])
    vh = em.count("V")

    wq_dram = P.di["L0_ffn_wqkv"][l].rearrange("(ic q) o -> q ic o", q=128)
    for ic in range(NCH):
        lw = em.dma("sync", [(P.wstage[:, 1024 * (ic % 2):1024 * (ic % 2) + 1024],
                              wq_dram[:, ic, 0:1024])],
                    inc="LW", wait=[("V", em.count("V"))])
        em.op("vector",
              lambda v, ic=ic: v.tensor_copy(
                  P.wbf[:, 1024 * ic:1024 * (ic + 1)],
                  P.wstage[:, 1024 * (ic % 2):1024 * (ic % 2) + 1024]),
              wait=[("LW", lw), ("T", em.count("T"))], inc="V")
    vw1 = em.count("V")

    def qsks_row(v, l=l):
        v.drain()
        return v.tensor_tensor(P.scr[0:1, 1024:1088],
                               P.qs_row[:, 64 * l:64 * (l + 1)],
                               P.ks_row[:, 64 * l:64 * (l + 1)], op=ALU.mult)
    em.op("vector", qsks_row, wait=[("LW", em.count("LW"))], inc="V")
    vq = em.count("V")
    em.op("tensor",
          lambda te: te.matmul(P.pb[5][:, 0:64], P.ones_f32[:],
                               P.scr[0:1, 1024:1088].bitcast(F32),
                               start=True, stop=True),
          wait=[("V", vq)], inc="T")
    tq = em.count("T")
    em.op("scalar", lambda s: s.copy(P.scr[:, 1024:1088], P.pb[5][:, 0:64]),
          wait=[("T", tq)], inc="SC")
    scq = em.count("SC")

    em.dma("sync", [(P.wbf[:, 8192:9216], P.dc["rotq_cos"]),
                    (P.wbf[:, 9216:10240], P.dc["rotq_sin"])],
           inc="LW", wait=[("T", em.count("T")), ("V", em.count("V"))])
    for j, nm in enumerate(("rotk_cos", "rotk_sin")):
        lw = em.dma("sync", [(P.wstage[:, 0:1024], P.dc[nm])],
                    inc="LW", wait=[("V", em.count("V"))])
        em.op("vector",
              lambda v, j=j: v.tensor_tensor(
                  P.arnA[:, 1024 * j:1024 * (j + 1)]
                  .rearrange("p (t d) -> p t d", t=NT),
                  P.wstage[:, 0:1024].rearrange("p (t d) -> p t d", t=NT),
                  P.scr[:, 1024:1088][:, None, :].broadcast_to([128, NT, 64]),
                  op=ALU.mult),
              wait=[("LW", lw), ("SC", scq)], inc="V")
    vkt = em.count("V")

    sc_sqrt = {}
    for t in range(NT):
        bq, bk = P.pb[2 * (t % 2)], P.pb[2 * (t % 2) + 1]
        w = [("V", max(vh, vw1, vkt))] if t == 0 else []
        if t >= 2:
            w.append(("V", em.count("V")))

        def qkmm(te, t=t, bq=bq, bk=bk):
            last = None
            for ic in range(NCH):
                lhsT = P.xT[:, 2048 * ic + 128 * t:2048 * ic + 128 * (t + 1)]
                last = te.matmul(bq[:, 0:512], lhsT,
                                 P.wbf[:, 1024 * ic:1024 * ic + 512],
                                 start=(ic == 0), stop=(ic == NCH - 1))
                last = te.matmul(bk[:, 0:512], lhsT,
                                 P.wbf[:, 1024 * ic + 512:1024 * (ic + 1)],
                                 start=(ic == 0), stop=(ic == NCH - 1))
            return last
        em.op("tensor", qkmm, wait=w, inc="T")
        tqk = em.count("T")

        def sq(s, bq=bq, bk=bk):
            s.activation(out=P.scr[:, 0:512], in_=bq[:, 0:512], func=ACT.Square)
            return s.activation(out=P.scr[:, 512:1024], in_=bk[:, 0:512],
                                func=ACT.Square)
        em.op("scalar", sq, wait=[("T", tqk)], inc="SC")
        s0 = em.count("SC")

        def vchain1(v):
            v.tensor_reduce(P.stat[:, 0:8],
                            P.scr[:, 0:512].rearrange("p (h d) -> p h d", h=8),
                            axis=AX.X, op=ALU.add)
            return v.tensor_reduce(P.stat[:, 8:16],
                                   P.scr[:, 512:1024].rearrange("p (h d) -> p h d", h=8),
                                   axis=AX.X, op=ALU.add)
        em.op("vector", vchain1, wait=[("SC", s0)], inc="V")
        v1 = em.count("V")
        em.op("scalar",
              lambda s: s.activation(out=P.stat[:, 16:32], in_=P.stat[:, 0:16],
                                     func=ACT.Sqrt),
              wait=[("V", v1)], inc="SC")
        s1 = em.count("SC")

        def vchain2(v, t=t, bq=bq, bk=bk, l=l):
            v.tensor_scalar_max(P.stat[:, 16:32], P.stat[:, 16:32], 1e-12)
            v.drain()
            v.reciprocal(P.stat[:, 32:48], P.stat[:, 16:32])
            v.tensor_tensor(P.scr[:, 0:512].rearrange("p (h d) -> p h d", h=8),
                            bq[:, 0:512].rearrange("p (h d) -> p h d", h=8),
                            P.wbf[:, 8192 + 64 * t:8192 + 64 * (t + 1)][:, None, :]
                            .broadcast_to([128, 8, 64]), op=ALU.mult)
            v.tensor_tensor(P.scr[:, 512:1024].rearrange("p (h d) -> p h d", h=8),
                            bk[:, 0:512].rearrange("p (h d) -> p h d", h=8),
                            P.arnA[:, 64 * t:64 * (t + 1)][:, None, :]
                            .bitcast(BF16)[:, :, :].broadcast_to([128, 8, 64])
                            if False else
                            P.arnA[:, 64 * t:64 * (t + 1)][:, None, :]
                            .broadcast_to([128, 8, 64]), op=ALU.mult)
            swq = bq[:, 0:512].rearrange("p (h s d) -> p h s d", h=8, s=2)[:, :, ::-1, :]
            swk = bk[:, 0:512].rearrange("p (h s d) -> p h s d", h=8, s=2)[:, :, ::-1, :]
            v.tensor_tensor(P.wstage[:, 0:512].rearrange("p (h s d) -> p h s d", h=8, s=2),
                            swq,
                            P.wbf[:, 9216 + 64 * t:9216 + 64 * (t + 1)][:, None, :]
                            .broadcast_to([128, 8, 64])
                            .rearrange("p h (s d) -> p h s d", s=2), op=ALU.mult)
            v.tensor_tensor(P.wstage[:, 512:1024].rearrange("p (h s d) -> p h s d", h=8, s=2),
                            swk,
                            P.arnA[:, 1024 + 64 * t:1024 + 64 * (t + 1)][:, None, :]
                            .broadcast_to([128, 8, 64])
                            .rearrange("p h (s d) -> p h s d", s=2), op=ALU.mult)
            v.drain()
            v.tensor_add(P.scr[:, 0:512], P.scr[:, 0:512], P.wstage[:, 0:512])
            v.tensor_add(P.scr[:, 512:1024], P.scr[:, 512:1024],
                         P.wstage[:, 512:1024])
            v.drain()
            v.tensor_tensor(P.arnC[:, 1024:1536].rearrange("p (h d) -> p h d", h=8),
                            P.scr[:, 0:512].rearrange("p (h d) -> p h d", h=8),
                            P.stat[:, 32:40][:, :, None].broadcast_to([128, 8, 64]),
                            op=ALU.mult)
            v.tensor_tensor(P.arnC[:, 1536:2048].rearrange("p (h d) -> p h d", h=8),
                            P.scr[:, 512:1024].rearrange("p (h d) -> p h d", h=8),
                            P.stat[:, 40:48][:, :, None].broadcast_to([128, 8, 64]),
                            op=ALU.mult)
            v.drain()
            v.tensor_tensor(P.scr[:, 0:512], P.arnC[:, 1024:1536],
                            P.arnC[:, 1536:2048], op=ALU.mult)
            v.drain()
            v.tensor_reduce(P.stat[:, 16:24],
                            P.scr[:, 0:512].rearrange("p (h d) -> p h d", h=8),
                            axis=AX.X, op=ALU.add)
            v.drain()
            if t == 0:
                v.tensor_copy(P.stat[:, 48:56], P.stat[:, 16:24])
                return v.tensor_copy(P.stat[:, 56:64], P.stat[:, 16:24])
            v.tensor_tensor(P.stat[:, 48:56], P.stat[:, 48:56],
                            P.stat[:, 16:24], op=ALU.min)
            return v.tensor_tensor(P.stat[:, 56:64], P.stat[:, 56:64],
                                   P.stat[:, 16:24], op=ALU.max)
        em.op("vector", vchain2, wait=[("SC", s1)], inc="V")
        v2 = em.count("V")

        def trans(te, t=t):
            last = None
            for c in range(4):
                last = te.transpose(
                    P.pt[0][:, 128 * c:128 * (c + 1)],
                    P.arnC[:, 1024 + 128 * c:1024 + 128 * (c + 1)], P.ident[:])
            for c in range(4):
                last = te.transpose(
                    P.pt[1][:, 128 * c:128 * (c + 1)],
                    P.arnC[:, 1536 + 128 * c:1536 + 128 * (c + 1)], P.ident[:])
            return last
        em.op("tensor", trans, wait=[("V", v2)], inc="T")
        tt = em.count("T")

        def evac(v, t=t):
            v.tensor_copy(
                P.arnD[:, 0:8192].rearrange("p (c s) -> p c s", c=4)
                [:, :, 128 * t:128 * (t + 1)],
                P.pt[0][:, 0:512].rearrange("p (c s) -> p c s", c=4))
            return v.tensor_copy(
                P.arnB[:, 0:8192].rearrange("p (c s) -> p c s", c=4)
                [:, :, 128 * t:128 * (t + 1)],
                P.pt[1][:, 0:512].rearrange("p (c s) -> p c s", c=4))
        em.op("vector", evac, wait=[("T", tt)], inc="V")
    v_qk = em.count("V")

    def pack_mnmx(v):
        v.drain()
        return v.tensor_scalar_mul(P.stat[:, 56:64], P.stat[:, 56:64], -1.0)
    em.op("vector", pack_mnmx, inc="V")
    vpm = em.count("V")
    em.op("tensor",
          lambda te: te.transpose(P.pt[0][:].bitcast(F32)[0:16, 0:128],
                                  P.stat[:, 48:64], P.ident_f32[:]),
          wait=[("V", vpm)], inc="T")
    ttr = em.count("T")
    def c_math_v(v):
        return v.tensor_reduce(P.scr[0:16, 0:1],
                               P.pt[0][:].bitcast(F32)[0:16, 0:128],
                               axis=AX.X, op=ALU.min)
    em.op("vector", c_math_v, wait=[("T", ttr)], inc="V")
    vcm = em.count("V")
    em.op("tensor",
          lambda te: te.transpose(P.pb[5][0:1, 0:16], P.scr[0:16, 0:1],
                                  P.ident_f32[0:16, 0:16]),
          wait=[("V", vcm)], inc="T")
    ttr2 = em.count("T")

    def c_row(v, l=l):
        v.tensor_copy(P.scr[0:1, 1024:1040], P.pb[5][0:1, 0:16])
        v.drain()
        v.tensor_tensor(P.scr[0:1, 1040:1048], P.scr[0:1, 1024:1032],
                        P.scr[0:1, 1032:1040], op=ALU.subtract)
        v.drain()
        v.tensor_scalar_mul(P.scr[0:1, 1040:1048], P.scr[0:1, 1040:1048], 0.5)
        v.reduce_max(P.scr[0:1, 1048:1049], P.qs_row[:, 64 * l:64 * (l + 1)],
                     axis=AX.X, apply_absolute_value=True)
        v.reduce_max(P.scr[0:1, 1049:1050], P.ks_row[:, 64 * l:64 * (l + 1)],
                     axis=AX.X, apply_absolute_value=True)
        v.drain()
        v.tensor_tensor(P.scr[0:1, 1048:1049], P.scr[0:1, 1048:1049],
                        P.scr[0:1, 1049:1050], op=ALU.mult)
        v.drain()
        v.tensor_scalar(P.scr[0:1, 1048:1049], P.scr[0:1, 1048:1049],
                        8.0, -70.0, op0=ALU.mult, op1=ALU.add)
        v.drain()
        v.tensor_tensor(P.scr[0:1, 1040:1048], P.scr[0:1, 1040:1048],
                        P.scr[0:1, 1048:1049].broadcast_to([1, 8]), op=ALU.max)
        v.drain()
        return v.tensor_scalar_mul(P.scr[0:1, 1040:1048],
                                   P.scr[0:1, 1040:1048], -1.0)
    em.op("vector", c_row, wait=[("T", ttr2)], inc="V")
    vcr = em.count("V")
    em.op("tensor",
          lambda te: te.matmul(P.pb[5][:, 0:8], P.ones_f32[:],
                               P.scr[0:1, 1040:1048], start=True, stop=True),
          wait=[("V", vcr)], inc="T")
    tcb = em.count("T")
    em.op("scalar", lambda s: s.copy(P.scr[:, 1024:1032], P.pb[5][:, 0:8]),
          wait=[("T", tcb)], inc="SC")
    sc_cb = em.count("SC")
    if P.dbg(f"CB_{l}", (128, 8), F32) is not None:
        em.tap_barrier(em.dma("sync",
                       [(P.dbg_tensors[f"CB_{l}"], P.scr[:, 1024:1032])],
                       inc="ST", wait=[("SC", sc_cb)]))
    if P.dbg(f"DIAG_{l}", (128, 16), F32) is not None:
        em.tap_barrier(em.dma("sync",
                       [(P.dbg_tensors[f"DIAG_{l}"], P.stat[:, 48:64])],
                       inc="ST", wait=[("V", vpm)]))
    return emit_l0_ffn_layer_part2(P, l, spill_tgt, v_qk, sc_cb)


def emit_l0_ffn_layer_part2(P: Prog, l: int, spill_tgt, v_qk, sc_cb):
    em = P.em

    wq_dram = P.di["L0_ffn_wqkv"][l].rearrange("(ic q) o -> q ic o", q=128)
    for hhalf in range(2):
        lw = em.dma("sync", [(P.wstage[:, 2048 * hhalf:2048 * hhalf + 2048]
                              .rearrange("q (ic o) -> q ic o", ic=4),
                              wq_dram[:, 4 * hhalf:4 * hhalf + 4, 1024:1536])],
                    inc="LW", wait=[("V", em.count("V"))])
        em.op("vector",
              lambda v, hh=hhalf: v.tensor_copy(
                  P.wbf[:, 2048 * hh:2048 * (hh + 1)],
                  P.wstage[:, 2048 * hh:2048 * hh + 2048]),
              wait=[("LW", lw), ("T", em.count("T"))], inc="V")
    vwv = em.count("V")

    def vslot(t):
        if t < 15:
            return P.wbf, 4096 + 520 * t
        return P.arnC, 0

    def ones_v(v):
        last = None
        for t in range(NT):
            buf, off = vslot(t)
            last = v.memset(buf[:, off:off + 520]
                            .rearrange("p (h c) -> p h c", h=8)[:, :, 64:65], 1.0)
        return last
    em.op("vector", ones_v, wait=[("V", vwv)], inc="V")

    for t in range(NT):
        bv = P.pb[t % 2]
        w = [("V", em.count("V"))]

        def vmm(te, t=t, bv=bv):
            last = None
            for ic in range(NCH):
                lhsT = P.xT[:, 2048 * ic + 128 * t:2048 * ic + 128 * (t + 1)]
                last = te.matmul(bv[:, 0:512], lhsT,
                                 P.wbf[:, 512 * ic:512 * (ic + 1)],
                                 start=(ic == 0), stop=(ic == NCH - 1))
            return last
        em.op("tensor", vmm, wait=w, inc="T")
        tv = em.count("T")
        buf, off = vslot(t)
        em.op("scalar",
              lambda s, t=t, bv=bv, buf=buf, off=off: s.copy(
                  buf[:, off:off + 520].rearrange("p (h c) -> p h c", h=8)[:, :, 0:64],
                  bv[:, 0:512].rearrange("p (h c) -> p h c", h=8)),
              wait=[("T", tv)], inc="SC")
    sc_v = em.count("SC")

    oT = P.wstage[:].bitcast(BF16)
    sc_prev_ev = sc_v
    for qg in range(8):
        kbs = [kb for kb in range(2 * qg - 4, 2 * qg + 2) if kb >= 0]
        for pr in range(4):
            sc_exp = {}
            pv_done = {}
            for ik, kb in enumerate(kbs):
                slot = ik % 2
                rel = kb - 2 * qg
                w = []
                if ik >= 2:
                    w.append(("SC", sc_exp[ik - 2]))
                else:
                    w.append(("SC", em.count("SC")))
                    if qg == 0 and pr == 0:
                        w.append(("V", em.count("V")))

                def score(te, qg=qg, pr=pr, kb=kb, slot=slot, rel=rel):
                    last = None
                    for h in range(2):
                        bank = P.pb[2 * slot + h]
                        lhsT = P.arnB[64 * h:64 * (h + 1), 0:8192] \
                            .rearrange("p (c s) -> p c s", c=4)[:, pr, 128 * kb:128 * (kb + 1)]
                        rhs = P.arnD[64 * h:64 * (h + 1), 0:8192] \
                            .rearrange("p (c s) -> p c s", c=4)[:, pr, 256 * qg:256 * (qg + 1)]
                        last = te.matmul(bank[:, 0:256], lhsT, rhs,
                                         start=True,
                                         stop=(rel in (-3, -2, -1)))
                    if rel not in (-3, -2, -1):
                        for h in range(2):
                            bank = P.pb[2 * slot + h]
                            last = te.matmul(bank[:, 0:256], P.ident[:],
                                             P.mask[f"mask_l0f_{rel}"][:],
                                             start=False, stop=True)
                    return last
                em.op("tensor", score, wait=w, inc="T")
                ts = em.count("T")

                def do_exp(s, pr=pr, slot=slot):
                    last = None
                    for h in range(2):
                        last = s.activation(
                            out=P.arnA[:, 2048 + 256 * (2 * h + slot):
                                       2048 + 256 * (2 * h + slot + 1)],
                            in_=P.pb[2 * slot + h][:, 0:256], func=ACT.Exp,
                            bias=P.scr[:, 1024 + 2 * pr + h:1025 + 2 * pr + h])
                    return last
                we = [("T", ts), ("SC", sc_cb)]
                if ik >= 2:
                    we.append(("T", pv_done.get(slot, 0)))
                em.op("scalar", do_exp, wait=we, inc="SC")
                sc_exp[ik] = em.count("SC")

                def do_pv(te, qg=qg, pr=pr, kb=kb, slot=slot, ik=ik,
                          nkb=len(kbs)):
                    last = None
                    for h in range(2):
                        ptile = P.arnA[:, 2048 + 256 * (2 * h + slot):
                                       2048 + 256 * (2 * h + slot + 1)]
                        buf, off = vslot(kb)
                        for qq in range(2):
                            last = te.matmul(
                                P.pb[4][:, 65 * (2 * h + qq):65 * (2 * h + qq) + 65],
                                ptile[:, 128 * qq:128 * (qq + 1)],
                                buf[:, off + 65 * (2 * pr + h):
                                    off + 65 * (2 * pr + h) + 65],
                                start=(ik == 0 and h == 0 and qq == 0),
                                stop=(ik == nkb - 1 and h == 1 and qq == 1))
                    return last
                em.op("tensor", do_pv, wait=[("SC", sc_exp[ik])], inc="T")
                pv_done[slot] = em.count("T")
            tpv = em.count("T")

            def recips(v, pr=pr):
                last = None
                for k in range(4):
                    last = v.reciprocal(P.stat[:, 32 + k:33 + k],
                                        P.pb[4][:, 65 * k + 64:65 * k + 65])
                return last
            em.op("vector", recips,
                  wait=[("T", tpv), ("SC", em.count("SC"))], inc="V")
            vr = em.count("V")

            def o_evac(s, qg=qg, pr=pr):
                last = None
                for h in range(2):
                    for qq in range(2):
                        last = s.activation(
                            out=P.arnC[:, 1024 + 512 * qq + 64 * (2 * pr + h):
                                       1024 + 512 * qq + 64 * (2 * pr + h) + 64],
                            in_=P.pb[4][:, 65 * (2 * h + qq):65 * (2 * h + qq) + 64],
                            func=ACT.Identity,
                            scale=P.stat[:, 32 + 2 * h + qq:33 + 2 * h + qq])
                return last
            em.op("scalar", o_evac, wait=[("V", vr)], inc="SC")
        sc_o = em.count("SC")

        for qq in range(2):
            qb = 2 * qg + qq

            def trans(te, qq=qq):
                last = None
                for c in range(4):
                    last = te.transpose(
                        P.pt[qq][:, 128 * c:128 * (c + 1)],
                        P.arnC[:, 1024 + 512 * qq + 128 * c:
                               1024 + 512 * qq + 128 * (c + 1)],
                        P.ident[:])
                return last
            em.op("tensor", trans, wait=[("SC", sc_o)], inc="T")
            tt = em.count("T")
            em.op("vector",
                  lambda v, qb=qb, qq=qq: v.tensor_copy(
                      oT[:, 512 * qb:512 * (qb + 1)],
                      P.pt[qq][:, 0:512]),
                  wait=[("T", tt)], inc="V")
    v_attn = em.count("V")
    if P.dbg(f"OT_{l}", (128, 8192), BF16) is not None:
        em.tap_barrier(em.dma("sync",
                       [(P.dbg_tensors[f"OT_{l}"], P.wstage[:].bitcast(BF16))],
                       inc="ST", wait=[("V", v_attn)]))
    if P.dbg(f"QT_{l}", (128, 8192), BF16) is not None:
        em.tap_barrier(em.dma("sync",
                       [(P.dbg_tensors[f"QT_{l}"], P.arnD[:, 0:8192])],
                       inc="ST", wait=[("V", v_attn)]))
    if P.dbg(f"KT_{l}", (128, 8192), BF16) is not None:
        em.tap_barrier(em.dma("sync",
                       [(P.dbg_tensors[f"KT_{l}"], P.arnB[:, 0:8192])],
                       inc="ST", wait=[("V", v_attn)]))
    return emit_l0_ffn_layer_part3(P, l, spill_tgt, v_attn)


def emit_l0_ffn_layer_part3(P: Prog, l: int, spill_tgt, v_attn):
    em = P.em
    oT = P.wstage[:].bitcast(BF16)

    wo_dram = P.di["L0_ffn_wo"][l].rearrange("(ic q) o -> q ic o", q=128)
    for hh in range(2):
        lw = em.dma("sync", [(P.arnB[:, 0:4096].bitcast(F32)
                              .rearrange("q (ic o) -> q ic o", ic=2),
                              wo_dram[:, 2 * hh:2 * hh + 2, :])],
                    inc="LW", wait=[("T", em.count("T")), ("V", em.count("V"))])
        em.op("vector",
              lambda v, hh=hh: v.tensor_copy(
                  P.wbf[:, 2048 * hh:2048 * (hh + 1)],
                  P.arnB[:, 0:4096].bitcast(F32)),
              wait=[("LW", lw), ("T", em.count("T"))], inc="V")
    v_wo = em.count("V")

    for t in range(NT):
        b0, b1 = P.pb[2 * (t % 2)], P.pb[2 * (t % 2) + 1]
        w = [("V", v_wo)] if t == 0 else []
        if t >= 2:
            w.append(("V", em.count("V")))

        def womm(te, t=t, b0=b0, b1=b1):
            last = None
            for c in range(4):
                lhsT = oT[:, 512 * t + 128 * c:512 * t + 128 * (c + 1)]
                last = te.matmul(b0[:, 0:512], lhsT,
                                 P.wbf[:, 1024 * c:1024 * c + 512],
                                 start=(c == 0), stop=(c == 3))
                last = te.matmul(b1[:, 0:512], lhsT,
                                 P.wbf[:, 1024 * c + 512:1024 * (c + 1)],
                                 start=(c == 0), stop=(c == 3))
            return last
        em.op("tensor", womm, wait=w, inc="T")
        tw = em.count("T")
        wadd = [("T", tw)]
        if l == 0 and t == 0:
            wadd.append(("SP", spill_tgt))

        def add_evac(v, t=t, b0=b0, b1=b1):
            v.tensor_add(P.xs[:, D * t:D * t + 512],
                         P.xs[:, D * t:D * t + 512], b0[:, 0:512])
            return v.tensor_add(P.xs[:, D * t + 512:D * (t + 1)],
                                P.xs[:, D * t + 512:D * (t + 1)], b1[:, 0:512])
        em.op("vector", add_evac, wait=wadd, inc="V")
    v_z2 = em.count("V")

    if P.dbg(f"Z2_{l}", (128, NT * D), F32) is not None:
        em.tap_barrier(em.dma("sync", [(P.dbg_tensors[f"Z2_{l}"], P.xs[:])],
                       inc="ST", wait=[("V", v_z2)]))
    emit_xT(P, src_f32=P.xs, dst=P.xT, shift_cols=(4, 5, 6, 7),
            wait_extra=[("V", v_z2), ("SC", em.count("SC")),
                        ("T", em.count("T"))])
    v_h2 = em.count("V")

    def stats_t(v, t):
        def f(v_, t=t):
            v_.reduce_sum(P.stat[:, t:t + 1], P.xs[:, D * t:D * t + 512],
                          axis=AX.X)
            return v_.reduce_sum(P.scr[:, 1 + t:2 + t],
                                 P.xs[:, D * t + 512:D * (t + 1)], axis=AX.X)
        return f
    for t in range(NT):
        w = [("V", v_h2)] if t == 0 else []
        em.op("vector", stats_t(None, t), wait=w, inc="V")
        em.op("scalar",
              lambda s, t=t: s.activation(
                  out=P.scr[:, 64:576], in_=P.xs[:, D * t:D * t + 512],
                  func=ACT.Square, accum_out=P.stat[:, 32 + t:33 + t]),
              wait=w, inc="SC")
        em.op("scalar",
              lambda s, t=t: s.activation(
                  out=P.scr[:, 64:576], in_=P.xs[:, D * t + 512:D * (t + 1)],
                  func=ACT.Square, accum_out=P.scr[:, 20 + t:21 + t]),
              inc="SC")
    v_st = em.count("V")
    sc_st = em.count("SC")

    def zcols(v):
        v.memset(P.scr[:, 0:1], 0.0)
        return v.memset(P.scr[:, 19:20], 0.0)
    em.op("vector", zcols, wait=[("SC", sc_st)], inc="V")
    vz = em.count("V")

    def shiftmm(te):
        te.matmul(P.pb[4][:, 0:16], P.shiftmat[:], P.scr[:, 1:17],
                  start=True, stop=False)
        te.matmul(P.pb[4][:, 0:16], P.sel127[:], P.scr[:, 0:16],
                  start=False, stop=True)
        te.matmul(P.pb[4][:, 16:32], P.shiftmat[:], P.scr[:, 20:36],
                  start=True, stop=False)
        return te.matmul(P.pb[4][:, 16:32], P.sel127[:], P.scr[:, 19:35],
                         start=False, stop=True)
    em.op("tensor", shiftmm, wait=[("V", vz)], inc="T")
    t_sh = em.count("T")

    def ln2_math(v):
        v.tensor_add(P.stat[:, 0:16], P.stat[:, 0:16], P.pb[4][:, 0:16])
        v.tensor_add(P.stat[:, 32:48], P.stat[:, 32:48], P.pb[4][:, 16:32])
        v.drain()
        v.tensor_scalar_mul(P.stat[:, 16:32], P.stat[:, 0:16], 1.0 / D)
        v.tensor_scalar_mul(P.stat[:, 48:64], P.stat[:, 32:48], 1.0 / D)
        v.drain()
        v.tensor_tensor(P.scr[:, 0:16], P.stat[:, 16:32], P.stat[:, 16:32],
                        op=ALU.mult)
        v.drain()
        v.tensor_sub(P.stat[:, 48:64], P.stat[:, 48:64], P.scr[:, 0:16])
        v.drain()
        return v.tensor_scalar_add(P.stat[:, 48:64], P.stat[:, 48:64], 1e-5)
    em.op("vector", ln2_math, wait=[("T", t_sh)], inc="V")
    v_lm = em.count("V")
    em.op("scalar",
          lambda s: s.activation(out=P.scr[:, 0:16], in_=P.stat[:, 48:64],
                                 func=ACT.Sqrt),
          wait=[("V", v_lm)], inc="SC")
    sc_sq = em.count("SC")

    def rstd_nb(v):
        v.reciprocal(P.stat[:, 48:64], P.scr[:, 0:16])
        v.drain()
        v.tensor_tensor(P.stat[:, 16:32], P.stat[:, 16:32], P.stat[:, 48:64],
                        op=ALU.mult)
        v.drain()
        return v.tensor_scalar_mul(P.stat[:, 16:32], P.stat[:, 16:32], -1.0)
    em.op("vector", rstd_nb, wait=[("SC", sc_sq)], inc="V")
    v_rn = em.count("V")

    def packrn(v):
        v.tensor_copy(P.scr[:, 0:16], P.stat[:, 48:64])
        return v.tensor_copy(P.scr[:, 16:32], P.stat[:, 16:32])
    em.op("vector", packrn, wait=[("V", v_rn)], inc="V")
    v_pk = em.count("V")
    em.op("tensor",
          lambda te: te.transpose(P.pb[4][:].bitcast(F32)[0:32, 0:128],
                                  P.scr[:, 0:32], P.ident_f32[:]),
          wait=[("V", v_pk)], inc="T")
    t_tr = em.count("T")
    em.op("scalar", lambda s: s.copy(P.scr[0:32, 64:192], P.pb[4][:].bitcast(F32)[0:32, 0:128]),
          wait=[("T", t_tr)], inc="SC")
    sc_tr = em.count("SC")
    rowbuf = P.arnB[:, 4096:8192].bitcast(F32)
    nbbuf = P.arnD[:, 0:4096].bitcast(F32)
    lwg = em.dma("gpsimd",
                 [(rowbuf[0:1, 0:2048].rearrange("o (t p) -> o t p", t=NT),
                   P.scr[0:16, 64:192][:, None, :]),
                  (nbbuf[0:1, 0:2048].rearrange("o (t p) -> o t p", t=NT),
                   P.scr[16:32, 64:192][:, None, :])],
                 inc="LG", wait=[("SC", sc_tr)])
    return emit_l0_ffn_layer_part4(P, l, lwg)


def emit_l0_ffn_layer_part4(P: Prog, l: int, lwg):
    em = P.em
    rowbuf = P.arnB[:, 4096:8192].bitcast(F32)
    nbbuf = P.arnD[:, 0:4096].bitcast(F32)

    for c in range(NCH):
        for sl in range(4):
            def bc(te, sl=sl):
                te.matmul(P.pb[4][:, 0:512], P.ones_f32[:],
                          rowbuf[0:1, 512 * sl:512 * (sl + 1)],
                          start=True, stop=True)
                return te.matmul(P.pb[5][:, 0:512], P.ones_f32[:],
                                 nbbuf[0:1, 512 * sl:512 * (sl + 1)],
                                 start=True, stop=True)
            w = [("LG", lwg)] if (c == 0 and sl == 0) else [("V", em.count("V"))]
            em.op("tensor", bc, wait=w, inc="T")
            tb = em.count("T")

            def norm(v, c=c, sl=sl, l=l):
                seg = P.xT[:, 2048 * c + 512 * sl:2048 * c + 512 * (sl + 1)]
                v.tensor_tensor(P.wstage[:, 0:512], seg, P.pb[4][:, 0:512],
                                op=ALU.mult)
                v.drain()
                v.tensor_add(P.wstage[:, 0:512], P.wstage[:, 0:512],
                             P.pb[5][:, 0:512])
                v.drain()
                return v.tensor_scalar(
                    out=seg, in0=P.wstage[:, 0:512],
                    scalar1=P.lngT[:, 8 * l + c:8 * l + c + 1],
                    scalar2=P.lnbT[:, 8 * l + c:8 * l + c + 1],
                    op0=ALU.mult, op1=ALU.add)
            em.op("vector", norm, wait=[("T", tb)], inc="V")
    v_h2n = em.count("V")
    if P.dbg(f"H2N_{l}", (128, NCH * S), BF16) is not None:
        em.tap_barrier(em.dma("sync", [(P.dbg_tensors[f"H2N_{l}"], P.xT[:])],
                       inc="ST", wait=[("V", v_h2n)]))

    w1_dram = P.di["L0_ffn_w1"][l].rearrange("(ic q) o -> q ic o", q=128)
    w2_dram = P.di["L0_ffn_w2"][l]

    def ft_ap(c):
        if c < 8:
            return P.arnD[:, 1024 * c:1024 * (c + 1)]
        if c < 16:
            return P.wbf[:, 4096 + 1024 * (c - 8):4096 + 1024 * (c - 7)]
        return P.arnB[:, 1024 * (c - 16):1024 * (c - 15)]

    NW = 22
    for g in range(2):
        tok0 = 1024 * g
        for uw in range(NW):
            gwid = 128 if uw < 21 else 42
            awid = gwid
            lw = em.dma("sync", [
                (P.wstage[:, 0:8 * awid].rearrange("q (ic o) -> q ic o", ic=8),
                 w1_dram[:, :, 128 * uw:128 * uw + awid]),
                (P.wstage[:, 1024:1024 + 8 * gwid]
                 .rearrange("q (ic o) -> q ic o", ic=8),
                 w1_dram[:, :, 2730 + 128 * uw:2730 + 128 * uw + gwid]),
            ], inc="LW", wait=[("V", em.count("V"))])
            em.op("vector",
                  lambda v, gwid=gwid, awid=awid, uw=uw: (
                      v.tensor_copy(P.wbf[:, 0:8 * awid] if uw % 2 == 0 else
                                    P.wbf[:, 2048:2048 + 8 * awid],
                                    P.wstage[:, 0:8 * awid]),
                      v.tensor_copy(
                          (P.wbf[:, 1024:1024 + 8 * gwid] if uw % 2 == 0 else
                           P.wbf[:, 3072:3072 + 8 * gwid]),
                          P.wstage[:, 1024:1024 + 8 * gwid]))[-1],
                  wait=[("LW", lw), ("T", em.count("T"))], inc="V")
            vcast = em.count("V")
            woff = 0 if uw % 2 == 0 else 2048
            pa = (P.pb[0], P.pb[1]) if uw % 2 == 0 else (P.pb[2], P.pb[3])
            pg = (P.pb[4], P.pb[5]) if uw % 2 == 0 else \
                (P.pt[0][:].bitcast(F32), P.pt[1][:].bitcast(F32))

            def w1mm(te, uw=uw, g=g, woff=woff, pa=pa, pg=pg, gwid=gwid,
                     awid=awid, tok0=tok0):
                last = None
                for ic in range(NCH):
                    rhs = P.xT[:, 2048 * ic + tok0:2048 * ic + tok0 + 1024]
                    last = te.matmul(pa[0][0:awid, 0:512],
                                     P.wbf[:, woff + awid * ic:woff + awid * (ic + 1)],
                                     rhs[:, 0:512], start=(ic == 0),
                                     stop=(ic == NCH - 1))
                    last = te.matmul(pa[1][0:awid, 0:512],
                                     P.wbf[:, woff + awid * ic:woff + awid * (ic + 1)],
                                     rhs[:, 512:1024],
                                     start=(ic == 0), stop=(ic == NCH - 1))
                    last = te.matmul(pg[0][0:gwid, 0:512],
                                     P.wbf[:, woff + 1024 + gwid * ic:
                                           woff + 1024 + gwid * (ic + 1)],
                                     rhs[:, 0:512], start=(ic == 0),
                                     stop=(ic == NCH - 1))
                    last = te.matmul(pg[1][0:gwid, 0:512],
                                     P.wbf[:, woff + 1024 + gwid * ic:
                                           woff + 1024 + gwid * (ic + 1)],
                                     rhs[:, 512:1024], start=(ic == 0),
                                     stop=(ic == NCH - 1))
                return last
            wmm = [("V", max(vcast, v_h2n))]
            if uw >= 2:
                wmm.append(("V", em.count("V")))
            em.op("tensor", w1mm, wait=wmm, inc="T")
            t1 = em.count("T")

            def gelu_g(s, uw=uw, l=l, pg=pg, gwid=gwid):
                s.activation(out=P.scr[0:gwid, 0:512],
                             in_=pg[0][0:gwid, 0:512], func=ACT.Gelu,
                             bias=P.b1gT[0:gwid, 22 * l + uw:22 * l + uw + 1])
                return s.activation(out=P.scr[0:gwid, 512:1024],
                                    in_=pg[1][0:gwid, 0:512], func=ACT.Gelu,
                                    bias=P.b1gT[0:gwid, 22 * l + uw:22 * l + uw + 1])
            em.op("scalar", gelu_g, wait=[("T", t1)], inc="SC")
            sg = em.count("SC")

            def fmul(v, uw=uw, l=l, pa=pa, g=g, gwid=gwid):
                dst = ft_ap(uw)
                v.scalar_tensor_tensor(
                    out=dst[0:gwid, 0:512], in0=pa[0][0:gwid, 0:512],
                    scalar=P.b1aT[0:gwid, 22 * l + uw:22 * l + uw + 1],
                    in1=P.scr[0:gwid, 0:512], op0=ALU.add, op1=ALU.mult)
                return v.scalar_tensor_tensor(
                    out=dst[0:gwid, 512:1024], in0=pa[1][0:gwid, 0:512],
                    scalar=P.b1aT[0:gwid, 22 * l + uw:22 * l + uw + 1],
                    in1=P.scr[0:gwid, 512:1024], op0=ALU.add, op1=ALU.mult)
            em.op("vector", fmul, wait=[("SC", sg)], inc="V")
        v_f = em.count("V")

        for op_ in range(4):
            banks = (P.pb[0], P.pb[1], P.pb[2], P.pb[3])
            for c in range(NW):
                cw = 128 if c < 21 else 42
                lw = em.dma("sync", [
                    (P.wstage[0:cw, 0:256],
                     w2_dram[128 * c:128 * c + cw, 256 * op_:256 * (op_ + 1)]),
                ], inc="LW", wait=[("V", em.count("V")), ("T", em.count("T"))])
                slot = 256 * (c % 2)
                em.op("vector",
                      lambda v, cw=cw, slot=slot: v.tensor_copy(
                          P.arnA[0:cw, slot:slot + 256], P.wstage[0:cw, 0:256]),
                      wait=[("LW", lw), ("T", em.count("T"))], inc="V")
                vw2 = em.count("V")

                def w2mm(te, c=c, cw=cw, slot=slot, banks=banks):
                    fa = ft_ap(c)
                    last = None
                    for oe in range(2):
                        lhsT = P.arnA[0:cw, slot + 128 * oe:slot + 128 * (oe + 1)]
                        last = te.matmul(banks[2 * oe][:, 0:512], lhsT,
                                         fa[0:cw, 0:512],
                                         start=(c == 0), stop=(c == NW - 1))
                        last = te.matmul(banks[2 * oe + 1][:, 0:512], lhsT,
                                         fa[0:cw, 512:1024],
                                         start=(c == 0), stop=(c == NW - 1))
                    return last
                wm = [("V", max(vw2, v_f))]
                if c == 0 and op_ > 0:
                    wm.append(("V", em.count("V")))
                em.op("tensor", w2mm, wait=wm, inc="T")
            t2 = em.count("T")

            def yevac(s, op_=op_, l=l):
                last = None
                for k in range(4):
                    oe = k // 2
                    last = s.activation(
                        out=P.arnC[:, 512 * k:512 * (k + 1)],
                        in_=P.pb[k][:, 0:512], func=ACT.Identity,
                        bias=P.b2T[:, 8 * l + 2 * op_ + oe:
                                   8 * l + 2 * op_ + oe + 1])
                return last
            em.op("scalar", yevac, wait=[("T", t2)], inc="SC")
            s2 = em.count("SC")
            for oe in range(2):
                for tl in range(8):
                    def ytr(te, oe=oe, tl=tl):
                        return te.transpose(
                            P.pt[tl % 2][:, 0:128],
                            P.arnC[:, 1024 * oe + 128 * tl:
                                   1024 * oe + 128 * (tl + 1)],
                            P.ident[:])
                    em.op("tensor", ytr,
                          wait=[("SC", s2)] if (oe == 0 and tl == 0)
                          else [("V", em.count("V"))], inc="T")
                    ty = em.count("T")

                    def yadd(v, oe=oe, tl=tl, op_=op_, g=g):
                        tglob = 8 * g + tl
                        col = D * tglob + 128 * (2 * op_ + oe)
                        return v.tensor_add(P.xs[:, col:col + 128],
                                            P.xs[:, col:col + 128],
                                            P.pt[tl % 2][:, 0:128])
                    em.op("vector", yadd, wait=[("T", ty)], inc="V")
    if P.dbg(f"Z3_{l}", (128, NT * D), F32) is not None:
        em.tap_barrier(em.dma("sync", [(P.dbg_tensors[f"Z3_{l}"], P.xs[:])],
                       inc="ST", wait=[("V", em.count("V"))]))
    return em.count("V")


def emit_wT_col(P: Prog, w_dram, oc, dst_ap, tag):
    em = P.em
    lw = em.dma("sync", [(P.wstage[:, 0:1024],
                          w_dram[128 * oc:128 * (oc + 1), :])],
                inc="LW", wait=[("V", em.count("V")), ("SC", em.count("SC"))])
    em.op("vector",
          lambda v: v.tensor_copy(P.xbf[:, 0:1024], P.wstage[:, 0:1024]),
          wait=[("LW", lw), ("T", em.count("T"))], inc="V")
    vc = em.count("V")

    def trans(te):
        last = None
        for ic in range(NCH):
            last = te.transpose(P.pt[0][:, 128 * ic:128 * (ic + 1)],
                                P.xbf[:, 128 * ic:128 * (ic + 1)], P.ident[:])
        return last
    em.op("tensor", trans, wait=[("V", vc)], inc="T")
    tt = em.count("T")
    em.op("scalar", lambda s: s.copy(dst_ap, P.pt[0][:, 0:1024]),
          wait=[("T", tt)], inc="SC")
    return em.count("SC")


def emit_l1_attn(P: Prog):
    em = P.em
    emit_xT(P, wait_extra=[("V", em.count("V")), ("SC", em.count("SC")),
                           ("T", em.count("T"))])
    v_xt = em.count("V")

    wq_nat = P.di["L1_attn_Wq"]
    wk_nat = P.di["L1_attn_Wk"]
    wv_nat = P.di["L1_attn_Wv"]

    def qT_ap(c, sl=None):
        buf, off = (P.arnB, 2048 * c) if c < 4 else (P.arnD, 2048 * (c - 4))
        a = buf[:, off:off + 2048]
        return a if sl is None else a[:, sl]

    def v_ap(t):
        if t < 12:
            return P.wbf[:, 1024 * t:1024 * (t + 1)]
        if t < 15:
            return P.arnA[:, 1024 * (t - 12):1024 * (t - 11)]
        return P.arnC[:, 0:1024]

    for oc in range(NCH):
        scw = emit_wT_col(P, wq_nat, oc, P.arnC[:, 1024:2048], "wq")
        for ns in range(4):
            def qmm(te, oc=oc, ns=ns):
                last = None
                for ic in range(NCH):
                    last = te.matmul(
                        P.pb[ns % 2][:, 0:512],
                        P.arnC[:, 1024 + 128 * ic:1024 + 128 * (ic + 1)],
                        P.xT[:, 2048 * ic + 512 * ns:2048 * ic + 512 * (ns + 1)],
                        start=(ic == 0), stop=(ic == NCH - 1))
                return last
            w = [("SC", scw)]
            if ns >= 2 or oc > 0:
                w.append(("SC", em.count("SC")))
            em.op("tensor", qmm, wait=w, inc="T")
            tq = em.count("T")
            em.op("scalar",
                  lambda s, oc=oc, ns=ns: s.copy(
                      qT_ap(oc, slice(512 * ns, 512 * (ns + 1))),
                      P.pb[ns % 2][:, 0:512]),
                  wait=[("T", tq)], inc="SC")
    sc_q = em.count("SC")

    for oc in range(NCH):
        scw = emit_wT_col(P, wk_nat, oc, P.arnC[:, 1024:2048], "wk")
        for ns in range(4):
            def kmm(te, oc=oc, ns=ns):
                last = None
                for ic in range(NCH):
                    last = te.matmul(
                        P.pb[2 + ns % 2][:, 0:512],
                        P.arnC[:, 1024 + 128 * ic:1024 + 128 * (ic + 1)],
                        P.xT[:, 2048 * ic + 512 * ns:2048 * ic + 512 * (ns + 1)],
                        start=(ic == 0), stop=(ic == NCH - 1))
                return last
            w = [("SC", scw)]
            w.append(("LG", em.count("LG")))
            em.op("tensor", kmm, wait=w, inc="T")
            tk = em.count("T")
            em.op("scalar",
                  lambda s, ns=ns: s.copy(P.xbf[:, 1024 + 512 * (ns % 2):
                                                1024 + 512 * (ns % 2) + 512],
                                          P.pb[2 + ns % 2][:, 0:512]),
                  wait=[("T", tk)], inc="SC")
            sck = em.count("SC")
            em.dma("gpsimd",
                   [(P.d_kspill[128 * oc:128 * (oc + 1),
                                512 * ns:512 * (ns + 1)],
                     P.xbf[:, 1024 + 512 * (ns % 2):1024 + 512 * (ns % 2) + 512])],
                   inc="LG", wait=[("SC", sck)])
    lg_k = em.count("LG")

    for oc in range(NCH):
        scw = emit_wT_col(P, wv_nat, oc, P.arnC[:, 1024:2048], "wv")
        for t in range(NT):
            def vmm(te, oc=oc, t=t):
                last = None
                for ic in range(NCH):
                    last = te.matmul(
                        P.pb[4 + t % 2][:, 0:128],
                        P.xT[:, 2048 * ic + 128 * t:2048 * ic + 128 * (t + 1)],
                        P.arnC[:, 1024 + 128 * ic:1024 + 128 * (ic + 1)],
                        start=(ic == 0), stop=(ic == NCH - 1))
                return last
            w = [("SC", scw)]
            if t >= 2:
                w.append(("SC", em.count("SC")))
            em.op("tensor", vmm, wait=w, inc="T")
            tv = em.count("T")
            em.op("scalar",
                  lambda s, oc=oc, t=t: s.copy(
                      v_ap(t)[:, 128 * oc:128 * (oc + 1)],
                      P.pb[4 + t % 2][:, 0:128]),
                  wait=[("T", tv)], inc="SC")
    sc_v = em.count("SC")
    if P.dbg("L1QT", (128, 16384), BF16) is not None:
        em.tap_barrier(em.dma("sync", [
            (P.dbg_tensors["L1QT"][:, 0:8192], P.arnB[:, 0:8192]),
            (P.dbg_tensors["L1QT"][:, 8192:16384], P.arnD[:, 0:8192]),
        ], inc="ST", wait=[("SC", sc_q)]))
    if P.dbg("L1V", (128, 16384), BF16) is not None:
        em.tap_barrier(em.dma("sync", [
            (P.dbg_tensors["L1V"][:, 0:12288], P.wbf[:, 0:12288]),
            (P.dbg_tensors["L1V"][:, 12288:15360], P.arnA[:, 0:3072]),
            (P.dbg_tensors["L1V"][:, 15360:16384], P.arnC[:, 0:1024]),
        ], inc="ST", wait=[("SC", sc_v)]))
    return emit_l1_attn_part2(P, sc_q, sc_v, lg_k)


def emit_l1_attn_part2(P: Prog, sc_q, sc_v, lg_k):
    em = P.em

    def qT_ap(c):
        buf, off = (P.arnB, 2048 * c) if c < 4 else (P.arnD, 2048 * (c - 4))
        return buf[:, off:off + 2048]

    def v_ap(t):
        if t < 12:
            return P.wbf[:, 1024 * t:1024 * (t + 1)]
        if t < 15:
            return P.arnA[:, 1024 * (t - 12):1024 * (t - 11)]
        return P.arnC[:, 0:1024]

    def ot_ap(qb, c):
        if qb == 0:
            return P.xbf[:, 1024 + 128 * c:1024 + 128 * (c + 1)]
        if qb == 1:
            return P.arnC[:, 1024 + 128 * c:1024 + 128 * (c + 1)]
        return qT_ap(c)[:, 128 * (qb - 2):128 * (qb - 1)]

    kwin = P.wstage[:].bitcast(BF16)

    def kw_ap(kb):
        return kwin[:, 1024 * (kb % 4):1024 * (kb % 4) + 1024]

    lgk = {}
    for kb in (0, 1):
        lgk[kb] = em.dma("gpsimd",
                         [(kw_ap(kb).rearrange("p (c s) -> p c s", c=8),
                           P.d_kspill[:, 128 * kb:128 * (kb + 1)]
                           .rearrange("(c p) s -> p c s", p=128))],
                         inc="LG",
                         wait=[("LG", lg_k), ("V", em.count("V")),
                               ("T", em.count("T"))])

    for qb in range(NT):
        kbs = [kb for kb in (qb - 1, qb, qb + 1) if 0 <= kb < NT]
        if qb + 2 < NT:
            lgk[qb + 2] = em.dma(
                "gpsimd",
                [(kw_ap(qb + 2).rearrange("p (c s) -> p c s", c=8),
                  P.d_kspill[:, 128 * (qb + 2):128 * (qb + 3)]
                  .rearrange("(c p) s -> p c s", p=128))],
                inc="LG", wait=[("T", em.count("T"))])
        for pr in range(8):
            sc_exp = {}
            pv_done = {}
            for ik, kb in enumerate(kbs):
                slot = ik % 2
                rel = kb - qb
                w = [("LG", lgk[kb])]
                if ik >= 2:
                    w.append(("SC", sc_exp[ik - 2]))
                else:
                    w.append(("SC", em.count("SC")))
                if qb == 0 and pr == 0 and ik == 0:
                    w.append(("SC", max(sc_q, sc_v)))

                def score(te, qb=qb, pr=pr, kb=kb, slot=slot, rel=rel):
                    last = None
                    for h in range(2):
                        bank = P.pb[2 * slot + h]
                        H = 2 * pr + h
                        lhsT = kw_ap(kb)[64 * (H % 2):64 * (H % 2) + 64,
                                         128 * (H // 2):128 * (H // 2) + 128]
                        rhs = qT_ap(H // 2)[64 * (H % 2):64 * (H % 2) + 64,
                                            128 * qb:128 * (qb + 1)]
                        last = te.matmul(bank[:, 0:128], lhsT, rhs,
                                         start=True, stop=(rel == 0))
                    if rel != 0:
                        for h in range(2):
                            last = te.matmul(P.pb[2 * slot + h][:, 0:128],
                                             P.ident[:],
                                             P.mask[f"mask_l1a_{rel}"][:],
                                             start=False, stop=True)
                    return last
                em.op("tensor", score, wait=w, inc="T")
                ts = em.count("T")

                def do_exp(s, slot=slot):
                    last = None
                    for h in range(2):
                        last = s.activation(
                            out=P.xbf[:, 128 * (2 * h + slot):
                                  128 * (2 * h + slot + 1)],
                            in_=P.pb[2 * slot + h][:, 0:128], func=ACT.Exp,
                            scale=0.125)
                    return last
                we = [("T", ts)]
                if ik >= 2:
                    we.append(("T", pv_done.get(slot, 0)))
                em.op("scalar", do_exp, wait=we, inc="SC")
                sc_exp[ik] = em.count("SC")

                def obank(H):
                    bi = H // 7
                    if bi < 2:
                        return P.pb[4 + bi][:], 65 * (H % 7)
                    return P.pt[0][:].bitcast(F32), 65 * (H - 14)

                def do_pv(te, qb=qb, pr=pr, kb=kb, slot=slot, ik=ik,
                          nkb=len(kbs), obank=obank):
                    last = None
                    for h in range(2):
                        H = 2 * pr + h
                        bank, reg = obank(H)
                        ptile = P.xbf[:, 128 * (2 * h + slot):
                                      128 * (2 * h + slot + 1)]
                        last = te.matmul(
                            bank[:, reg:reg + 64], ptile,
                            v_ap(kb)[:, 64 * H:64 * (H + 1)],
                            start=(ik == 0 and H % 7 == 0 and H < 14),
                            stop=False)
                        last = te.matmul(
                            bank[:, reg + 64:reg + 65], ptile,
                            P.ones_col[:],
                            start=False,
                            stop=(ik == nkb - 1 and H in (6, 13, 15)))
                    return last
                em.op("tensor", do_pv, wait=[("SC", sc_exp[ik])], inc="T")
                pv_done[slot] = em.count("T")
        tpv = em.count("T")

        def recips(v):
            v.reciprocal(P.stat[:, 32:39],
                         P.pb[4][:, 0:455]
                         .rearrange("p (h c) -> p h c", h=7)[:, :, 64:65])
            v.reciprocal(P.stat[:, 39:46],
                         P.pb[5][:, 0:455]
                         .rearrange("p (h c) -> p h c", h=7)[:, :, 64:65])
            return v.reciprocal(P.stat[:, 46:48],
                                P.pt[0][:].bitcast(F32)[:, 0:130]
                                .rearrange("p (h c) -> p h c", h=2)[:, :, 64:65])
        em.op("vector", recips,
              wait=[("T", tpv), ("SC", em.count("SC"))], inc="V")
        vr = em.count("V")

        def o_evac(s, qb=qb):
            last = None
            for H in range(16):
                bi = H // 7
                if bi < 2:
                    bank, reg = P.pb[4 + bi][:], 65 * (H % 7)
                else:
                    bank, reg = P.pt[0][:].bitcast(F32), 65 * (H - 14)
                last = s.activation(
                    out=P.scr[:, 0:512].bitcast(BF16)[:, 64 * H:64 * (H + 1)],
                    in_=bank[:, reg:reg + 64], func=ACT.Identity,
                    scale=P.stat[:, 32 + H:33 + H])
            return last
        em.op("scalar", o_evac, wait=[("V", vr)], inc="SC")
        so = em.count("SC")

        for c in range(NCH):
            def otr(te, c=c):
                return te.transpose(
                    P.pt[c % 2][:, 0:128],
                    P.scr[:, 0:512].bitcast(BF16)[:, 128 * c:128 * (c + 1)],
                    P.ident[:])
            em.op("tensor", otr,
                  wait=[("SC", so)] if c == 0 else [("V", em.count("V"))],
                  inc="T")
            to = em.count("T")
            em.op("vector",
                  lambda v, qb=qb, c=c: v.tensor_copy(ot_ap(qb, c),
                                                      P.pt[c % 2][:, 0:128]),
                  wait=[("T", to)], inc="V")
    v_ot = em.count("V")
    return emit_l1_attn_part3(P, v_ot)


def emit_l1_attn_part3(P: Prog, v_ot):
    em = P.em
    wo_nat = P.di["L1_attn_Wo"]

    def qT_ap(c):
        buf, off = (P.arnB, 2048 * c) if c < 4 else (P.arnD, 2048 * (c - 4))
        return buf[:, off:off + 2048]

    def ot_ap(qb, c):
        if qb == 0:
            return P.xbf[:, 1024 + 128 * c:1024 + 128 * (c + 1)]
        if qb == 1:
            return P.arnC[:, 1024 + 128 * c:1024 + 128 * (c + 1)]
        return qT_ap(c)[:, 128 * (qb - 2):128 * (qb - 1)]

    for oc in range(NCH):
        emit_wT_col(P, wo_nat, oc, P.wbf[:, 1024 * oc:1024 * (oc + 1)], "wo")
    sc_wo = em.count("SC")

    for oc in range(NCH):
        for ns2 in range(2):
            b0, b1 = P.pb[2 * (ns2 % 2)], P.pb[2 * (ns2 % 2) + 1]

            def ymm(te, oc=oc, ns2=ns2, b0=b0, b1=b1):
                last = None
                for dc in range(NCH):
                    lhsT = P.wbf[:, 1024 * oc + 128 * dc:
                                 1024 * oc + 128 * (dc + 1)]
                    for half in range(2):
                        t0 = 1024 * ns2 + 512 * half
                        for tl in range(4):
                            qb = (t0 + 128 * tl) // 128
                            last = te.matmul(
                                (b0 if half == 0 else b1)[:, 128 * tl:128 * (tl + 1)],
                                lhsT, ot_ap(qb, dc),
                                start=(dc == 0 and tl == 0),
                                stop=(dc == NCH - 1 and tl == 3))
                return last
            w = [("SC", sc_wo), ("V", v_ot)] if (oc == 0 and ns2 == 0) else \
                [("V", em.count("V"))]
            em.op("tensor", ymm, wait=w, inc="T")
            ty = em.count("T")

            em.op("scalar", lambda s, b0=b0, b1=b1: (
                s.copy(P.arnC[:, 0:512], b0[:, 0:512]),
                s.copy(P.arnC[:, 512:1024], b1[:, 0:512]))[-1],
                wait=[("T", ty)], inc="SC")
            s2 = em.count("SC")
            for tl in range(8):
                def ytr(te, tl=tl):
                    return te.transpose(
                        P.pt[tl % 2][:, 0:128],
                        P.arnC[:, 128 * tl:128 * (tl + 1)], P.ident[:])
                em.op("tensor", ytr,
                      wait=[("SC", s2)] if tl == 0 else [("V", em.count("V"))],
                      inc="T")
                tyy = em.count("T")

                def yadd(v, oc=oc, ns2=ns2, tl=tl):
                    tglob = 8 * ns2 + tl
                    col = D * tglob + 128 * oc
                    return v.tensor_add(P.xs[:, col:col + 128],
                                        P.xs[:, col:col + 128],
                                        P.pt[tl % 2][:, 0:128])
                em.op("vector", yadd, wait=[("T", tyy)], inc="V")
    emit_ln(P, wait_v=em.count("V"))


def emit_round(P, src, ab, c, sgn, tscr, out_, wait_sc, scale=None,
               wait_v=()):
    em = P.em

    def sc_op(s):
        s.activation(out=sgn, in_=src, func=ACT.Sign)
        if scale is None:
            s.activation(out=ab, in_=src, func=ACT.Abs)
            return s.activation(out=c, in_=src, func=ACT.Abs)
        s.activation(out=ab, in_=src, func=ACT.Abs, scale=scale)
        return s.activation(out=c, in_=src, func=ACT.Abs, scale=scale)
    em.op("scalar", sc_op, wait=wait_sc, inc="SC")
    ssc = em.count("SC")

    def v_op(v):
        p = 64.0
        for _ in range(7):
            v.tensor_scalar(tscr, c, p, p, op0=ALU.is_ge, op1=ALU.mult)
            v.drain()
            v.tensor_sub(c, c, tscr)
            v.drain()
            p /= 2.0
        v.tensor_sub(ab, ab, c)
        v.drain()
        v.scalar_tensor_tensor(out=tscr, in0=c, scalar=0.5, in1=ab,
                               op0=ALU.is_gt, op1=ALU.add)
        v.drain()
        return v.tensor_tensor(out_, tscr, sgn, op=ALU.mult)
    em.op("vector", v_op, wait=[("SC", ssc)] + list(wait_v), inc="V")


def emit_l1_ffn(P: Prog):
    em = P.em

    wnames = (("L1_ffn_w_gate", P.d_wgT, 4096, 1024),
              ("L1_ffn_w_up", P.d_wuT, 4096, 1024),
              ("L1_ffn_w_down", P.d_wdT, 1024, 4096))
    swcols = {}
    t_prev = None
    sc_copy = []
    lg_tgts = []
    g = 0
    for wi, (nm, dT, R, Cc) in enumerate(wnames):
        w_dram = P.di[nm]
        nrt = R // 128
        CH = min(Cc, 2048)
        idxs = [(rt, ch) for rt in range(nrt) for ch in range(Cc // CH)]
        sc_tgts = []
        for idx, (rt, ch) in enumerate(idxs):
            slot = 2048 * (idx % 2)
            wait = []
            if idx < 2:
                wait = [("V", em.count("V")), ("SC", em.count("SC"))]
            else:
                wait = [("SC", sc_tgts[idx - 2])]
            lw = em.dma("sync", [(P.wstage[:, slot:slot + CH],
                                  w_dram[128 * rt:128 * (rt + 1),
                                         CH * ch:CH * (ch + 1)])],
                        inc="LW", wait=wait)
            em.op("scalar",
                  lambda s, idx=idx, slot=slot, CH=CH: s.activation(
                      out=P.arnB[:, 0:2 * CH].bitcast(F32),
                      in_=P.wstage[:, slot:slot + CH], func=ACT.Abs,
                      accum_out=P.stat[:, idx % 32:idx % 32 + 1]),
                  wait=[("LW", lw)], inc="SC")
            sc_tgts.append(em.count("SC"))
        ncol = len(idxs)
        assert ncol <= 32

        def part(v, ncol=ncol):
            v.drain()
            return v.reduce_sum(P.stat[:, 32:33], P.stat[:, 0:ncol], axis=AX.X)
        em.op("vector", part, wait=[("SC", em.count("SC"))], inc="V")
        vsum = em.count("V")
        em.op("tensor",
              lambda te: te.matmul(P.pb[5][0:1, 0:1], P.onescol_f32[:],
                                   P.stat[:, 32:33], start=True, stop=True),
              wait=[("V", vsum)], inc="T")
        tsum = em.count("T")

        def swmath(v, R=R, Cc=Cc):
            v.tensor_scalar_mul(P.scr[0:1, 0:1], P.pb[5][0:1, 0:1],
                                1.0 / (R * Cc))
            v.drain()
            v.tensor_scalar_max(P.scr[0:1, 0:1], P.scr[0:1, 0:1], 1e-5)
            v.drain()
            v.reciprocal(P.scr[0:1, 1:2], P.scr[0:1, 0:1])
            v.drain()
            v.tensor_scalar_mul(P.scr[0:1, 2:3], P.scr[0:1, 1:2], -1.0)
            v.drain()
            return v.tensor_copy(P.scr[0:1, 3:4], P.scr[0:1, 0:1])
        em.op("vector", swmath, wait=[("T", tsum)], inc="V")
        vsw = em.count("V")
        em.op("tensor",
              lambda te: te.matmul(P.pb[5][:, 0:3], P.ones_f32[:],
                                   P.scr[0:1, 1:4], start=True, stop=True),
              wait=[("V", vsw)], inc="T")
        tbc = em.count("T")
        em.op("scalar",
              lambda s, wi=wi: s.copy(P.stat[:, 34 + 3 * 0:37], P.pb[5][:, 0:3]),
              wait=[("T", tbc)], inc="SC")
        scbc = em.count("SC")
        em.op("vector",
              lambda v, wi=wi: v.tensor_copy(P.stat[:, 40 + wi:41 + wi],
                                             P.stat[:, 36:37]),
              wait=[("SC", scbc)], inc="V")

        v_tgts = {}
        ks = [(rt, ch) for rt in range(nrt) for ch in range((Cc + 2047) // 2048)]
        for k, (rt, ch) in enumerate(ks):
            Cs = min(2048, Cc - 2048 * ch)
            slot = 2048 * (k % 2)
            wait = ([("V", em.count("V")), ("T", em.count("T"))] if k < 2
                    else [("V", v_tgts[k - 2])])
            lw = em.dma("sync", [(P.wstage[:, slot:slot + Cs],
                                  w_dram[128 * rt:128 * (rt + 1),
                                         2048 * ch:2048 * ch + Cs])],
                        inc="LW", wait=wait)

            def quant(v, Cs=Cs, slot=slot):
                v.tensor_scalar(P.arnB[:, 0:2 * Cs].bitcast(F32),
                                P.wstage[:, slot:slot + Cs],
                                P.stat[:, 34:35], 0.5,
                                op0=ALU.mult, op1=ALU.is_gt)
                v.tensor_scalar(P.wstage[:, slot:slot + Cs],
                                P.wstage[:, slot:slot + Cs],
                                P.stat[:, 35:36], 0.5,
                                op0=ALU.mult, op1=ALU.is_gt)
                v.drain()
                return v.tensor_sub(P.arnC[:, 0:Cs],
                                    P.arnB[:, 0:2 * Cs].bitcast(F32),
                                    P.wstage[:, slot:slot + Cs])
            qw = [("LW", lw), ("SC", scbc)]
            if t_prev is not None:
                qw.append(("T", t_prev))
            em.op("vector", quant, wait=qw, inc="V")
            v_tgts[k] = em.count("V")
            nb = Cs // 128
            for b0 in range(0, nb, 8):
                xslot = 1024 * (g % 2)

                def tr(te, b0=b0):
                    last = None
                    for j in range(8):
                        last = te.transpose(
                            P.pt[0][:, 128 * j:128 * (j + 1)],
                            P.arnC[:, 128 * (b0 + j):128 * (b0 + j + 1)],
                            P.ident[:])
                    return last
                tw = [("V", v_tgts[k])]
                if sc_copy:
                    tw.append(("SC", sc_copy[-1]))
                em.op("tensor", tr, wait=tw, inc="T")
                t_prev = em.count("T")
                cw = [("T", t_prev)]
                if g >= 2:
                    cw.append(("LG", lg_tgts[g - 2]))
                em.op("scalar",
                      lambda s, xslot=xslot: s.copy(
                          P.xbf[:, xslot:xslot + 1024], P.pt[0][:, 0:1024]),
                      wait=cw, inc="SC")
                sc_copy.append(em.count("SC"))
                em.dma("gpsimd",
                       [(dT[2048 * ch + 128 * b0:2048 * ch + 128 * b0 + 1024,
                            128 * rt:128 * (rt + 1)]
                         .rearrange("(c p) s -> p c s", p=128),
                         P.xbf[:, xslot:xslot + 1024]
                         .rearrange("p (c s) -> p c s", c=8))],
                       inc="LG", wait=[("SC", sc_copy[-1])])
                lg_tgts.append(em.count("LG"))
                g += 1
    lg_w = em.count("LG")
    if P.dbg("WST", (128, 64), F32) is not None:
        em.tap_barrier(em.dma("sync", [(P.dbg_tensors["WST"], P.stat[:])],
                              inc="ST", wait=[("V", em.count("V")),
                                              ("LG", lg_w)]))
    for nm, dT, sh in (("WGT", P.d_wgT, (D, 4096)), ("WUT", P.d_wuT, (D, 4096)),
                       ("WDT", P.d_wdT, (4096, D))):
        if P.dbg(nm, sh, BF16) is not None:
            em.dma("sync", [(P.dbg_tensors[nm], dT[:, :])], inc="ST",
                   wait=[("LG", lg_w)])
    return emit_l1_ffn_part2(P, lg_w)


def emit_l1_ffn_part2(P: Prog, lg_w):
    em = P.em

    for t in range(NT):
        w = [("LG", lg_w)] if t == 0 else []
        em.op("vector", lambda v, t=t: v.reduce_max(
            P.stat[:, 0 + t:1 + t], P.xs[:, D * t:D * (t + 1)],
            axis=AX.X, apply_absolute_value=True), wait=w, inc="V")
    vam = em.count("V")

    def sxmath(v):
        v.drain()
        v.tensor_scalar_max(P.stat[:, 0:16], P.stat[:, 0:16], 1e-5)
        v.drain()
        v.tensor_scalar_mul(P.stat[:, 16:32], P.stat[:, 0:16], 1.0 / 127.0)
        v.drain()
        v.reciprocal(P.stat[:, 0:16], P.stat[:, 16:32])
        v.drain()
        v.tensor_tensor(P.stat[:, 48:64], P.stat[:, 16:32],
                        P.stat[:, 40:41].broadcast_to([128, 16]), op=ALU.mult)
        v.drain()
        return v.tensor_tensor(P.stat[:, 16:32], P.stat[:, 16:32],
                               P.stat[:, 41:42].broadcast_to([128, 16]),
                               op=ALU.mult)
    em.op("vector", sxmath, wait=[("V", vam)], inc="V")
    v_sx = em.count("V")

    def xqT_ap(c):
        buf, off = (P.arnB, 2048 * c) if c < 4 else (P.arnD, 2048 * (c - 4))
        return buf[:, off:off + 2048]

    xbF = P.xbf.bitcast(F32)
    anF = P.arnA.bitcast(F32)
    xsets = [dict(ab=P.scr[:, 0:512], c=xbF[:, 0:512],
                  sgn=P.scr[:, 512:1024], tscr=xbF[:, 512:1024]),
             dict(ab=anF[:, 0:512], c=anF[:, 512:1024],
                  sgn=anF[:, 1024:1536], tscr=P.pb[4][:, 0:512])]
    xq_v = []
    xq_t = []
    xq_sc = []
    xq_e = []
    for m in range(2 * NT):
        t, h = m // 2, m % 2
        vslot = P.wstage[:, 512 * (m % 2):512 * (m % 2) + 512]
        gw = [("V", v_sx)]
        if m >= 2:
            gw.append(("SC", xq_sc[m - 2]))
        else:
            gw += [("SC", em.count("SC")), ("LG", em.count("LG")),
                   ("LW", em.count("LW"))]
        em.op("gpsimd", lambda g, t=t, h=h, vslot=vslot: g.tensor_tensor(
            vslot, P.xs[:, D * t + 512 * h:D * t + 512 * (h + 1)],
            P.stat[:, 0 + t:1 + t].broadcast_to([128, 512]), op=ALU.mult),
            wait=gw, inc="GP")
        wsc = [("GP", em.count("GP")),
               ("V", xq_v[m - 2] if m >= 2 else em.count("V"))]
        wv = ([("T", xq_t[m - 2])] if m >= 2
              else [("T", em.count("T"))])
        st = xsets[m % 2]
        emit_round(P, src=vslot, ab=st["ab"], c=st["c"], sgn=st["sgn"],
                   tscr=st["tscr"],
                   out_=P.arnC[:, 512 * (m % 2):512 * (m % 2) + 512],
                   wait_sc=wsc, wait_v=wv)
        xq_v.append(em.count("V"))
        xq_sc.append(em.count("SC"))

        def tr(te, m=m):
            last = None
            for jj in range(4):
                last = te.transpose(
                    P.pt[m % 2][:, 128 * jj:128 * (jj + 1)],
                    P.arnC[:, 512 * (m % 2) + 128 * jj:
                            512 * (m % 2) + 128 * (jj + 1)],
                    P.ident[:])
            return last
        tw = [("V", xq_v[-1])]
        tw.append(("V", xq_e[m - 2]) if m >= 2 else ("SC", em.count("SC")))
        em.op("tensor", tr, wait=tw, inc="T")
        xq_t.append(em.count("T"))

        def evac(v, t=t, h=h, m=m):
            return v.tensor_copy(
                (P.arnB if h == 0 else P.arnD)
                [:, 0:8192].rearrange("p (c s) -> p c s", c=4)
                [:, :, 128 * t:128 * (t + 1)],
                P.pt[m % 2][:, 0:512].rearrange("p (c s) -> p c s", c=4))
        em.op("vector", evac, wait=[("T", xq_t[-1])], inc="V")
        xq_e.append(em.count("V"))
    v_xq = em.count("V")
    if P.dbg("XQTB", (128, 8192), BF16) is not None:
        P.dbg("XQTD", (128, 8192), BF16)
        em.dma("sync", [(P.dbg_tensors["XQTB"], P.arnB[:, 0:8192])], inc="ST",
               wait=[("V", v_xq)])
        em.tap_barrier(em.dma("sync",
               [(P.dbg_tensors["XQTD"], P.arnD[:, 0:8192])],
               inc="ST", wait=[("V", v_xq)]))
    if P.dbg("SX", (128, 64), F32) is not None:
        em.dma("sync", [(P.dbg_tensors["SX"], P.stat[:])], inc="ST",
               wait=[("V", v_xq)])
    return emit_l1_ffn_part3(P, v_xq)




def emit_l1_ffn_part3(P: Prog, v_xq):
    em = P.em

    def xqT_ap(c):
        buf, off = (P.arnB, 2048 * c) if c < 4 else (P.arnD, 2048 * (c - 4))
        return buf[:, off:off + 2048]

    em.op("vector", lambda v: v.memset(P.stat[:, 0:16], 0.0),
          wait=[("V", v_xq)], inc="V")
    zm_v = []
    lg_sp = []
    anF0 = P.arnA.bitcast(F32)
    for j in range(8):
        lwg = em.dma("sync", [(P.wbf[:, 0:4096]
                               .rearrange("p (c s) -> p c s", c=8),
                               P.d_wgT[:, 512 * j:512 * (j + 1)]
                               .rearrange("(c p) s -> p c s", p=128)),
                              (P.wbf[:, 4096:8192]
                               .rearrange("p (c s) -> p c s", c=8),
                               P.d_wuT[:, 512 * j:512 * (j + 1)]
                               .rearrange("(c p) s -> p c s", p=128))],
                     inc="LW", wait=[("T", em.count("T")),
                                     ("V", em.count("V"))])
        for t in range(NT):
            k = 16 * j + t
            bg, bu = P.pb[2 * (t % 2)], P.pb[2 * (t % 2) + 1]
            w = [("V", zm_v[k - 2])] if k >= 2 else [("V", em.count("V"))]
            if t == 0:
                w.append(("LW", lwg))

            def gumm(te, t=t, bg=bg, bu=bu):
                last = None
                for ic in range(NCH):
                    lhsT = xqT_ap(ic)[:, 128 * t:128 * (t + 1)]
                    last = te.matmul(bg[:, 0:512], lhsT,
                                     P.wbf[:, 512 * ic:512 * (ic + 1)],
                                     start=(ic == 0), stop=(ic == NCH - 1))
                    last = te.matmul(bu[:, 0:512], lhsT,
                                     P.wbf[:, 4096 + 512 * ic:4096 + 512 * (ic + 1)],
                                     start=(ic == 0), stop=(ic == NCH - 1))
                return last
            em.op("tensor", gumm, wait=w, inc="T")
            tg = em.count("T")

            sslot = P.scr[:, 512 * (t % 2):512 * (t % 2) + 512]
            aslot = anF0[:, 512 * (k % 2):512 * (k % 2) + 512]

            def silu(s, t=t, bg=bg, sslot=sslot):
                return s.activation(out=sslot, in_=bg[:, 0:512],
                                    func=ACT.Silu,
                                    scale=P.stat[:, 48 + t:49 + t])
            em.op("scalar", silu, wait=[("T", tg)], inc="SC")
            ssl = em.count("SC")

            def zmul(v, t=t, bu=bu, sslot=sslot, aslot=aslot):
                v.scalar_tensor_tensor(
                    out=aslot,
                    in0=bu[:, 0:512], scalar=P.stat[:, 16 + t:17 + t],
                    in1=sslot, op0=ALU.mult, op1=ALU.mult)
                v.drain()
                v.reduce_max(P.scr[:, 1024:1025], aslot,
                             axis=AX.X, apply_absolute_value=True)
                v.drain()
                return v.tensor_tensor(P.stat[:, 0 + t:1 + t],
                                       P.stat[:, 0 + t:1 + t],
                                       P.scr[:, 1024:1025], op=ALU.max)
            zw = [("SC", ssl)]
            if k >= 2:
                zw.append(("LG", lg_sp[k - 2]))
            em.op("vector", zmul, wait=zw, inc="V")
            vz = em.count("V")
            zm_v.append(vz)
            em.dma("gpsimd",
                   [(P.d_zspill[128 * t:128 * (t + 1), 512 * j:512 * (j + 1)],
                     aslot)],
                   inc="LG", wait=[("V", vz)])
            lg_sp.append(em.count("LG"))
    lg_z = em.count("LG")
    if P.dbg("ZSP", (S, 4096), F32) is not None:
        em.tap_barrier(em.dma("sync", [(P.dbg_tensors["ZSP"], P.d_zspill[:, :])],
               inc="ST", wait=[("LG", lg_z)]))

    def szmath(v):
        v.drain()
        v.tensor_scalar_max(P.stat[:, 0:16], P.stat[:, 0:16], 1e-5)
        v.drain()
        v.tensor_scalar_mul(P.stat[:, 16:32], P.stat[:, 0:16], 1.0 / 127.0)
        v.drain()
        v.reciprocal(P.stat[:, 48:64], P.stat[:, 16:32])
        v.drain()
        return v.tensor_tensor(P.stat[:, 16:32], P.stat[:, 16:32],
                               P.stat[:, 42:43].broadcast_to([128, 16]),
                               op=ALU.mult)
    em.op("vector", szmath, inc="V")
    v_sz = em.count("V")
    if P.dbg("ZST", (128, 64), F32) is not None:
        em.tap_barrier(em.dma("sync", [(P.dbg_tensors["ZST"], P.stat[:])],
                              inc="ST", wait=[("V", v_sz)]))

    xbF = P.xbf.bitcast(F32)
    anF = P.arnA.bitcast(F32)
    zsets = [dict(ab=P.scr[:, 0:512], c=xbF[:, 0:512],
                  sgn=P.scr[:, 512:1024], tscr=xbF[:, 512:1024]),
             dict(ab=anF[:, 0:512], c=anF[:, 512:1024],
                  sgn=anF[:, 1024:1536], tscr=P.pb[4][:, 0:512])]
    zq_v = []
    zq_t = []
    zq_c = []
    q = 0
    for tq in range(4):
        for ti in range(4):
            t = 4 * tq + ti
            lz = em.dma("sync", [(P.wstage[:, 0:4096],
                                  P.d_zspill[128 * t:128 * (t + 1), :])],
                        inc="LW", wait=[("LG", lg_z), ("V", em.count("V")),
                                        ("T", em.count("T")),
                                        ("GP", em.count("GP")),
                                        ("SC", em.count("SC"))])
            for sl in range(8):
                srcw = P.wstage[:, 512 * sl:512 * (sl + 1)]
                em.op("gpsimd", lambda g_, srcw=srcw, t=t: g_.tensor_tensor(
                    srcw, srcw,
                    P.stat[:, 48 + t:49 + t].broadcast_to([128, 512]),
                    op=ALU.mult),
                    wait=[("LW", lz), ("V", v_sz)], inc="GP")
                wsc = [("GP", em.count("GP")),
                       ("V", zq_v[q - 2] if q >= 2 else em.count("V"))]
                wv = [("T", zq_t[q - 2])] if q >= 2 else []
                st = zsets[q % 2]
                emit_round(P, src=srcw, ab=st["ab"], c=st["c"],
                           sgn=st["sgn"], tscr=st["tscr"],
                           out_=P.arnC[:, 512 * (q % 2):512 * (q % 2) + 512],
                           wait_sc=wsc, wait_v=wv)
                zq_v.append(em.count("V"))

                def ztr(te, q=q):
                    last = None
                    for jj in range(4):
                        last = te.transpose(
                            P.pt[q % 2][:, 128 * jj:128 * (jj + 1)],
                            P.arnC[:, 512 * (q % 2) + 128 * jj:
                                    512 * (q % 2) + 128 * (jj + 1)],
                            P.ident[:])
                    return last
                tw = [("V", zq_v[-1])]
                if q >= 2:
                    tw.append(("SC", zq_c[q - 2]))
                em.op("tensor", ztr, wait=tw, inc="T")
                zq_t.append(em.count("T"))
                em.op("scalar",
                      lambda s_, ti=ti, sl=sl, q=q: s_.copy(
                          (P.arnB if ti < 2 else P.arnD)
                          [:, 4096 * (ti % 2) + 512 * sl:
                           4096 * (ti % 2) + 512 * (sl + 1)],
                          P.pt[q % 2][:, 0:512]),
                      wait=[("T", zq_t[-1])], inc="SC")
                zq_c.append(em.count("SC"))
                q += 1
        v_zq = em.count("V")

        def obank(ti, half):
            if ti < 3:
                return P.pb[2 * ti + half][:, 0:512]
            return P.pt[half].bitcast(F32)[:, 0:512]
        lwd = [em.dma("sync", [(P.xbf[:, 0:1024], P.d_wdT[0:128, :])],
                      inc="LW", wait=[("T", em.count("T")),
                                      ("SC", em.count("SC"))])]
        for c in range(32):
            if c + 1 < 32:
                h = 1024 * ((c + 1) % 2)
                lwd.append(em.dma(
                    "sync", [(P.xbf[:, h:h + 1024],
                              P.d_wdT[128 * (c + 1):128 * (c + 2), :])],
                    inc="LW", wait=[("T", em.count("T"))]))
            for ti in range(4):
                def dmm(te, c=c, ti=ti):
                    zq = (P.arnB if ti < 2 else P.arnD)
                    lhsT = zq[:, 4096 * (ti % 2):4096 * (ti % 2) + 4096] \
                        .rearrange("p (cc s) -> p cc s", cc=32)[:, c, :]
                    h = 1024 * (c % 2)
                    te.matmul(obank(ti, 0), lhsT, P.xbf[:, h:h + 512],
                              start=(c == 0), stop=(c == 31))
                    return te.matmul(obank(ti, 1), lhsT,
                                     P.xbf[:, h + 512:h + 1024],
                                     start=(c == 0), stop=(c == 31))
                em.op("tensor", dmm, wait=[("LW", lwd[c])] if ti == 0 else [],
                      inc="T")
        td = em.count("T")
        if tq == 0 and P.dbg("ZQT0B", (128, 8192), BF16) is not None:
            P.dbg("ZQT0D", (128, 8192), BF16)
            em.dma("sync", [(P.dbg_tensors["ZQT0B"], P.arnB[:, 0:8192])],
                   inc="ST", wait=[("T", td)])
            em.tap_barrier(em.dma("sync",
                   [(P.dbg_tensors["ZQT0D"], P.arnD[:, 0:8192])],
                   inc="ST", wait=[("T", td)]))
        for ti in range(4):
            t = 4 * tq + ti

            def yadd(v, t=t, ti=ti):
                v.scalar_tensor_tensor(
                    out=P.xs[:, D * t:D * t + 512],
                    in0=obank(ti, 0),
                    scalar=P.stat[:, 16 + t:17 + t],
                    in1=P.xs[:, D * t:D * t + 512],
                    op0=ALU.mult, op1=ALU.add)
                v.drain()
                return v.scalar_tensor_tensor(
                    out=P.xs[:, D * t + 512:D * (t + 1)],
                    in0=obank(ti, 1),
                    scalar=P.stat[:, 16 + t:17 + t],
                    in1=P.xs[:, D * t + 512:D * (t + 1)],
                    op0=ALU.mult, op1=ALU.add)
            em.op("vector", yadd, wait=[("T", td)], inc="V")
    if P.dbg("Y4", (128, NT * D), F32) is not None:
        em.tap_barrier(em.dma("sync", [(P.dbg_tensors["Y4"], P.xs[:])],
                              inc="ST", wait=[("V", em.count("V"))]))
    emit_ln(P, wait_v=em.count("V"))


# version 08 (final)
# speedup vs baseline: 1.0946x; 1.0946x over previous
import math

import numpy as np
import ml_dtypes

import concourse.bass as bass
import concourse.mybir as mybir
from concourse.bass_utils import run_bass_kernel_spmd

F32 = mybir.dt.float32
BF16 = mybir.dt.bfloat16
ALU = mybir.AluOpType
ACT = mybir.ActivationFunctionType
AX = mybir.AxisListType

BF = ml_dtypes.bfloat16

S = 2048
D = 1024
NT = S // 128
NCH = D // 128
NEG = -1.0e30

N_CORES = 8



ENGINES = ("sync", "tensor", "vector", "scalar", "gpsimd")


class Em:

    def __init__(self, nc):
        self.nc = nc
        self.progs = {e: [] for e in ENGINES}
        self.counts = {}
        self.sems = {}
        self.sem_names = []

    def sem(self, name):
        if name not in self.counts:
            self.counts[name] = 0
            self.sem_names.append(name)
        return name

    def op(self, eng, fn, wait=(), inc=None, inc_by=1):
        tgt = None
        if inc is not None:
            self.sem(inc)
            self.counts[inc] += inc_by
            tgt = self.counts[inc]
        wait = tuple(wait)
        for s, _v in wait:
            self.sem(s)

        def thunk(e, fn=fn, wait=wait, inc=inc, inc_by=inc_by):
            for s, v in wait:
                e.wait_ge(self.sems[s], v)
            last = fn(e)
            if inc is not None:
                assert last is not None, "inc set but fn returned no instruction"
                last.then_inc(self.sems[inc], inc_by)

        self.progs[eng].append(thunk)
        return tgt

    def dma(self, eng, pairs, inc, wait=()):
        self.sem(inc)
        wait = tuple(wait)
        for s, _v in wait:
            self.sem(s)
        self.counts[inc] += 16 * len(pairs)
        tgt = self.counts[inc]

        def thunk(e, pairs=tuple(pairs), wait=wait, inc=inc):
            for s, v in wait:
                e.wait_ge(self.sems[s], v)
            for out_ap, in_ap in pairs:
                e.dma_start(out=out_ap, in_=in_ap).then_inc(self.sems[inc], 16)

        self.progs[eng].append(thunk)
        return tgt

    def count(self, sem):
        return self.counts.get(sem, 0)

    def tap_barrier(self, tgt):
        self.sem("ST")
        for e in ENGINES:
            self.progs[e].append(
                lambda eng, tgt=tgt: eng.wait_ge(self.sems["ST"], tgt))

    def replay(self):
        nc = self.nc
        import contextlib

        with contextlib.ExitStack() as stack:
            for name in self.sem_names:
                self.sems[name] = stack.enter_context(nc.semaphore(name))
            with nc.Block() as block:
                @block.sync
                def _(sync):
                    for t in self.progs["sync"]:
                        t(sync)

                @block.tensor
                def _(tensor):
                    for t in self.progs["tensor"]:
                        t(tensor)

                @block.vector
                def _(vector):
                    for t in self.progs["vector"]:
                        t(vector)

                @block.scalar
                def _(scalar):
                    for t in self.progs["scalar"]:
                        t(scalar)

                @block.gpsimd
                def _(gpsimd):
                    for t in self.progs["gpsimd"]:
                        t(gpsimd)



def make_constants():
    c = {}
    c["ident"] = np.eye(128, dtype=BF)
    c["ident_f32"] = np.eye(128, dtype=np.float32)
    c["ones_bf"] = np.ones((1, 128), dtype=BF)

    for r in range(2):
        j = np.arange(128)[:, None]
        i = np.arange(256)[None, :]
        m = (128 * r + j > i).astype(np.float32) * NEG
        c[f"mask_l0a_{r}"] = m.astype(BF)

    for rel in (-4, -3, 0, 1):
        j = np.arange(128)[:, None]
        i = np.arange(256)[None, :]
        qmk = (i) - (128 * rel + j)
        m = ((qmk < 0) | (qmk > 512)).astype(np.float32) * NEG
        c[f"mask_l0f_{rel}"] = m.astype(BF)

    for rel in (-1, 1):
        j = np.arange(128)[:, None]
        i = np.arange(128)[None, :]
        diff = (i) - (128 * rel + j)
        m = (np.abs(diff) > 128).astype(np.float32) * NEG
        c[f"mask_l1a_{rel}"] = m.astype(BF)

    DH = 64
    half = np.arange(0, DH, 2, dtype=np.float64)
    inv_freq = 1.0 / 10000.0 ** (half / DH)
    t = np.arange(S, dtype=np.float64)
    freqs = t[:, None] * inv_freq[None, :]
    freqs = np.concatenate([freqs, freqs], axis=1)
    base = (half + 0.4 * DH) / (1.4 * DH)
    power = (t - S // 2) / 256.0
    scale = base[None, :] ** power[:, None]
    scale = np.concatenate([scale, scale], axis=1)
    cos, sin = np.cos(freqs), np.sin(freqs)
    sgn = np.concatenate([-np.ones(32), np.ones(32)])[None, :]

    sm = np.zeros((128, 128), dtype=np.float32)
    for m in range(1, 128):
        sm[m - 1, m] = 1.0
    c["shiftmat"] = sm
    se = np.zeros((128, 128), dtype=np.float32)
    se[127, 0] = 1.0
    c["sel127"] = se

    def tokmaj(a):
        return np.ascontiguousarray(
            a.reshape(NT, 128, 64).transpose(1, 0, 2).reshape(128, NT * 64))
    c["rotq_cos"] = tokmaj(cos * scale * 8.0).astype(BF)
    c["rotq_sin"] = tokmaj(sin * scale * sgn * 8.0).astype(BF)
    c["rotk_cos"] = tokmaj(cos / scale).astype(np.float32)
    c["rotk_sin"] = tokmaj(sin / scale * sgn).astype(np.float32)
    return c



class TruncSchedule(Exception):
    pass


class Prog:
    def __init__(self, debug_taps=()):
        self.trunc = False
        nc = bass.Bass("TRN2", target_bir_lowering=False)
        self.nc = nc
        self.em = Em(nc)
        self.debug_taps = set(debug_taps)
        self.dbg_tensors = {}

        di = {}
        def din(name, shape):
            di[name] = nc.dram_tensor(name, list(shape), F32, kind="ExternalInput").ap()
        din("x_in", (S, D))
        din("L0_attn_Wk", (D, D)); din("L0_attn_bk", (D,))
        din("L0_attn_Wv", (D, 4096)); din("L0_attn_bv", (4096,))
        din("L0_attn_Wf", (4096, D)); din("L0_attn_bf", (D,))
        din("L0_ffn_wqkv", (2, D, 1536)); din("L0_ffn_q_scale", (2, 64))
        din("L0_ffn_k_scale", (2, 64)); din("L0_ffn_wo", (2, 512, D))
        din("L0_ffn_ln_g", (2, D)); din("L0_ffn_ln_b", (2, D))
        din("L0_ffn_w1", (2, D, 5460)); din("L0_ffn_b1", (2, 5460))
        din("L0_ffn_w2", (2, 2730, D)); din("L0_ffn_b2", (2, D))
        din("L1_attn_Wq", (D, D)); din("L1_attn_Wk", (D, D))
        din("L1_attn_Wv", (D, D)); din("L1_attn_Wo", (D, D))
        din("L1_ffn_w_gate", (4096, D)); din("L1_ffn_w_up", (4096, D))
        din("L1_ffn_w_down", (D, 4096))
        self.di = di

        self.consts = make_constants()
        self.dc = {}
        for k, v in self.consts.items():
            dt = BF16 if v.dtype == BF else F32
            self.dc[k] = nc.dram_tensor("c_" + k, list(v.shape), dt,
                                        kind="ExternalInput").ap()

        self.dr = {}
        for name, shape in (("bkT", (128, 8)), ("b1aT", (128, 44)),
                            ("b1gT", (128, 44)), ("lngT", (128, 16)),
                            ("lnbT", (128, 16)), ("b2T", (128, 16))):
            self.dr[name] = nc.dram_tensor("r_" + name, list(shape), F32,
                                           kind="ExternalInput").ap()

        self.d_out = nc.dram_tensor("x_out", [S, D], F32, kind="ExternalOutput").ap()

        self.d_xspill = nc.dram_tensor("xspill", [S, D], F32, kind="Internal").ap()
        self.d_zspill = nc.dram_tensor("zspill", [S, 4096], F32, kind="Internal").ap()
        self.d_kspill = nc.dram_tensor("kspill", [D, S], BF16, kind="Internal").ap()
        self.d_wgT = nc.dram_tensor("wgT", [D, 4096], BF16, kind="Internal").ap()
        self.d_wuT = nc.dram_tensor("wuT", [D, 4096], BF16, kind="Internal").ap()
        self.d_wdT = nc.dram_tensor("wdT", [4096, D], BF16, kind="Internal").ap()

        A = nc.alloc_sbuf_tensor
        self.xs = A("xs", [128, NT * D], F32)
        self.xbf = A("xbf", [128, 2 * D], BF16)
        self.ident = A("ident", [128, 128], BF16)
        self.ident_f32 = A("ident_f32", [128, 128], F32)
        self.ones_bf = A("ones_bf", [1, 128], BF16)
        self.mask = {}
        for k in self.consts:
            if k.startswith("mask_"):
                sh = self.consts[k].shape
                self.mask[k] = A(k, list(sh), BF16)

        self.bkT = A("bkT", [128, 8], F32)
        self.bv_row = A("bv_row", [1, 4096], BF16)
        self.bf_row = A("bf_row", [1, D], BF16)
        self.b2T = A("b2T", [128, 16], F32)
        self.b1aT = A("b1aT", [128, 2 * 22], F32)
        self.b1gT = A("b1gT", [128, 2 * 22], F32)
        self.lngT = A("lngT", [128, 2 * 8], F32)
        self.lnbT = A("lnbT", [128, 2 * 8], F32)
        self.qs_row = A("qs_row", [1, 128], F32)
        self.ks_row = A("ks_row", [1, 128], F32)
        self.shiftmat = A("shiftmat", [128, 128], F32)
        self.sel127 = A("sel127", [128, 128], F32)
        self.ones_f32 = A("ones_f32", [1, 128], F32)
        self.ones_col = A("ones_col", [128, 1], BF16)
        self.onescol_f32 = A("onescol_f32", [128, 1], F32)

        self.xT = A("xT", [128, NCH * S], BF16)
        self.arnA = A("arnA", [128, 3072], BF16)
        self.arnB = A("arnB", [128, 8224], BF16)
        self.arnC = A("arnC", [128, 2048], BF16)
        self.arnD = A("arnD", [128, 8192], BF16)
        self.wstage = A("wstage", [128, 2 * 2048], F32)
        self.wbf = A("wbf", [128, 12288], BF16)
        self.scr = A("scr", [128, 1088], F32)
        self.stat = A("stat", [128, 64], F32)

        P = nc.alloc_psum_tensor
        self.pb = [P(f"pb{i}", [128, 512], F32) for i in range(6)]
        self.pt = [P(f"pt{i}", [128, 1024], BF16) for i in range(2)]

        self.ew = self.em

    def dbg(self, name, shape, dtype=F32):
        if name in self.debug_taps and name not in self.dbg_tensors:
            self.dbg_tensors[name] = self.nc.dram_tensor(
                "dbg_" + name, list(shape), dtype, kind="ExternalOutput").ap()
        return self.dbg_tensors.get(name)


def bf(a):
    return np.ascontiguousarray(a, dtype=BF)


def f32(a):
    return np.ascontiguousarray(a, dtype=np.float32)



def emit_preamble(P: Prog):
    em = P.em
    pairs = [(P.xs[:].rearrange("p (t d) -> p t d", t=NT),
              P.di["x_in"].rearrange("(t p) d -> p t d", p=128))]
    pairs.append((P.ident[:], P.dc["ident"]))
    pairs.append((P.ident_f32[:], P.dc["ident_f32"]))
    pairs.append((P.ones_bf[:], P.dc["ones_bf"]))
    for k in P.mask:
        pairs.append((P.mask[k][:], P.dc[k]))
    pairs.append((P.bkT[:], P.dr["bkT"]))
    pairs.append((P.b1aT[:], P.dr["b1aT"]))
    pairs.append((P.b1gT[:], P.dr["b1gT"]))
    pairs.append((P.lngT[:], P.dr["lngT"]))
    pairs.append((P.lnbT[:], P.dr["lnbT"]))
    pairs.append((P.qs_row[:], P.di["L0_ffn_q_scale"].rearrange("l d -> (l d)")[None, :]))
    pairs.append((P.ks_row[:], P.di["L0_ffn_k_scale"].rearrange("l d -> (l d)")[None, :]))
    pairs.append((P.shiftmat[:], P.dc["shiftmat"]))
    pairs.append((P.sel127[:], P.dc["sel127"]))
    pairs.append((P.wstage[0:1, 0:4096], P.di["L0_attn_bv"][None, :]))
    pairs.append((P.scr[0:1, 0:1024], P.di["L0_attn_bf"][None, :]))
    pairs.append((P.b2T[:], P.dr["b2T"]))
    lw = em.dma("sync", pairs, inc="LW")

    def casts(v):
        v.tensor_copy(P.bv_row[:], P.wstage[0:1, 0:4096])
        v.tensor_copy(P.bf_row[:], P.scr[0:1, 0:1024])
        v.memset(P.ones_col[:], 1.0)
        v.memset(P.onescol_f32[:], 1.0)
        return v.memset(P.ones_f32[:], 1.0)
    em.op("vector", casts, wait=[("LW", lw)], inc="V")
    P.pre_LW = em.count("LW")
    P.pre_V = em.count("V")


def emit_xT(P: Prog, src_f32=None, dst=None, shift_cols=None, wait_extra=(),
            tag=""):
    em = P.em
    nc = P.nc
    src = src_f32 if src_f32 is not None else P.xs
    dst = dst if dst is not None else P.xT
    if shift_cols:
        def zf(v):
            last = None
            for c in shift_cols:
                last = v.memset(dst[:, 2048 * c: 2048 * c + 1], 0.0)
            return last
        em.op("vector", zf, wait=wait_extra, inc="V")
    for t in range(NT):
        half = t % 2
        def cast(v, t=t, half=half):
            return v.tensor_copy(P.xbf[:, half * D:(half + 1) * D],
                                 src[:, t * D:(t + 1) * D])
        w = list(wait_extra) if t == 0 else []
        if t >= 2:
            pass
        em.op("vector", cast, wait=w, inc="V")
        vt = em.count("V")

        def trans(te, t=t, half=half):
            last = None
            for c in range(NCH):
                last = te.transpose(
                    P.pt[half][:, 128 * c:128 * (c + 1)],
                    P.xbf[:, half * D + 128 * c: half * D + 128 * (c + 1)],
                    P.ident[:])
            return last
        em.op("tensor", trans, wait=[("V", vt)], inc="T")
        tt = em.count("T")

        def evac(v, t=t, half=half):
            last = None
            for c in range(NCH):
                off = 1 if (shift_cols and c in shift_cols) else 0
                if t == NT - 1 and off == 1:
                    last = v.tensor_copy(
                        dst[:, 2048 * c + 128 * t + 1: 2048 * c + 128 * t + 128],
                        P.pt[half][:, 128 * c: 128 * c + 127])
                else:
                    last = v.tensor_copy(
                        dst[:, 2048 * c + 128 * t + off:
                            2048 * c + 128 * t + 128 + off],
                        P.pt[half][:, 128 * c:128 * (c + 1)])
            return last
        em.op("vector", evac, wait=[("T", tt)], inc="V")
    return em.count("V"), em.count("T")


def emit_ln(P: Prog, wait_v=None, wait_t=None, wait_s=None, out_bf_cb=None,
            gamma_col=None, beta_col=None, src=None):
    em = P.em
    xs = src if src is not None else P.xs
    for t in range(NT):
        w = []
        if t == 0:
            if wait_v is not None:
                w.append(("V", wait_v))
            if wait_t is not None:
                w.append(("T", wait_t))
            if wait_s is not None:
                w.append(("SC", wait_s))
        em.op("vector",
              lambda v, t=t: v.reduce_sum(P.stat[:, t:t + 1],
                                          xs[:, t * D:(t + 1) * D], axis=AX.X),
              wait=w, inc="V")
        em.op("scalar",
              lambda s, t=t: s.activation(out=P.scr[:, 0:D],
                                          in_=xs[:, t * D:(t + 1) * D],
                                          func=ACT.Square,
                                          accum_out=P.stat[:, 16 + t:16 + t + 1]),
              wait=w, inc="SC")
    vstats = em.count("V")
    sstats = em.count("SC")

    def stats_math(v):
        v.drain()
        v.tensor_scalar_mul(P.stat[:, 32:48], P.stat[:, 0:16], 1.0 / D)
        v.tensor_scalar_mul(P.stat[:, 48:64], P.stat[:, 16:32], 1.0 / D)
        v.drain()
        v.tensor_tensor(P.scr[:, 0:16], P.stat[:, 32:48], P.stat[:, 32:48],
                        op=ALU.mult)
        v.drain()
        v.tensor_sub(P.stat[:, 48:64], P.stat[:, 48:64], P.scr[:, 0:16])
        v.drain()
        return v.tensor_scalar_add(P.stat[:, 48:64], P.stat[:, 48:64], 1e-5)
    em.op("vector", stats_math, wait=[("SC", sstats)], inc="V")
    vsm = em.count("V")
    if P.dbg("STAT2", (128, 64), F32) is not None:
        em.tap_barrier(em.dma("sync", [(P.dbg_tensors["STAT2"], P.stat[:])],
                       inc="ST", wait=[("V", vsm)]))

    def rsq(s):
        return s.activation(out=P.scr[:, 0:16], in_=P.stat[:, 48:64],
                            func=ACT.Sqrt)
    em.op("scalar", rsq, wait=[("V", vsm)], inc="SC")
    ssq = em.count("SC")

    def recip(v):
        v.reciprocal(P.stat[:, 48:64], P.scr[:, 0:16])
        v.drain()
        v.tensor_tensor(P.stat[:, 32:48], P.stat[:, 32:48], P.stat[:, 48:64],
                        op=ALU.mult)
        v.drain()
        return v.tensor_scalar_mul(P.stat[:, 32:48], P.stat[:, 32:48], -1.0)
    em.op("vector", recip, wait=[("SC", ssq)], inc="V")
    vrec = em.count("V")
    if P.dbg("STAT", (128, 64), F32) is not None:
        em.tap_barrier(em.dma("sync", [(P.dbg_tensors["STAT"], P.stat[:])],
                       inc="ST", wait=[("V", vrec)]))

    for t in range(NT):
        w = [("V", vrec)] if t == 0 else []

        def apply_f32(s, t=t):
            return s.activation(out=xs[:, t * D:(t + 1) * D],
                                in_=xs[:, t * D:(t + 1) * D], func=ACT.Identity,
                                bias=P.stat[:, 32 + t:33 + t],
                                scale=P.stat[:, 48 + t:49 + t])
        em.op("scalar", apply_f32, wait=w, inc="SC")
        st = em.count("SC")
        if out_bf_cb is not None:
            em.op("vector",
                  lambda v, t=t: v.tensor_copy(out_bf_cb(t),
                                               xs[:, t * D:(t + 1) * D]),
                  wait=[("SC", st)], inc="V")
    if gamma_col is not None:
        raise NotImplementedError("inner LN gamma/beta handled separately")
    return em.count("V"), em.count("SC")


def emit_l0_attn(P: Prog):
    em = P.em
    em.op("vector", lambda v: v.tensor_scalar_mul(P.xs[:], P.xs[:], 2.0),
          inc="V")
    nc = P.nc
    KT = P.arnA
    ptil_off = 2048
    Vb = P.arnB
    Gb = P.arnC
    GT = P.arnD
    wk_bf = P.wbf[:, 0:1024]
    wv_bf = P.wbf[:, 1024:1024 + 4096]
    wf_bf = P.wbf[:, 5120:5120 + 4096]

    x_in = P.di

    for p in range(8):
        wk_dram = P.di["L0_attn_Wk"].rearrange("(ic q) o -> q ic o", q=128)
        lw0 = em.dma("sync", [
            (P.wstage[:, 0:1024].rearrange("q (ic o) -> q ic o", ic=8),
             wk_dram[:, :, 128 * p:128 * (p + 1)]),
        ], inc="LW", wait=[("V", em.count("V"))])
        em.op("vector", lambda v: v.tensor_copy(wk_bf, P.wstage[:, 0:1024]),
              wait=[("LW", lw0), ("T", em.count("T"))], inc="V")
        vk = em.count("V")

        wv_dram = P.di["L0_attn_Wv"].rearrange("(ic q) o -> q ic o", q=128)
        for hhalf in range(2):
            slot = 2048 * (1 - hhalf)
            lwv = em.dma("sync", [
                (P.wstage[:, slot:slot + 2048]
                 .rearrange("q (ic o) -> q ic o", ic=4),
                 wv_dram[:, 4 * hhalf:4 * hhalf + 4, 512 * p:512 * (p + 1)]),
            ], inc="LW", wait=[("V", em.count("V") if hhalf == 0 else vk)])
            em.op("vector",
                  lambda v, hh=hhalf, slot=slot: v.tensor_copy(
                      wv_bf[:, 2048 * hh:2048 * (hh + 1)],
                      P.wstage[:, slot:slot + 2048]),
                  wait=[("LW", lwv), ("T", em.count("T"))], inc="V")
        vv = em.count("V")

        wf_dram = P.di["L0_attn_Wf"].rearrange("(ic q) o -> q ic o", q=128)
        lwf = em.dma("sync", [
            (P.wstage[:, 0:2048].rearrange("q (ic o) -> q ic o", ic=2),
             wf_dram[:, 4 * p: 4 * p + 2, :]),
            (P.wstage[:, 2048:4096].rearrange("q (ic o) -> q ic o", ic=2),
             wf_dram[:, 4 * p + 2: 4 * p + 4, :]),
        ], inc="LW", wait=[("V", vv)])
        em.op("vector", lambda v: v.tensor_copy(wf_bf, P.wstage[:, 0:4096]),
              wait=[("LW", lwf), ("T", em.count("T"))], inc="V")
        vwf = em.count("V")

        if p == 0:
            em.op("vector",
                  lambda v: v.memset(
                      Vb[:, 0:NT * 514]
                      .rearrange("q (t h c) -> q t h c", t=NT, h=2)[:, :, :, 256:257],
                      1.0),
                  inc="V")

        tprev = em.count("T")

        def kproj(te, vk=vk):
            last = None
            for ic in range(NCH):
                for ns in range(4):
                    last = te.matmul(
                        P.pb[ns][:, 0:512],
                        wk_bf[:, 128 * ic:128 * (ic + 1)],
                        P.xT[:, 2048 * ic + 512 * ns: 2048 * ic + 512 * (ns + 1)],
                        start=(ic == 0), stop=(ic == NCH - 1))
            return last
        em.op("tensor", kproj, wait=[("V", vk)], inc="T")
        tk = em.count("T")
        for ns in range(4):
            em.op("scalar",
                  lambda s, ns=ns, p=p: s.activation(
                      out=KT[:, 512 * ns:512 * (ns + 1)],
                      in_=P.pb[ns][:, 0:512], func=ACT.Identity,
                      bias=P.bkT[:, p:p + 1]),
                  wait=[("T", tk)] if ns == 0 else [], inc="SC")
        sck = em.count("SC")

        def vproj_t(te, t, vwait, p=p):
            def f(te_, t=t, p=p):
                last = None
                bank = P.pb[4 + (t % 2)]
                for ic in range(NCH):
                    last = te_.matmul(
                        bank[:, 0:512],
                        P.xT[:, 2048 * ic + 128 * t: 2048 * ic + 128 * (t + 1)],
                        wv_bf[:, 512 * ic:512 * (ic + 1)],
                        start=(ic == 0), stop=False)
                last = te_.matmul(bank[:, 0:512], P.ones_bf[:],
                                  P.bv_row[:, 512 * p:512 * (p + 1)],
                                  start=False, stop=True)
                return last
            return f

        for t in range(NT):
            w = [("V", vv)] if t == 0 else []
            if t >= 2:
                w.append(("SC", em.count("SC")))
            em.op("tensor", vproj_t(None, t, None), wait=w, inc="T")
            tv = em.count("T")
            em.op("scalar",
                  lambda s, t=t: s.copy(
                      Vb[:, 514 * t:514 * t + 514]
                      .rearrange("q (h c) -> q h c", h=2)[:, :, 0:256],
                      P.pb[4 + (t % 2)][:, 0:512]
                      .rearrange("q (h c) -> q h c", h=2)),
                  wait=[("T", tv)], inc="SC")
        scv = em.count("SC")

        if p == 0:
            if P.dbg("KT", (128, 2048), BF16) is not None:
                em.tap_barrier(em.dma("sync", [(P.dbg_tensors["KT"], KT[:, 0:2048])],
                       inc="ST", wait=[("SC", sck)]))
            if P.dbg("V0", (128, 8224), BF16) is not None:
                em.tap_barrier(em.dma("sync", [(P.dbg_tensors["V0"], Vb[:, 0:8224])],
                       inc="ST", wait=[("SC", scv)]))

        def obank(i):
            if i < 2:
                return P.pb[4 + i][:]
            return P.pt[i - 2][:].bitcast(F32)
        def score_mm(te, qp, kb, slot, p=p):
            rel = kb - 2 * qp
            last = None
            for h in range(2):
                bank = P.pb[2 * slot + h]
                lhsT = KT[64 * h:64 * (h + 1), 128 * kb:128 * (kb + 1)]
                rhs = P.xT[64 * h:64 * (h + 1),
                           2048 * p + 256 * qp: 2048 * p + 256 * (qp + 1)]
                last = te.matmul(bank[:, 0:256], lhsT, rhs,
                                 start=True, stop=(rel < 0))
                if rel >= 0:
                    last = te.matmul(bank[:, 0:256], P.ident[:],
                                     P.mask[f"mask_l0a_{rel}"][:],
                                     start=False, stop=True)
            return last

        for qp in range(8):
            nkb = 2 * qp + 2
            sc_exp = {}
            t_pv_done_for_slot = {}
            for kb in range(nkb):
                slot = kb % 2
                w = []
                if kb >= 2:
                    w.append(("SC", sc_exp[kb - 2]))
                elif qp > 0:
                    w.append(("SC", em.count("SC")))
                    w.append(("V", em.count("V")))
                else:
                    w.append(("SC", scv))
                em.op("tensor", lambda te, qp=qp, kb=kb, slot=slot,
                      f=score_mm: f(te, qp, kb, slot), wait=w, inc="T")
                ts = em.count("T")
                if p == 0 and qp == 0 and kb == 0 and \
                        P.dbg("SC00", (128, 768), F32) is not None:
                    def ctl(te):
                        te.matmul(P.pb[2][:, 0:256],
                                  P.ident[:], P.mask["mask_l0a_0"][:],
                                  start=True, stop=True)
                        return te.matmul(P.pb[3][:, 0:256],
                                         KT[:, 0:128], P.xT[:, 0:256],
                                         start=True, stop=True)
                    em.op("tensor", ctl, wait=[("T", ts)], inc="T")
                    tc = em.count("T")
                    em.op("vector", lambda v: (
                        v.tensor_copy(P.scr[:, 0:256], P.pb[0][:, 0:256]),
                        v.tensor_copy(P.scr[:, 256:512], P.pb[2][:, 0:256]),
                        v.tensor_copy(P.scr[:, 512:768], P.pb[3][:, 0:256]))[-1],
                        wait=[("T", tc)], inc="V")
                    em.tap_barrier(em.dma("sync",
                        [(P.dbg_tensors["SC00"], P.scr[:, 0:768])],
                        inc="ST", wait=[("V", em.count("V"))]))
                    if P.trunc == 1:
                        raise TruncSchedule()
                def do_exp(s, slot=slot, kb=kb):
                    last = None
                    for h in range(2):
                        last = s.activation(
                            out=P.arnA[:, ptil_off + 256 * (2 * h + slot):
                                       ptil_off + 256 * (2 * h + slot + 1)],
                            in_=P.pb[2 * slot + h][:, 0:256], func=ACT.Exp)
                    return last
                we = [("T", ts)]
                if kb >= 2:
                    we = [("T", max(ts, t_pv_done_for_slot.get(slot, 0)))]
                em.op("scalar", do_exp, wait=we, inc="SC")
                sc_exp[kb] = em.count("SC")
                if p == 0 and qp == 0 and kb == 0 and \
                        P.dbg("PT00", (128, 1024), BF16) is not None:
                    em.tap_barrier(em.dma("sync", [(P.dbg_tensors["PT00"],
                                     P.arnA[:, 2048:3072])],
                           inc="ST", wait=[("SC", sc_exp[kb])]))
                def do_pv(te, qp=qp, kb=kb, slot=slot):
                    last = None
                    for h in range(2):
                        ptile = P.arnA[:, ptil_off + 256 * (2 * h + slot):
                                       ptil_off + 256 * (2 * h + slot + 1)]
                        for qq in range(2):
                            qb = 2 * qp + qq
                            if kb > qb:
                                continue
                            last = te.matmul(
                                obank(2 * h + qq)[:, 0:257],
                                ptile[:, 128 * qq:128 * (qq + 1)],
                                Vb[:, 514 * kb + 257 * h:
                                   514 * kb + 257 * (h + 1)],
                                start=(kb == 0), stop=(kb == qb))
                    return last
                em.op("tensor", do_pv, wait=[("SC", sc_exp[kb])], inc="T")
                t_pv_done_for_slot[slot] = em.count("T")
            tpv = em.count("T")
            def recips(v, qp=qp):
                last = None
                for h in range(2):
                    for qq in range(2):
                        last = v.reciprocal(P.stat[:, 40 + 2 * h + qq:
                                                   41 + 2 * h + qq],
                                            obank(2 * h + qq)[:, 256:257])
                return last
            em.op("vector", recips,
                  wait=[("T", tpv), ("SC", em.count("SC"))], inc="V")
            vr = em.count("V")

            def gelu_evac(s, qp=qp):
                last = None
                for h in range(2):
                    for qq in range(2):
                        qb = 2 * qp + qq
                        last = s.activation(
                            out=Gb[:, 512 * (qb % 4) + 256 * h:
                                   512 * (qb % 4) + 256 * (h + 1)],
                            in_=obank(2 * h + qq)[:, 0:256],
                            func=ACT.Gelu,
                            scale=P.stat[:, 40 + 2 * h + qq:41 + 2 * h + qq])
                return last
            em.op("scalar", gelu_evac, wait=[("V", vr)], inc="SC")
            sc_g = em.count("SC")
            if p == 0 and qp == 1 and \
                    P.dbg("G03", (128, 2048), BF16) is not None:
                em.tap_barrier(em.dma("sync", [(P.dbg_tensors["G03"], P.arnC[:, 0:2048])],
                       inc="ST", wait=[("SC", sc_g)]))

            for qq in range(2):
                qb = 2 * qp + qq

                def trans(te, qb=qb, qq=qq):
                    last = None
                    for c in range(4):
                        last = te.transpose(
                            P.pb[4 + qq][:].bitcast(BF16)[:, 128 * c:128 * (c + 1)],
                            Gb[:, 512 * (qb % 4) + 128 * c:
                               512 * (qb % 4) + 128 * (c + 1)],
                            P.ident[:])
                    return last
                em.op("tensor", trans, wait=[("SC", sc_g)], inc="T")
                tt = em.count("T")
                em.op("vector",
                      lambda v, qb=qb, qq=qq: v.tensor_copy(
                          GT[:].rearrange("q (c s) -> q c s", c=4)
                          [:, :, 128 * qb:128 * (qb + 1)],
                          P.pb[4 + qq][:].bitcast(BF16)[:, 0:512]
                          .rearrange("q (c s) -> q c s", c=4)),
                      wait=[("T", tt)], inc="V")
        vgt = em.count("V")

        if p == 0 and P.dbg("GT0", (128, 8192), BF16) is not None:
            em.tap_barrier(em.dma("sync", [(P.dbg_tensors["GT0"], GT[:, 0:8192])],
                   inc="ST", wait=[("V", vgt)]))

        for t in range(NT):
            w = [("V", max(vgt, vwf))] if t == 0 else []
            if t >= 2:
                w.append(("V", em.count("V")))

            def wfmm(te, t=t, p=p):
                last = None
                for c in range(4):
                    lhsT = GT[:, 2048 * c + 128 * t: 2048 * c + 128 * (t + 1)]
                    for half in range(2):
                        last = te.matmul(
                            P.pb[2 * (t % 2) + half][:, 0:512],
                            lhsT,
                            wf_bf[:, 1024 * c + 512 * half:
                                  1024 * c + 512 * (half + 1)],
                            start=(c == 0), stop=(c == 3 and p != 7))
                if p == 7:
                    for half in range(2):
                        last = te.matmul(
                            P.pb[2 * (t % 2) + half][:, 0:512],
                            P.ones_bf[:],
                            P.bf_row[:, 512 * half:512 * (half + 1)],
                            start=False, stop=True)
                return last
            em.op("tensor", wfmm, wait=w, inc="T")
            tw = em.count("T")

            def add_evac(v, t=t):
                v.tensor_add(P.xs[:, D * t: D * t + 512],
                             P.xs[:, D * t: D * t + 512],
                             P.pb[2 * (t % 2)][:, 0:512])
                return v.tensor_add(P.xs[:, D * t + 512: D * (t + 1)],
                                    P.xs[:, D * t + 512: D * (t + 1)],
                                    P.pb[2 * (t % 2) + 1][:, 0:512])
            em.op("vector", add_evac, wait=[("T", tw)], inc="V")
        if P.trunc == 3 and p == 0:
            raise TruncSchedule()
        if P.trunc == 4 and p == 1:
            raise TruncSchedule()
    return em.count("V")


def emit_store(P: Prog):
    em = P.em
    wv, wsc = em.count("V"), em.count("SC")
    st = em.dma("sync", [
        (P.d_out.rearrange("(t p) d -> p t d", p=128),
         P.xs[:].rearrange("p (t d) -> p t d", t=NT)),
    ], inc="ST", wait=[("V", wv), ("SC", wsc), ("T", em.count("T"))])

    em.progs["sync"].append(lambda e, st=st: e.wait_ge(em.sems["ST"], st))


def build_program(n_stages=99, debug_taps=()):
    import os
    P = Prog(debug_taps)
    P.trunc = int(os.environ.get("KTRUNC", "0"))
    emit_preamble(P)
    vxt, txt = emit_xT(P, wait_extra=[("LW", P.pre_LW), ("V", P.pre_V)])
    if P.dbg("XT", (128, NCH * S), BF16) is not None:
        P.em.dma("sync", [(P.dbg_tensors["XT"], P.xT[:])],
                 inc="ST", wait=[("V", vxt)])
    try:
        emit_l0_attn(P)
    except TruncSchedule:
        emit_store(P)
        P.em.replay()
        return P
    if P.dbg("Y", (128, NT * D), F32) is not None:
        P.em.tap_barrier(P.em.dma("sync", [(P.dbg_tensors["Y"], P.xs[:])],
                         inc="ST", wait=[("V", P.em.count("V"))]))
    kbar = os.environ.get("KBAR", "")
    if kbar:
        em = P.em
        em.op("vector", lambda v: v.engine_nop(),
              wait=[("SC", em.count("SC")), ("T", em.count("T"))], inc="V")
        vb = em.count("V")
        for e in kbar.split(","):
            if e and e != "vector":
                em.op(e, lambda x: None, wait=[("V", vb)])
    emit_ln(P, wait_v=P.em.count("V"))
    if n_stages >= 2 and "emit_l0_ffn" in globals():
        emit_l0_ffn(P)
    if n_stages >= 3 and "emit_l1_attn" in globals():
        emit_l1_attn(P)
    if n_stages >= 4 and "emit_l1_ffn" in globals():
        emit_l1_ffn(P)
    emit_store(P)
    P.em.replay()
    return P



_CACHE = {}


def _repack_params(inputs):
    r = {}
    def cols(v, pad_to=None):
        v = np.asarray(v, dtype=np.float32).reshape(-1)
        n = v.shape[0]
        ncol = -(-n // 128)
        if pad_to is not None:
            ncol = pad_to
        out = np.zeros((128, ncol), dtype=np.float32)
        full = v[: (n // 128) * 128].reshape(-1, 128).T
        out[:, : full.shape[1]] = full
        rem = n - (n // 128) * 128
        if rem:
            out[:rem, full.shape[1]] = v[(n // 128) * 128:]
        return out
    r["bkT"] = cols(inputs["L0_attn_bk"])
    b1 = np.asarray(inputs["L0_ffn_b1"], dtype=np.float32)
    r["b1aT"] = np.concatenate([cols(b1[l][:2730], 22) for l in range(2)], axis=1)
    r["b1gT"] = np.concatenate([cols(b1[l][2730:], 22) for l in range(2)], axis=1)
    r["lngT"] = np.concatenate(
        [cols(np.asarray(inputs["L0_ffn_ln_g"])[l]) for l in range(2)], axis=1)
    r["lnbT"] = np.concatenate(
        [cols(np.asarray(inputs["L0_ffn_ln_b"])[l]) for l in range(2)], axis=1)
    r["b2T"] = np.concatenate(
        [cols(np.asarray(inputs["L0_ffn_b2"])[l]) for l in range(2)], axis=1)
    return r


def _get_prog(n_stages=99, debug_taps=()):
    key = (n_stages, tuple(sorted(debug_taps)))
    if key not in _CACHE:
        _CACHE[key] = build_program(n_stages, debug_taps)
    return _CACHE[key]


def run_cores(inputs, n_stages=99, debug_taps=(), core_ids=None):
    P = _get_prog(n_stages, debug_taps)
    x = np.ascontiguousarray(np.asarray(inputs["L0_attn_x"], dtype=np.float32))
    reps = _repack_params(inputs)
    consts = P.consts
    base = {}
    for k, v in inputs.items():
        if k == "L0_attn_x":
            continue
        base[k] = np.ascontiguousarray(np.asarray(v, dtype=np.float32))
    in_maps = []
    cores = core_ids if core_ids is not None else list(range(N_CORES))
    for b in range(len(cores)):
        m = {"x_in": x[cores[b]]}
        m.update(base)
        for k, v in reps.items():
            m["r_" + k] = v
        for k, v in consts.items():
            m["c_" + k] = np.ascontiguousarray(v)
        in_maps.append(m)
    res = run_bass_kernel_spmd(P.nc, in_maps, core_ids=cores)
    return P, res


def kernel(**inputs):
    P, res = run_cores(inputs)
    out = np.stack([np.asarray(r["x_out"], dtype=np.float32)
                    for r in res.results], axis=0)
    return out


def emit_l0_ffn(P: Prog):
    em = P.em
    nc = P.nc

    sp = em.dma("sync", [(P.d_xspill.rearrange("(t p) d -> p t d", p=128),
                          P.xs[:].rearrange("p (t d) -> p t d", t=NT))],
                inc="SP", wait=[("V", em.count("V")), ("SC", em.count("SC"))])

    for l in range(2):
        emit_l0_ffn_layer(P, l, sp)

    xadd_v = []
    for t in range(NT):
        lw = em.dma("sync",
                    [(P.wstage[:, 1024 * (t % 2):1024 * (t % 2) + 1024],
                      P.d_xspill[128 * t:128 * (t + 1), :])],
                    inc="LW", wait=[("SP", sp),
                                    ("V", xadd_v[t - 2] if t >= 2
                                     else em.count("V"))])
        em.op("vector",
              lambda v, t=t: v.tensor_add(
                  P.xs[:, D * t:D * (t + 1)], P.xs[:, D * t:D * (t + 1)],
                  P.wstage[:, 1024 * (t % 2):1024 * (t % 2) + 1024]),
              wait=[("LW", lw)], inc="V")
        xadd_v.append(em.count("V"))
    emit_ln(P, wait_v=em.count("V"))


def emit_l0_ffn_layer(P: Prog, l: int, spill_tgt):
    em = P.em
    F = 64 * NT

    emit_xT(P, src_f32=P.xs, dst=P.xT, shift_cols=(4, 5, 6, 7),
            wait_extra=[("V", em.count("V")), ("SC", em.count("SC")),
                        ("T", em.count("T"))])
    vh = em.count("V")

    wq_dram = P.di["L0_ffn_wqkv"][l].rearrange("(ic q) o -> q ic o", q=128)
    for ic in range(NCH):
        lw = em.dma("sync", [(P.wstage[:, 1024 * (ic % 2):1024 * (ic % 2) + 1024],
                              wq_dram[:, ic, 0:1024])],
                    inc="LW", wait=[("V", em.count("V"))])
        em.op("vector",
              lambda v, ic=ic: v.tensor_copy(
                  P.wbf[:, 1024 * ic:1024 * (ic + 1)],
                  P.wstage[:, 1024 * (ic % 2):1024 * (ic % 2) + 1024]),
              wait=[("LW", lw), ("T", em.count("T"))], inc="V")
    vw1 = em.count("V")

    def qsks_row(v, l=l):
        v.drain()
        return v.tensor_tensor(P.scr[0:1, 1024:1088],
                               P.qs_row[:, 64 * l:64 * (l + 1)],
                               P.ks_row[:, 64 * l:64 * (l + 1)], op=ALU.mult)
    em.op("vector", qsks_row, wait=[("LW", em.count("LW"))], inc="V")
    vq = em.count("V")
    em.op("tensor",
          lambda te: te.matmul(P.pb[5][:, 0:64], P.ones_f32[:],
                               P.scr[0:1, 1024:1088].bitcast(F32),
                               start=True, stop=True),
          wait=[("V", vq)], inc="T")
    tq = em.count("T")
    em.op("scalar", lambda s: s.copy(P.scr[:, 1024:1088], P.pb[5][:, 0:64]),
          wait=[("T", tq)], inc="SC")
    scq = em.count("SC")

    em.dma("sync", [(P.wbf[:, 8192:9216], P.dc["rotq_cos"]),
                    (P.wbf[:, 9216:10240], P.dc["rotq_sin"])],
           inc="LW", wait=[("T", em.count("T")), ("V", em.count("V"))])
    for j, nm in enumerate(("rotk_cos", "rotk_sin")):
        lw = em.dma("sync", [(P.wstage[:, 0:1024], P.dc[nm])],
                    inc="LW", wait=[("V", em.count("V"))])
        em.op("vector",
              lambda v, j=j: v.tensor_tensor(
                  P.arnA[:, 1024 * j:1024 * (j + 1)]
                  .rearrange("p (t d) -> p t d", t=NT),
                  P.wstage[:, 0:1024].rearrange("p (t d) -> p t d", t=NT),
                  P.scr[:, 1024:1088][:, None, :].broadcast_to([128, NT, 64]),
                  op=ALU.mult),
              wait=[("LW", lw), ("SC", scq)], inc="V")
    vkt = em.count("V")

    sc_sqrt = {}
    for t in range(NT):
        bq, bk = P.pb[2 * (t % 2)], P.pb[2 * (t % 2) + 1]
        w = [("V", max(vh, vw1, vkt))] if t == 0 else []
        if t >= 2:
            w.append(("V", em.count("V")))

        def qkmm(te, t=t, bq=bq, bk=bk):
            last = None
            for ic in range(NCH):
                lhsT = P.xT[:, 2048 * ic + 128 * t:2048 * ic + 128 * (t + 1)]
                last = te.matmul(bq[:, 0:512], lhsT,
                                 P.wbf[:, 1024 * ic:1024 * ic + 512],
                                 start=(ic == 0), stop=(ic == NCH - 1))
                last = te.matmul(bk[:, 0:512], lhsT,
                                 P.wbf[:, 1024 * ic + 512:1024 * (ic + 1)],
                                 start=(ic == 0), stop=(ic == NCH - 1))
            return last
        em.op("tensor", qkmm, wait=w, inc="T")
        tqk = em.count("T")

        def sq(s, bq=bq, bk=bk):
            s.activation(out=P.scr[:, 0:512], in_=bq[:, 0:512], func=ACT.Square)
            return s.activation(out=P.scr[:, 512:1024], in_=bk[:, 0:512],
                                func=ACT.Square)
        em.op("scalar", sq, wait=[("T", tqk)], inc="SC")
        s0 = em.count("SC")

        def vchain1(v):
            v.tensor_reduce(P.stat[:, 0:8],
                            P.scr[:, 0:512].rearrange("p (h d) -> p h d", h=8),
                            axis=AX.X, op=ALU.add)
            return v.tensor_reduce(P.stat[:, 8:16],
                                   P.scr[:, 512:1024].rearrange("p (h d) -> p h d", h=8),
                                   axis=AX.X, op=ALU.add)
        em.op("vector", vchain1, wait=[("SC", s0)], inc="V")
        v1 = em.count("V")
        em.op("scalar",
              lambda s: s.activation(out=P.stat[:, 16:32], in_=P.stat[:, 0:16],
                                     func=ACT.Sqrt),
              wait=[("V", v1)], inc="SC")
        s1 = em.count("SC")

        def vchain2(v, t=t, bq=bq, bk=bk, l=l):
            v.tensor_scalar_max(P.stat[:, 16:32], P.stat[:, 16:32], 1e-12)
            v.drain()
            v.reciprocal(P.stat[:, 32:48], P.stat[:, 16:32])
            v.tensor_tensor(P.scr[:, 0:512].rearrange("p (h d) -> p h d", h=8),
                            bq[:, 0:512].rearrange("p (h d) -> p h d", h=8),
                            P.wbf[:, 8192 + 64 * t:8192 + 64 * (t + 1)][:, None, :]
                            .broadcast_to([128, 8, 64]), op=ALU.mult)
            v.tensor_tensor(P.scr[:, 512:1024].rearrange("p (h d) -> p h d", h=8),
                            bk[:, 0:512].rearrange("p (h d) -> p h d", h=8),
                            P.arnA[:, 64 * t:64 * (t + 1)][:, None, :]
                            .bitcast(BF16)[:, :, :].broadcast_to([128, 8, 64])
                            if False else
                            P.arnA[:, 64 * t:64 * (t + 1)][:, None, :]
                            .broadcast_to([128, 8, 64]), op=ALU.mult)
            swq = bq[:, 0:512].rearrange("p (h s d) -> p h s d", h=8, s=2)[:, :, ::-1, :]
            swk = bk[:, 0:512].rearrange("p (h s d) -> p h s d", h=8, s=2)[:, :, ::-1, :]
            v.tensor_tensor(P.wstage[:, 0:512].rearrange("p (h s d) -> p h s d", h=8, s=2),
                            swq,
                            P.wbf[:, 9216 + 64 * t:9216 + 64 * (t + 1)][:, None, :]
                            .broadcast_to([128, 8, 64])
                            .rearrange("p h (s d) -> p h s d", s=2), op=ALU.mult)
            v.tensor_tensor(P.wstage[:, 512:1024].rearrange("p (h s d) -> p h s d", h=8, s=2),
                            swk,
                            P.arnA[:, 1024 + 64 * t:1024 + 64 * (t + 1)][:, None, :]
                            .broadcast_to([128, 8, 64])
                            .rearrange("p h (s d) -> p h s d", s=2), op=ALU.mult)
            v.drain()
            v.tensor_add(P.scr[:, 0:512], P.scr[:, 0:512], P.wstage[:, 0:512])
            v.tensor_add(P.scr[:, 512:1024], P.scr[:, 512:1024],
                         P.wstage[:, 512:1024])
            v.drain()
            v.tensor_tensor(P.arnC[:, 1024:1536].rearrange("p (h d) -> p h d", h=8),
                            P.scr[:, 0:512].rearrange("p (h d) -> p h d", h=8),
                            P.stat[:, 32:40][:, :, None].broadcast_to([128, 8, 64]),
                            op=ALU.mult)
            v.tensor_tensor(P.arnC[:, 1536:2048].rearrange("p (h d) -> p h d", h=8),
                            P.scr[:, 512:1024].rearrange("p (h d) -> p h d", h=8),
                            P.stat[:, 40:48][:, :, None].broadcast_to([128, 8, 64]),
                            op=ALU.mult)
            v.drain()
            v.tensor_tensor(P.scr[:, 0:512], P.arnC[:, 1024:1536],
                            P.arnC[:, 1536:2048], op=ALU.mult)
            v.drain()
            v.tensor_reduce(P.stat[:, 16:24],
                            P.scr[:, 0:512].rearrange("p (h d) -> p h d", h=8),
                            axis=AX.X, op=ALU.add)
            v.drain()
            if t == 0:
                v.tensor_copy(P.stat[:, 48:56], P.stat[:, 16:24])
                return v.tensor_copy(P.stat[:, 56:64], P.stat[:, 16:24])
            v.tensor_tensor(P.stat[:, 48:56], P.stat[:, 48:56],
                            P.stat[:, 16:24], op=ALU.min)
            return v.tensor_tensor(P.stat[:, 56:64], P.stat[:, 56:64],
                                   P.stat[:, 16:24], op=ALU.max)
        em.op("vector", vchain2, wait=[("SC", s1)], inc="V")
        v2 = em.count("V")

        def trans(te, t=t):
            last = None
            for c in range(4):
                last = te.transpose(
                    P.pt[0][:, 128 * c:128 * (c + 1)],
                    P.arnC[:, 1024 + 128 * c:1024 + 128 * (c + 1)], P.ident[:])
            for c in range(4):
                last = te.transpose(
                    P.pt[1][:, 128 * c:128 * (c + 1)],
                    P.arnC[:, 1536 + 128 * c:1536 + 128 * (c + 1)], P.ident[:])
            return last
        em.op("tensor", trans, wait=[("V", v2)], inc="T")
        tt = em.count("T")

        def evac(v, t=t):
            v.tensor_copy(
                P.arnD[:, 0:8192].rearrange("p (c s) -> p c s", c=4)
                [:, :, 128 * t:128 * (t + 1)],
                P.pt[0][:, 0:512].rearrange("p (c s) -> p c s", c=4))
            return v.tensor_copy(
                P.arnB[:, 0:8192].rearrange("p (c s) -> p c s", c=4)
                [:, :, 128 * t:128 * (t + 1)],
                P.pt[1][:, 0:512].rearrange("p (c s) -> p c s", c=4))
        em.op("vector", evac, wait=[("T", tt)], inc="V")
    v_qk = em.count("V")

    def pack_mnmx(v):
        v.drain()
        return v.tensor_scalar_mul(P.stat[:, 56:64], P.stat[:, 56:64], -1.0)
    em.op("vector", pack_mnmx, inc="V")
    vpm = em.count("V")
    em.op("tensor",
          lambda te: te.transpose(P.pt[0][:].bitcast(F32)[0:16, 0:128],
                                  P.stat[:, 48:64], P.ident_f32[:]),
          wait=[("V", vpm)], inc="T")
    ttr = em.count("T")
    def c_math_v(v):
        return v.tensor_reduce(P.scr[0:16, 0:1],
                               P.pt[0][:].bitcast(F32)[0:16, 0:128],
                               axis=AX.X, op=ALU.min)
    em.op("vector", c_math_v, wait=[("T", ttr)], inc="V")
    vcm = em.count("V")
    em.op("tensor",
          lambda te: te.transpose(P.pb[5][0:1, 0:16], P.scr[0:16, 0:1],
                                  P.ident_f32[0:16, 0:16]),
          wait=[("V", vcm)], inc="T")
    ttr2 = em.count("T")

    def c_row(v, l=l):
        v.tensor_copy(P.scr[0:1, 1024:1040], P.pb[5][0:1, 0:16])
        v.drain()
        v.tensor_tensor(P.scr[0:1, 1040:1048], P.scr[0:1, 1024:1032],
                        P.scr[0:1, 1032:1040], op=ALU.subtract)
        v.drain()
        v.tensor_scalar_mul(P.scr[0:1, 1040:1048], P.scr[0:1, 1040:1048], 0.5)
        v.reduce_max(P.scr[0:1, 1048:1049], P.qs_row[:, 64 * l:64 * (l + 1)],
                     axis=AX.X, apply_absolute_value=True)
        v.reduce_max(P.scr[0:1, 1049:1050], P.ks_row[:, 64 * l:64 * (l + 1)],
                     axis=AX.X, apply_absolute_value=True)
        v.drain()
        v.tensor_tensor(P.scr[0:1, 1048:1049], P.scr[0:1, 1048:1049],
                        P.scr[0:1, 1049:1050], op=ALU.mult)
        v.drain()
        v.tensor_scalar(P.scr[0:1, 1048:1049], P.scr[0:1, 1048:1049],
                        8.0, -70.0, op0=ALU.mult, op1=ALU.add)
        v.drain()
        v.tensor_tensor(P.scr[0:1, 1040:1048], P.scr[0:1, 1040:1048],
                        P.scr[0:1, 1048:1049].broadcast_to([1, 8]), op=ALU.max)
        v.drain()
        return v.tensor_scalar_mul(P.scr[0:1, 1040:1048],
                                   P.scr[0:1, 1040:1048], -1.0)
    em.op("vector", c_row, wait=[("T", ttr2)], inc="V")
    vcr = em.count("V")
    em.op("tensor",
          lambda te: te.matmul(P.pb[5][:, 0:8], P.ones_f32[:],
                               P.scr[0:1, 1040:1048], start=True, stop=True),
          wait=[("V", vcr)], inc="T")
    tcb = em.count("T")
    em.op("scalar", lambda s: s.copy(P.scr[:, 1024:1032], P.pb[5][:, 0:8]),
          wait=[("T", tcb)], inc="SC")
    sc_cb = em.count("SC")
    if P.dbg(f"CB_{l}", (128, 8), F32) is not None:
        em.tap_barrier(em.dma("sync",
                       [(P.dbg_tensors[f"CB_{l}"], P.scr[:, 1024:1032])],
                       inc="ST", wait=[("SC", sc_cb)]))
    if P.dbg(f"DIAG_{l}", (128, 16), F32) is not None:
        em.tap_barrier(em.dma("sync",
                       [(P.dbg_tensors[f"DIAG_{l}"], P.stat[:, 48:64])],
                       inc="ST", wait=[("V", vpm)]))
    return emit_l0_ffn_layer_part2(P, l, spill_tgt, v_qk, sc_cb)


def emit_l0_ffn_layer_part2(P: Prog, l: int, spill_tgt, v_qk, sc_cb):
    em = P.em

    wq_dram = P.di["L0_ffn_wqkv"][l].rearrange("(ic q) o -> q ic o", q=128)
    for hhalf in range(2):
        lw = em.dma("sync", [(P.wstage[:, 2048 * hhalf:2048 * hhalf + 2048]
                              .rearrange("q (ic o) -> q ic o", ic=4),
                              wq_dram[:, 4 * hhalf:4 * hhalf + 4, 1024:1536])],
                    inc="LW", wait=[("V", em.count("V"))])
        em.op("vector",
              lambda v, hh=hhalf: v.tensor_copy(
                  P.wbf[:, 2048 * hh:2048 * (hh + 1)],
                  P.wstage[:, 2048 * hh:2048 * hh + 2048]),
              wait=[("LW", lw), ("T", em.count("T"))], inc="V")
    vwv = em.count("V")

    def vslot(t):
        if t < 15:
            return P.wbf, 4096 + 520 * t
        return P.arnC, 0

    def ones_v(v):
        last = None
        for t in range(NT):
            buf, off = vslot(t)
            last = v.memset(buf[:, off:off + 520]
                            .rearrange("p (h c) -> p h c", h=8)[:, :, 64:65], 1.0)
        return last
    em.op("vector", ones_v, wait=[("V", vwv)], inc="V")

    for t in range(NT):
        bv = P.pb[t % 2]
        w = [("V", em.count("V"))]

        def vmm(te, t=t, bv=bv):
            last = None
            for ic in range(NCH):
                lhsT = P.xT[:, 2048 * ic + 128 * t:2048 * ic + 128 * (t + 1)]
                last = te.matmul(bv[:, 0:512], lhsT,
                                 P.wbf[:, 512 * ic:512 * (ic + 1)],
                                 start=(ic == 0), stop=(ic == NCH - 1))
            return last
        em.op("tensor", vmm, wait=w, inc="T")
        tv = em.count("T")
        buf, off = vslot(t)
        em.op("scalar",
              lambda s, t=t, bv=bv, buf=buf, off=off: s.copy(
                  buf[:, off:off + 520].rearrange("p (h c) -> p h c", h=8)[:, :, 0:64],
                  bv[:, 0:512].rearrange("p (h c) -> p h c", h=8)),
              wait=[("T", tv)], inc="SC")
    sc_v = em.count("SC")

    oT = P.wstage[:].bitcast(BF16)
    sc_prev_ev = sc_v
    for qg in range(8):
        kbs = [kb for kb in range(2 * qg - 4, 2 * qg + 2) if kb >= 0]
        for pr in range(4):
            sc_exp = {}
            pv_done = {}
            for ik, kb in enumerate(kbs):
                slot = ik % 2
                rel = kb - 2 * qg
                w = []
                if ik >= 2:
                    w.append(("SC", sc_exp[ik - 2]))
                else:
                    w.append(("SC", em.count("SC")))
                    if qg == 0 and pr == 0:
                        w.append(("V", em.count("V")))

                def score(te, qg=qg, pr=pr, kb=kb, slot=slot, rel=rel):
                    last = None
                    for h in range(2):
                        bank = P.pb[2 * slot + h]
                        lhsT = P.arnB[64 * h:64 * (h + 1), 0:8192] \
                            .rearrange("p (c s) -> p c s", c=4)[:, pr, 128 * kb:128 * (kb + 1)]
                        rhs = P.arnD[64 * h:64 * (h + 1), 0:8192] \
                            .rearrange("p (c s) -> p c s", c=4)[:, pr, 256 * qg:256 * (qg + 1)]
                        last = te.matmul(bank[:, 0:256], lhsT, rhs,
                                         start=True,
                                         stop=(rel in (-3, -2, -1)))
                    if rel not in (-3, -2, -1):
                        for h in range(2):
                            bank = P.pb[2 * slot + h]
                            last = te.matmul(bank[:, 0:256], P.ident[:],
                                             P.mask[f"mask_l0f_{rel}"][:],
                                             start=False, stop=True)
                    return last
                em.op("tensor", score, wait=w, inc="T")
                ts = em.count("T")

                def do_exp(s, pr=pr, slot=slot):
                    last = None
                    for h in range(2):
                        last = s.activation(
                            out=P.arnA[:, 2048 + 256 * (2 * h + slot):
                                       2048 + 256 * (2 * h + slot + 1)],
                            in_=P.pb[2 * slot + h][:, 0:256], func=ACT.Exp,
                            bias=P.scr[:, 1024 + 2 * pr + h:1025 + 2 * pr + h])
                    return last
                we = [("T", ts), ("SC", sc_cb)]
                if ik >= 2:
                    we.append(("T", pv_done.get(slot, 0)))
                em.op("scalar", do_exp, wait=we, inc="SC")
                sc_exp[ik] = em.count("SC")

                def do_pv(te, qg=qg, pr=pr, kb=kb, slot=slot, ik=ik,
                          nkb=len(kbs)):
                    last = None
                    for h in range(2):
                        ptile = P.arnA[:, 2048 + 256 * (2 * h + slot):
                                       2048 + 256 * (2 * h + slot + 1)]
                        buf, off = vslot(kb)
                        for qq in range(2):
                            last = te.matmul(
                                P.pb[4][:, 65 * (2 * h + qq):65 * (2 * h + qq) + 65],
                                ptile[:, 128 * qq:128 * (qq + 1)],
                                buf[:, off + 65 * (2 * pr + h):
                                    off + 65 * (2 * pr + h) + 65],
                                start=(ik == 0 and h == 0 and qq == 0),
                                stop=(ik == nkb - 1 and h == 1 and qq == 1))
                    return last
                em.op("tensor", do_pv, wait=[("SC", sc_exp[ik])], inc="T")
                pv_done[slot] = em.count("T")
            tpv = em.count("T")

            def recips(v, pr=pr):
                last = None
                for k in range(4):
                    last = v.reciprocal(P.stat[:, 32 + k:33 + k],
                                        P.pb[4][:, 65 * k + 64:65 * k + 65])
                return last
            em.op("vector", recips,
                  wait=[("T", tpv), ("SC", em.count("SC"))], inc="V")
            vr = em.count("V")

            def o_evac(s, qg=qg, pr=pr):
                last = None
                for h in range(2):
                    for qq in range(2):
                        last = s.activation(
                            out=P.arnC[:, 1024 + 512 * qq + 64 * (2 * pr + h):
                                       1024 + 512 * qq + 64 * (2 * pr + h) + 64],
                            in_=P.pb[4][:, 65 * (2 * h + qq):65 * (2 * h + qq) + 64],
                            func=ACT.Identity,
                            scale=P.stat[:, 32 + 2 * h + qq:33 + 2 * h + qq])
                return last
            em.op("scalar", o_evac, wait=[("V", vr)], inc="SC")
        sc_o = em.count("SC")

        for qq in range(2):
            qb = 2 * qg + qq

            def trans(te, qq=qq):
                last = None
                for c in range(4):
                    last = te.transpose(
                        P.pt[qq][:, 128 * c:128 * (c + 1)],
                        P.arnC[:, 1024 + 512 * qq + 128 * c:
                               1024 + 512 * qq + 128 * (c + 1)],
                        P.ident[:])
                return last
            em.op("tensor", trans, wait=[("SC", sc_o)], inc="T")
            tt = em.count("T")
            em.op("vector",
                  lambda v, qb=qb, qq=qq: v.tensor_copy(
                      oT[:, 512 * qb:512 * (qb + 1)],
                      P.pt[qq][:, 0:512]),
                  wait=[("T", tt)], inc="V")
    v_attn = em.count("V")
    if P.dbg(f"OT_{l}", (128, 8192), BF16) is not None:
        em.tap_barrier(em.dma("sync",
                       [(P.dbg_tensors[f"OT_{l}"], P.wstage[:].bitcast(BF16))],
                       inc="ST", wait=[("V", v_attn)]))
    if P.dbg(f"QT_{l}", (128, 8192), BF16) is not None:
        em.tap_barrier(em.dma("sync",
                       [(P.dbg_tensors[f"QT_{l}"], P.arnD[:, 0:8192])],
                       inc="ST", wait=[("V", v_attn)]))
    if P.dbg(f"KT_{l}", (128, 8192), BF16) is not None:
        em.tap_barrier(em.dma("sync",
                       [(P.dbg_tensors[f"KT_{l}"], P.arnB[:, 0:8192])],
                       inc="ST", wait=[("V", v_attn)]))
    return emit_l0_ffn_layer_part3(P, l, spill_tgt, v_attn)


def emit_l0_ffn_layer_part3(P: Prog, l: int, spill_tgt, v_attn):
    em = P.em
    oT = P.wstage[:].bitcast(BF16)

    wo_dram = P.di["L0_ffn_wo"][l].rearrange("(ic q) o -> q ic o", q=128)
    for hh in range(2):
        lw = em.dma("sync", [(P.arnB[:, 0:4096].bitcast(F32)
                              .rearrange("q (ic o) -> q ic o", ic=2),
                              wo_dram[:, 2 * hh:2 * hh + 2, :])],
                    inc="LW", wait=[("T", em.count("T")), ("V", em.count("V"))])
        em.op("vector",
              lambda v, hh=hh: v.tensor_copy(
                  P.wbf[:, 2048 * hh:2048 * (hh + 1)],
                  P.arnB[:, 0:4096].bitcast(F32)),
              wait=[("LW", lw), ("T", em.count("T"))], inc="V")
    v_wo = em.count("V")

    for t in range(NT):
        b0, b1 = P.pb[2 * (t % 2)], P.pb[2 * (t % 2) + 1]
        w = [("V", v_wo)] if t == 0 else []
        if t >= 2:
            w.append(("V", em.count("V")))

        def womm(te, t=t, b0=b0, b1=b1):
            last = None
            for c in range(4):
                lhsT = oT[:, 512 * t + 128 * c:512 * t + 128 * (c + 1)]
                last = te.matmul(b0[:, 0:512], lhsT,
                                 P.wbf[:, 1024 * c:1024 * c + 512],
                                 start=(c == 0), stop=(c == 3))
                last = te.matmul(b1[:, 0:512], lhsT,
                                 P.wbf[:, 1024 * c + 512:1024 * (c + 1)],
                                 start=(c == 0), stop=(c == 3))
            return last
        em.op("tensor", womm, wait=w, inc="T")
        tw = em.count("T")
        wadd = [("T", tw)]
        if l == 0 and t == 0:
            wadd.append(("SP", spill_tgt))

        def add_evac(v, t=t, b0=b0, b1=b1):
            v.tensor_add(P.xs[:, D * t:D * t + 512],
                         P.xs[:, D * t:D * t + 512], b0[:, 0:512])
            return v.tensor_add(P.xs[:, D * t + 512:D * (t + 1)],
                                P.xs[:, D * t + 512:D * (t + 1)], b1[:, 0:512])
        em.op("vector", add_evac, wait=wadd, inc="V")
    v_z2 = em.count("V")

    if P.dbg(f"Z2_{l}", (128, NT * D), F32) is not None:
        em.tap_barrier(em.dma("sync", [(P.dbg_tensors[f"Z2_{l}"], P.xs[:])],
                       inc="ST", wait=[("V", v_z2)]))
    emit_xT(P, src_f32=P.xs, dst=P.xT, shift_cols=(4, 5, 6, 7),
            wait_extra=[("V", v_z2), ("SC", em.count("SC")),
                        ("T", em.count("T"))])
    v_h2 = em.count("V")

    def stats_t(v, t):
        def f(v_, t=t):
            v_.reduce_sum(P.stat[:, t:t + 1], P.xs[:, D * t:D * t + 512],
                          axis=AX.X)
            return v_.reduce_sum(P.scr[:, 1 + t:2 + t],
                                 P.xs[:, D * t + 512:D * (t + 1)], axis=AX.X)
        return f
    for t in range(NT):
        w = [("V", v_h2)] if t == 0 else []
        em.op("vector", stats_t(None, t), wait=w, inc="V")
        em.op("scalar",
              lambda s, t=t: s.activation(
                  out=P.scr[:, 64:576], in_=P.xs[:, D * t:D * t + 512],
                  func=ACT.Square, accum_out=P.stat[:, 32 + t:33 + t]),
              wait=w, inc="SC")
        em.op("scalar",
              lambda s, t=t: s.activation(
                  out=P.scr[:, 64:576], in_=P.xs[:, D * t + 512:D * (t + 1)],
                  func=ACT.Square, accum_out=P.scr[:, 20 + t:21 + t]),
              inc="SC")
    v_st = em.count("V")
    sc_st = em.count("SC")

    def zcols(v):
        v.memset(P.scr[:, 0:1], 0.0)
        return v.memset(P.scr[:, 19:20], 0.0)
    em.op("vector", zcols, wait=[("SC", sc_st)], inc="V")
    vz = em.count("V")

    def shiftmm(te):
        te.matmul(P.pb[4][:, 0:16], P.shiftmat[:], P.scr[:, 1:17],
                  start=True, stop=False)
        te.matmul(P.pb[4][:, 0:16], P.sel127[:], P.scr[:, 0:16],
                  start=False, stop=True)
        te.matmul(P.pb[4][:, 16:32], P.shiftmat[:], P.scr[:, 20:36],
                  start=True, stop=False)
        return te.matmul(P.pb[4][:, 16:32], P.sel127[:], P.scr[:, 19:35],
                         start=False, stop=True)
    em.op("tensor", shiftmm, wait=[("V", vz)], inc="T")
    t_sh = em.count("T")

    def ln2_math(v):
        v.tensor_add(P.stat[:, 0:16], P.stat[:, 0:16], P.pb[4][:, 0:16])
        v.tensor_add(P.stat[:, 32:48], P.stat[:, 32:48], P.pb[4][:, 16:32])
        v.drain()
        v.tensor_scalar_mul(P.stat[:, 16:32], P.stat[:, 0:16], 1.0 / D)
        v.tensor_scalar_mul(P.stat[:, 48:64], P.stat[:, 32:48], 1.0 / D)
        v.drain()
        v.tensor_tensor(P.scr[:, 0:16], P.stat[:, 16:32], P.stat[:, 16:32],
                        op=ALU.mult)
        v.drain()
        v.tensor_sub(P.stat[:, 48:64], P.stat[:, 48:64], P.scr[:, 0:16])
        v.drain()
        return v.tensor_scalar_add(P.stat[:, 48:64], P.stat[:, 48:64], 1e-5)
    em.op("vector", ln2_math, wait=[("T", t_sh)], inc="V")
    v_lm = em.count("V")
    em.op("scalar",
          lambda s: s.activation(out=P.scr[:, 0:16], in_=P.stat[:, 48:64],
                                 func=ACT.Sqrt),
          wait=[("V", v_lm)], inc="SC")
    sc_sq = em.count("SC")

    def rstd_nb(v):
        v.reciprocal(P.stat[:, 48:64], P.scr[:, 0:16])
        v.drain()
        v.tensor_tensor(P.stat[:, 16:32], P.stat[:, 16:32], P.stat[:, 48:64],
                        op=ALU.mult)
        v.drain()
        return v.tensor_scalar_mul(P.stat[:, 16:32], P.stat[:, 16:32], -1.0)
    em.op("vector", rstd_nb, wait=[("SC", sc_sq)], inc="V")
    v_rn = em.count("V")

    def packrn(v):
        v.tensor_copy(P.scr[:, 0:16], P.stat[:, 48:64])
        return v.tensor_copy(P.scr[:, 16:32], P.stat[:, 16:32])
    em.op("vector", packrn, wait=[("V", v_rn)], inc="V")
    v_pk = em.count("V")
    em.op("tensor",
          lambda te: te.transpose(P.pb[4][:].bitcast(F32)[0:32, 0:128],
                                  P.scr[:, 0:32], P.ident_f32[:]),
          wait=[("V", v_pk)], inc="T")
    t_tr = em.count("T")
    em.op("scalar", lambda s: s.copy(P.scr[0:32, 64:192], P.pb[4][:].bitcast(F32)[0:32, 0:128]),
          wait=[("T", t_tr)], inc="SC")
    sc_tr = em.count("SC")
    rowbuf = P.arnB[:, 4096:8192].bitcast(F32)
    nbbuf = P.arnD[:, 0:4096].bitcast(F32)
    lwg = em.dma("gpsimd",
                 [(rowbuf[0:1, 0:2048].rearrange("o (t p) -> o t p", t=NT),
                   P.scr[0:16, 64:192][:, None, :]),
                  (nbbuf[0:1, 0:2048].rearrange("o (t p) -> o t p", t=NT),
                   P.scr[16:32, 64:192][:, None, :])],
                 inc="LG", wait=[("SC", sc_tr)])
    return emit_l0_ffn_layer_part4(P, l, lwg)


def emit_l0_ffn_layer_part4(P: Prog, l: int, lwg):
    em = P.em
    rowbuf = P.arnB[:, 4096:8192].bitcast(F32)
    nbbuf = P.arnD[:, 0:4096].bitcast(F32)

    for c in range(NCH):
        for sl in range(4):
            def bc(te, sl=sl):
                te.matmul(P.pb[4][:, 0:512], P.ones_f32[:],
                          rowbuf[0:1, 512 * sl:512 * (sl + 1)],
                          start=True, stop=True)
                return te.matmul(P.pb[5][:, 0:512], P.ones_f32[:],
                                 nbbuf[0:1, 512 * sl:512 * (sl + 1)],
                                 start=True, stop=True)
            w = [("LG", lwg)] if (c == 0 and sl == 0) else [("V", em.count("V"))]
            em.op("tensor", bc, wait=w, inc="T")
            tb = em.count("T")

            def norm(v, c=c, sl=sl, l=l):
                seg = P.xT[:, 2048 * c + 512 * sl:2048 * c + 512 * (sl + 1)]
                v.tensor_tensor(P.wstage[:, 0:512], seg, P.pb[4][:, 0:512],
                                op=ALU.mult)
                v.drain()
                v.tensor_add(P.wstage[:, 0:512], P.wstage[:, 0:512],
                             P.pb[5][:, 0:512])
                v.drain()
                return v.tensor_scalar(
                    out=seg, in0=P.wstage[:, 0:512],
                    scalar1=P.lngT[:, 8 * l + c:8 * l + c + 1],
                    scalar2=P.lnbT[:, 8 * l + c:8 * l + c + 1],
                    op0=ALU.mult, op1=ALU.add)
            em.op("vector", norm, wait=[("T", tb)], inc="V")
    v_h2n = em.count("V")
    if P.dbg(f"H2N_{l}", (128, NCH * S), BF16) is not None:
        em.tap_barrier(em.dma("sync", [(P.dbg_tensors[f"H2N_{l}"], P.xT[:])],
                       inc="ST", wait=[("V", v_h2n)]))

    w1_dram = P.di["L0_ffn_w1"][l].rearrange("(ic q) o -> q ic o", q=128)
    w2_dram = P.di["L0_ffn_w2"][l]

    def ft_ap(c):
        if c < 8:
            return P.arnD[:, 1024 * c:1024 * (c + 1)]
        if c < 16:
            return P.wbf[:, 4096 + 1024 * (c - 8):4096 + 1024 * (c - 7)]
        return P.arnB[:, 1024 * (c - 16):1024 * (c - 15)]

    NW = 22
    for g in range(2):
        tok0 = 1024 * g
        for uw in range(NW):
            gwid = 128 if uw < 21 else 42
            awid = gwid
            lw = em.dma("sync", [
                (P.wstage[:, 0:8 * awid].rearrange("q (ic o) -> q ic o", ic=8),
                 w1_dram[:, :, 128 * uw:128 * uw + awid]),
                (P.wstage[:, 1024:1024 + 8 * gwid]
                 .rearrange("q (ic o) -> q ic o", ic=8),
                 w1_dram[:, :, 2730 + 128 * uw:2730 + 128 * uw + gwid]),
            ], inc="LW", wait=[("V", em.count("V"))])
            em.op("vector",
                  lambda v, gwid=gwid, awid=awid, uw=uw: (
                      v.tensor_copy(P.wbf[:, 0:8 * awid] if uw % 2 == 0 else
                                    P.wbf[:, 2048:2048 + 8 * awid],
                                    P.wstage[:, 0:8 * awid]),
                      v.tensor_copy(
                          (P.wbf[:, 1024:1024 + 8 * gwid] if uw % 2 == 0 else
                           P.wbf[:, 3072:3072 + 8 * gwid]),
                          P.wstage[:, 1024:1024 + 8 * gwid]))[-1],
                  wait=[("LW", lw), ("T", em.count("T"))], inc="V")
            vcast = em.count("V")
            woff = 0 if uw % 2 == 0 else 2048
            pa = (P.pb[0], P.pb[1]) if uw % 2 == 0 else (P.pb[2], P.pb[3])
            pg = (P.pb[4], P.pb[5]) if uw % 2 == 0 else \
                (P.pt[0][:].bitcast(F32), P.pt[1][:].bitcast(F32))

            def w1mm(te, uw=uw, g=g, woff=woff, pa=pa, pg=pg, gwid=gwid,
                     awid=awid, tok0=tok0):
                last = None
                for ic in range(NCH):
                    rhs = P.xT[:, 2048 * ic + tok0:2048 * ic + tok0 + 1024]
                    last = te.matmul(pa[0][0:awid, 0:512],
                                     P.wbf[:, woff + awid * ic:woff + awid * (ic + 1)],
                                     rhs[:, 0:512], start=(ic == 0),
                                     stop=(ic == NCH - 1))
                    last = te.matmul(pa[1][0:awid, 0:512],
                                     P.wbf[:, woff + awid * ic:woff + awid * (ic + 1)],
                                     rhs[:, 512:1024],
                                     start=(ic == 0), stop=(ic == NCH - 1))
                    last = te.matmul(pg[0][0:gwid, 0:512],
                                     P.wbf[:, woff + 1024 + gwid * ic:
                                           woff + 1024 + gwid * (ic + 1)],
                                     rhs[:, 0:512], start=(ic == 0),
                                     stop=(ic == NCH - 1))
                    last = te.matmul(pg[1][0:gwid, 0:512],
                                     P.wbf[:, woff + 1024 + gwid * ic:
                                           woff + 1024 + gwid * (ic + 1)],
                                     rhs[:, 512:1024], start=(ic == 0),
                                     stop=(ic == NCH - 1))
                return last
            wmm = [("V", max(vcast, v_h2n))]
            if uw >= 2:
                wmm.append(("V", em.count("V")))
            em.op("tensor", w1mm, wait=wmm, inc="T")
            t1 = em.count("T")

            def gelu_g(s, uw=uw, l=l, pg=pg, gwid=gwid):
                s.activation(out=P.scr[0:gwid, 0:512],
                             in_=pg[0][0:gwid, 0:512], func=ACT.Gelu,
                             bias=P.b1gT[0:gwid, 22 * l + uw:22 * l + uw + 1])
                return s.activation(out=P.scr[0:gwid, 512:1024],
                                    in_=pg[1][0:gwid, 0:512], func=ACT.Gelu,
                                    bias=P.b1gT[0:gwid, 22 * l + uw:22 * l + uw + 1])
            em.op("scalar", gelu_g, wait=[("T", t1)], inc="SC")
            sg = em.count("SC")

            def fmul(v, uw=uw, l=l, pa=pa, g=g, gwid=gwid):
                dst = ft_ap(uw)
                v.scalar_tensor_tensor(
                    out=dst[0:gwid, 0:512], in0=pa[0][0:gwid, 0:512],
                    scalar=P.b1aT[0:gwid, 22 * l + uw:22 * l + uw + 1],
                    in1=P.scr[0:gwid, 0:512], op0=ALU.add, op1=ALU.mult)
                return v.scalar_tensor_tensor(
                    out=dst[0:gwid, 512:1024], in0=pa[1][0:gwid, 0:512],
                    scalar=P.b1aT[0:gwid, 22 * l + uw:22 * l + uw + 1],
                    in1=P.scr[0:gwid, 512:1024], op0=ALU.add, op1=ALU.mult)
            em.op("vector", fmul, wait=[("SC", sg)], inc="V")
        v_f = em.count("V")

        w2_cv = []
        w2_tt = []
        for op_ in range(4):
            banks = (P.pb[0], P.pb[1], P.pb[2], P.pb[3])
            for c in range(NW):
                cw = 128 if c < 21 else 42
                slot = 256 * (c % 2)
                k2 = len(w2_cv)
                lw = em.dma("sync", [
                    (P.wstage[0:cw, slot:slot + 256],
                     w2_dram[128 * c:128 * c + cw, 256 * op_:256 * (op_ + 1)]),
                ], inc="LW", wait=[("V", w2_cv[k2 - 2] if k2 >= 2
                                   else em.count("V"))])
                cpw = [("LW", lw)]
                cpw.append(("T", w2_tt[k2 - 2]) if k2 >= 2
                           else ("T", em.count("T")))
                em.op("vector",
                      lambda v, cw=cw, slot=slot: v.tensor_copy(
                          P.arnA[0:cw, slot:slot + 256],
                          P.wstage[0:cw, slot:slot + 256]),
                      wait=cpw, inc="V")
                vw2 = em.count("V")
                w2_cv.append(vw2)

                def w2mm(te, c=c, cw=cw, slot=slot, banks=banks):
                    fa = ft_ap(c)
                    last = None
                    for oe in range(2):
                        lhsT = P.arnA[0:cw, slot + 128 * oe:slot + 128 * (oe + 1)]
                        last = te.matmul(banks[2 * oe][:, 0:512], lhsT,
                                         fa[0:cw, 0:512],
                                         start=(c == 0), stop=(c == NW - 1))
                        last = te.matmul(banks[2 * oe + 1][:, 0:512], lhsT,
                                         fa[0:cw, 512:1024],
                                         start=(c == 0), stop=(c == NW - 1))
                    return last
                wm = [("V", max(vw2, v_f))]
                if c == 0 and op_ > 0:
                    wm.append(("V", em.count("V")))
                em.op("tensor", w2mm, wait=wm, inc="T")
                w2_tt.append(em.count("T"))
            t2 = em.count("T")

            def yevac(s, op_=op_, l=l):
                last = None
                for k in range(4):
                    oe = k // 2
                    last = s.activation(
                        out=P.arnC[:, 512 * k:512 * (k + 1)],
                        in_=P.pb[k][:, 0:512], func=ACT.Identity,
                        bias=P.b2T[:, 8 * l + 2 * op_ + oe:
                                   8 * l + 2 * op_ + oe + 1])
                return last
            em.op("scalar", yevac, wait=[("T", t2)], inc="SC")
            s2 = em.count("SC")
            for oe in range(2):
                for tl in range(8):
                    def ytr(te, oe=oe, tl=tl):
                        return te.transpose(
                            P.pt[tl % 2][:, 0:128],
                            P.arnC[:, 1024 * oe + 128 * tl:
                                   1024 * oe + 128 * (tl + 1)],
                            P.ident[:])
                    em.op("tensor", ytr,
                          wait=[("SC", s2)] if (oe == 0 and tl == 0)
                          else [("V", em.count("V"))], inc="T")
                    ty = em.count("T")

                    def yadd(v, oe=oe, tl=tl, op_=op_, g=g):
                        tglob = 8 * g + tl
                        col = D * tglob + 128 * (2 * op_ + oe)
                        return v.tensor_add(P.xs[:, col:col + 128],
                                            P.xs[:, col:col + 128],
                                            P.pt[tl % 2][:, 0:128])
                    em.op("vector", yadd, wait=[("T", ty)], inc="V")
    if P.dbg(f"Z3_{l}", (128, NT * D), F32) is not None:
        em.tap_barrier(em.dma("sync", [(P.dbg_tensors[f"Z3_{l}"], P.xs[:])],
                       inc="ST", wait=[("V", em.count("V"))]))
    return em.count("V")


def emit_wT_col(P: Prog, w_dram, oc, dst_ap, tag):
    em = P.em
    lw = em.dma("sync", [(P.wstage[:, 0:1024],
                          w_dram[128 * oc:128 * (oc + 1), :])],
                inc="LW", wait=[("V", em.count("V")), ("SC", em.count("SC"))])
    em.op("vector",
          lambda v: v.tensor_copy(P.xbf[:, 0:1024], P.wstage[:, 0:1024]),
          wait=[("LW", lw), ("T", em.count("T"))], inc="V")
    vc = em.count("V")

    def trans(te):
        last = None
        for ic in range(NCH):
            last = te.transpose(P.pt[0][:, 128 * ic:128 * (ic + 1)],
                                P.xbf[:, 128 * ic:128 * (ic + 1)], P.ident[:])
        return last
    em.op("tensor", trans, wait=[("V", vc)], inc="T")
    tt = em.count("T")
    em.op("scalar", lambda s: s.copy(dst_ap, P.pt[0][:, 0:1024]),
          wait=[("T", tt)], inc="SC")
    return em.count("SC")


def emit_l1_attn(P: Prog):
    em = P.em
    emit_xT(P, wait_extra=[("V", em.count("V")), ("SC", em.count("SC")),
                           ("T", em.count("T"))])
    v_xt = em.count("V")

    wq_nat = P.di["L1_attn_Wq"]
    wk_nat = P.di["L1_attn_Wk"]
    wv_nat = P.di["L1_attn_Wv"]

    def qT_ap(c, sl=None):
        buf, off = (P.arnB, 2048 * c) if c < 4 else (P.arnD, 2048 * (c - 4))
        a = buf[:, off:off + 2048]
        return a if sl is None else a[:, sl]

    def v_ap(t):
        if t < 12:
            return P.wbf[:, 1024 * t:1024 * (t + 1)]
        if t < 15:
            return P.arnA[:, 1024 * (t - 12):1024 * (t - 11)]
        return P.arnC[:, 0:1024]

    for oc in range(NCH):
        scw = emit_wT_col(P, wq_nat, oc, P.arnC[:, 1024:2048], "wq")
        for ns in range(4):
            def qmm(te, oc=oc, ns=ns):
                last = None
                for ic in range(NCH):
                    last = te.matmul(
                        P.pb[ns % 2][:, 0:512],
                        P.arnC[:, 1024 + 128 * ic:1024 + 128 * (ic + 1)],
                        P.xT[:, 2048 * ic + 512 * ns:2048 * ic + 512 * (ns + 1)],
                        start=(ic == 0), stop=(ic == NCH - 1))
                return last
            w = [("SC", scw)]
            if ns >= 2 or oc > 0:
                w.append(("SC", em.count("SC")))
            em.op("tensor", qmm, wait=w, inc="T")
            tq = em.count("T")
            em.op("scalar",
                  lambda s, oc=oc, ns=ns: s.copy(
                      qT_ap(oc, slice(512 * ns, 512 * (ns + 1))),
                      P.pb[ns % 2][:, 0:512]),
                  wait=[("T", tq)], inc="SC")
    sc_q = em.count("SC")

    for oc in range(NCH):
        scw = emit_wT_col(P, wk_nat, oc, P.arnC[:, 1024:2048], "wk")
        for ns in range(4):
            def kmm(te, oc=oc, ns=ns):
                last = None
                for ic in range(NCH):
                    last = te.matmul(
                        P.pb[2 + ns % 2][:, 0:512],
                        P.arnC[:, 1024 + 128 * ic:1024 + 128 * (ic + 1)],
                        P.xT[:, 2048 * ic + 512 * ns:2048 * ic + 512 * (ns + 1)],
                        start=(ic == 0), stop=(ic == NCH - 1))
                return last
            w = [("SC", scw)]
            w.append(("LG", em.count("LG")))
            em.op("tensor", kmm, wait=w, inc="T")
            tk = em.count("T")
            em.op("scalar",
                  lambda s, ns=ns: s.copy(P.xbf[:, 1024 + 512 * (ns % 2):
                                                1024 + 512 * (ns % 2) + 512],
                                          P.pb[2 + ns % 2][:, 0:512]),
                  wait=[("T", tk)], inc="SC")
            sck = em.count("SC")
            em.dma("gpsimd",
                   [(P.d_kspill[128 * oc:128 * (oc + 1),
                                512 * ns:512 * (ns + 1)],
                     P.xbf[:, 1024 + 512 * (ns % 2):1024 + 512 * (ns % 2) + 512])],
                   inc="LG", wait=[("SC", sck)])
    lg_k = em.count("LG")

    for oc in range(NCH):
        scw = emit_wT_col(P, wv_nat, oc, P.arnC[:, 1024:2048], "wv")
        for t in range(NT):
            def vmm(te, oc=oc, t=t):
                last = None
                for ic in range(NCH):
                    last = te.matmul(
                        P.pb[4 + t % 2][:, 0:128],
                        P.xT[:, 2048 * ic + 128 * t:2048 * ic + 128 * (t + 1)],
                        P.arnC[:, 1024 + 128 * ic:1024 + 128 * (ic + 1)],
                        start=(ic == 0), stop=(ic == NCH - 1))
                return last
            w = [("SC", scw)]
            if t >= 2:
                w.append(("SC", em.count("SC")))
            em.op("tensor", vmm, wait=w, inc="T")
            tv = em.count("T")
            em.op("scalar",
                  lambda s, oc=oc, t=t: s.copy(
                      v_ap(t)[:, 128 * oc:128 * (oc + 1)],
                      P.pb[4 + t % 2][:, 0:128]),
                  wait=[("T", tv)], inc="SC")
    sc_v = em.count("SC")
    if P.dbg("L1QT", (128, 16384), BF16) is not None:
        em.tap_barrier(em.dma("sync", [
            (P.dbg_tensors["L1QT"][:, 0:8192], P.arnB[:, 0:8192]),
            (P.dbg_tensors["L1QT"][:, 8192:16384], P.arnD[:, 0:8192]),
        ], inc="ST", wait=[("SC", sc_q)]))
    if P.dbg("L1V", (128, 16384), BF16) is not None:
        em.tap_barrier(em.dma("sync", [
            (P.dbg_tensors["L1V"][:, 0:12288], P.wbf[:, 0:12288]),
            (P.dbg_tensors["L1V"][:, 12288:15360], P.arnA[:, 0:3072]),
            (P.dbg_tensors["L1V"][:, 15360:16384], P.arnC[:, 0:1024]),
        ], inc="ST", wait=[("SC", sc_v)]))
    return emit_l1_attn_part2(P, sc_q, sc_v, lg_k)


def emit_l1_attn_part2(P: Prog, sc_q, sc_v, lg_k):
    em = P.em

    def qT_ap(c):
        buf, off = (P.arnB, 2048 * c) if c < 4 else (P.arnD, 2048 * (c - 4))
        return buf[:, off:off + 2048]

    def v_ap(t):
        if t < 12:
            return P.wbf[:, 1024 * t:1024 * (t + 1)]
        if t < 15:
            return P.arnA[:, 1024 * (t - 12):1024 * (t - 11)]
        return P.arnC[:, 0:1024]

    def ot_ap(qb, c):
        if qb == 0:
            return P.xbf[:, 1024 + 128 * c:1024 + 128 * (c + 1)]
        if qb == 1:
            return P.arnC[:, 1024 + 128 * c:1024 + 128 * (c + 1)]
        return qT_ap(c)[:, 128 * (qb - 2):128 * (qb - 1)]

    kwin = P.wstage[:].bitcast(BF16)

    def kw_ap(kb):
        return kwin[:, 1024 * (kb % 4):1024 * (kb % 4) + 1024]

    lgk = {}
    for kb in (0, 1):
        lgk[kb] = em.dma("gpsimd",
                         [(kw_ap(kb).rearrange("p (c s) -> p c s", c=8),
                           P.d_kspill[:, 128 * kb:128 * (kb + 1)]
                           .rearrange("(c p) s -> p c s", p=128))],
                         inc="LG",
                         wait=[("LG", lg_k), ("V", em.count("V")),
                               ("T", em.count("T"))])

    for qb in range(NT):
        kbs = [kb for kb in (qb - 1, qb, qb + 1) if 0 <= kb < NT]
        if qb + 2 < NT:
            lgk[qb + 2] = em.dma(
                "gpsimd",
                [(kw_ap(qb + 2).rearrange("p (c s) -> p c s", c=8),
                  P.d_kspill[:, 128 * (qb + 2):128 * (qb + 3)]
                  .rearrange("(c p) s -> p c s", p=128))],
                inc="LG", wait=[("T", em.count("T"))])
        for pr in range(8):
            sc_exp = {}
            pv_done = {}
            for ik, kb in enumerate(kbs):
                slot = ik % 2
                rel = kb - qb
                w = [("LG", lgk[kb])]
                if ik >= 2:
                    w.append(("SC", sc_exp[ik - 2]))
                else:
                    w.append(("SC", em.count("SC")))
                if qb == 0 and pr == 0 and ik == 0:
                    w.append(("SC", max(sc_q, sc_v)))

                def score(te, qb=qb, pr=pr, kb=kb, slot=slot, rel=rel):
                    last = None
                    for h in range(2):
                        bank = P.pb[2 * slot + h]
                        H = 2 * pr + h
                        lhsT = kw_ap(kb)[64 * (H % 2):64 * (H % 2) + 64,
                                         128 * (H // 2):128 * (H // 2) + 128]
                        rhs = qT_ap(H // 2)[64 * (H % 2):64 * (H % 2) + 64,
                                            128 * qb:128 * (qb + 1)]
                        last = te.matmul(bank[:, 0:128], lhsT, rhs,
                                         start=True, stop=(rel == 0))
                    if rel != 0:
                        for h in range(2):
                            last = te.matmul(P.pb[2 * slot + h][:, 0:128],
                                             P.ident[:],
                                             P.mask[f"mask_l1a_{rel}"][:],
                                             start=False, stop=True)
                    return last
                em.op("tensor", score, wait=w, inc="T")
                ts = em.count("T")

                def do_exp(s, slot=slot):
                    last = None
                    for h in range(2):
                        last = s.activation(
                            out=P.xbf[:, 128 * (2 * h + slot):
                                  128 * (2 * h + slot + 1)],
                            in_=P.pb[2 * slot + h][:, 0:128], func=ACT.Exp,
                            scale=0.125)
                    return last
                we = [("T", ts)]
                if ik >= 2:
                    we.append(("T", pv_done.get(slot, 0)))
                em.op("scalar", do_exp, wait=we, inc="SC")
                sc_exp[ik] = em.count("SC")

                def obank(H):
                    bi = H // 7
                    if bi < 2:
                        return P.pb[4 + bi][:], 65 * (H % 7)
                    return P.pt[0][:].bitcast(F32), 65 * (H - 14)

                def do_pv(te, qb=qb, pr=pr, kb=kb, slot=slot, ik=ik,
                          nkb=len(kbs), obank=obank):
                    last = None
                    for h in range(2):
                        H = 2 * pr + h
                        bank, reg = obank(H)
                        ptile = P.xbf[:, 128 * (2 * h + slot):
                                      128 * (2 * h + slot + 1)]
                        last = te.matmul(
                            bank[:, reg:reg + 64], ptile,
                            v_ap(kb)[:, 64 * H:64 * (H + 1)],
                            start=(ik == 0 and H % 7 == 0 and H < 14),
                            stop=False)
                        last = te.matmul(
                            bank[:, reg + 64:reg + 65], ptile,
                            P.ones_col[:],
                            start=False,
                            stop=(ik == nkb - 1 and H in (6, 13, 15)))
                    return last
                em.op("tensor", do_pv, wait=[("SC", sc_exp[ik])], inc="T")
                pv_done[slot] = em.count("T")
        tpv = em.count("T")

        def recips(v):
            v.reciprocal(P.stat[:, 32:39],
                         P.pb[4][:, 0:455]
                         .rearrange("p (h c) -> p h c", h=7)[:, :, 64:65])
            v.reciprocal(P.stat[:, 39:46],
                         P.pb[5][:, 0:455]
                         .rearrange("p (h c) -> p h c", h=7)[:, :, 64:65])
            return v.reciprocal(P.stat[:, 46:48],
                                P.pt[0][:].bitcast(F32)[:, 0:130]
                                .rearrange("p (h c) -> p h c", h=2)[:, :, 64:65])
        em.op("vector", recips,
              wait=[("T", tpv), ("SC", em.count("SC"))], inc="V")
        vr = em.count("V")

        def o_evac(s, qb=qb):
            last = None
            for H in range(16):
                bi = H // 7
                if bi < 2:
                    bank, reg = P.pb[4 + bi][:], 65 * (H % 7)
                else:
                    bank, reg = P.pt[0][:].bitcast(F32), 65 * (H - 14)
                last = s.activation(
                    out=P.scr[:, 0:512].bitcast(BF16)[:, 64 * H:64 * (H + 1)],
                    in_=bank[:, reg:reg + 64], func=ACT.Identity,
                    scale=P.stat[:, 32 + H:33 + H])
            return last
        em.op("scalar", o_evac, wait=[("V", vr)], inc="SC")
        so = em.count("SC")

        for c in range(NCH):
            def otr(te, c=c):
                return te.transpose(
                    P.pt[c % 2][:, 0:128],
                    P.scr[:, 0:512].bitcast(BF16)[:, 128 * c:128 * (c + 1)],
                    P.ident[:])
            em.op("tensor", otr,
                  wait=[("SC", so)] if c == 0 else [("V", em.count("V"))],
                  inc="T")
            to = em.count("T")
            em.op("vector",
                  lambda v, qb=qb, c=c: v.tensor_copy(ot_ap(qb, c),
                                                      P.pt[c % 2][:, 0:128]),
                  wait=[("T", to)], inc="V")
    v_ot = em.count("V")
    return emit_l1_attn_part3(P, v_ot)


def emit_l1_attn_part3(P: Prog, v_ot):
    em = P.em
    wo_nat = P.di["L1_attn_Wo"]

    def qT_ap(c):
        buf, off = (P.arnB, 2048 * c) if c < 4 else (P.arnD, 2048 * (c - 4))
        return buf[:, off:off + 2048]

    def ot_ap(qb, c):
        if qb == 0:
            return P.xbf[:, 1024 + 128 * c:1024 + 128 * (c + 1)]
        if qb == 1:
            return P.arnC[:, 1024 + 128 * c:1024 + 128 * (c + 1)]
        return qT_ap(c)[:, 128 * (qb - 2):128 * (qb - 1)]

    for oc in range(NCH):
        emit_wT_col(P, wo_nat, oc, P.wbf[:, 1024 * oc:1024 * (oc + 1)], "wo")
    sc_wo = em.count("SC")

    for oc in range(NCH):
        for ns2 in range(2):
            b0, b1 = P.pb[2 * (ns2 % 2)], P.pb[2 * (ns2 % 2) + 1]

            def ymm(te, oc=oc, ns2=ns2, b0=b0, b1=b1):
                last = None
                for dc in range(NCH):
                    lhsT = P.wbf[:, 1024 * oc + 128 * dc:
                                 1024 * oc + 128 * (dc + 1)]
                    for half in range(2):
                        t0 = 1024 * ns2 + 512 * half
                        for tl in range(4):
                            qb = (t0 + 128 * tl) // 128
                            last = te.matmul(
                                (b0 if half == 0 else b1)[:, 128 * tl:128 * (tl + 1)],
                                lhsT, ot_ap(qb, dc),
                                start=(dc == 0 and tl == 0),
                                stop=(dc == NCH - 1 and tl == 3))
                return last
            w = [("SC", sc_wo), ("V", v_ot)] if (oc == 0 and ns2 == 0) else \
                [("V", em.count("V"))]
            em.op("tensor", ymm, wait=w, inc="T")
            ty = em.count("T")

            em.op("scalar", lambda s, b0=b0, b1=b1: (
                s.copy(P.arnC[:, 0:512], b0[:, 0:512]),
                s.copy(P.arnC[:, 512:1024], b1[:, 0:512]))[-1],
                wait=[("T", ty)], inc="SC")
            s2 = em.count("SC")
            for tl in range(8):
                def ytr(te, tl=tl):
                    return te.transpose(
                        P.pt[tl % 2][:, 0:128],
                        P.arnC[:, 128 * tl:128 * (tl + 1)], P.ident[:])
                em.op("tensor", ytr,
                      wait=[("SC", s2)] if tl == 0 else [("V", em.count("V"))],
                      inc="T")
                tyy = em.count("T")

                def yadd(v, oc=oc, ns2=ns2, tl=tl):
                    tglob = 8 * ns2 + tl
                    col = D * tglob + 128 * oc
                    return v.tensor_add(P.xs[:, col:col + 128],
                                        P.xs[:, col:col + 128],
                                        P.pt[tl % 2][:, 0:128])
                em.op("vector", yadd, wait=[("T", tyy)], inc="V")
    emit_ln(P, wait_v=em.count("V"))


def emit_round(P, src, ab, c, sgn, tscr, out_, wait_sc, scale=None,
               wait_v=()):
    em = P.em

    def sc_op(s):
        s.activation(out=sgn, in_=src, func=ACT.Sign)
        if scale is None:
            s.activation(out=ab, in_=src, func=ACT.Abs)
            return s.activation(out=c, in_=src, func=ACT.Abs)
        s.activation(out=ab, in_=src, func=ACT.Abs, scale=scale)
        return s.activation(out=c, in_=src, func=ACT.Abs, scale=scale)
    em.op("scalar", sc_op, wait=wait_sc, inc="SC")
    ssc = em.count("SC")

    def v_op(v):
        p = 64.0
        for _ in range(7):
            v.tensor_scalar(tscr, c, p, p, op0=ALU.is_ge, op1=ALU.mult)
            v.drain()
            v.tensor_sub(c, c, tscr)
            v.drain()
            p /= 2.0
        v.tensor_sub(ab, ab, c)
        v.drain()
        v.scalar_tensor_tensor(out=tscr, in0=c, scalar=0.5, in1=ab,
                               op0=ALU.is_gt, op1=ALU.add)
        v.drain()
        return v.tensor_tensor(out_, tscr, sgn, op=ALU.mult)
    em.op("vector", v_op, wait=[("SC", ssc)] + list(wait_v), inc="V")


def emit_l1_ffn(P: Prog):
    em = P.em

    wnames = (("L1_ffn_w_gate", P.d_wgT, 4096, 1024),
              ("L1_ffn_w_up", P.d_wuT, 4096, 1024),
              ("L1_ffn_w_down", P.d_wdT, 1024, 4096))
    swcols = {}
    t_prev = None
    sc_copy = []
    lg_tgts = []
    g = 0
    for wi, (nm, dT, R, Cc) in enumerate(wnames):
        w_dram = P.di[nm]
        nrt = R // 128
        CH = min(Cc, 2048)
        idxs = [(rt, ch) for rt in range(nrt) for ch in range(Cc // CH)]
        sc_tgts = []
        for idx, (rt, ch) in enumerate(idxs):
            slot = 2048 * (idx % 2)
            wait = []
            if idx < 2:
                wait = [("V", em.count("V")), ("SC", em.count("SC"))]
            else:
                wait = [("SC", sc_tgts[idx - 2])]
            lw = em.dma("sync", [(P.wstage[:, slot:slot + CH],
                                  w_dram[128 * rt:128 * (rt + 1),
                                         CH * ch:CH * (ch + 1)])],
                        inc="LW", wait=wait)
            em.op("scalar",
                  lambda s, idx=idx, slot=slot, CH=CH: s.activation(
                      out=P.arnB[:, 0:2 * CH].bitcast(F32),
                      in_=P.wstage[:, slot:slot + CH], func=ACT.Abs,
                      accum_out=P.stat[:, idx % 32:idx % 32 + 1]),
                  wait=[("LW", lw)], inc="SC")
            sc_tgts.append(em.count("SC"))
        ncol = len(idxs)
        assert ncol <= 32

        def part(v, ncol=ncol):
            v.drain()
            return v.reduce_sum(P.stat[:, 32:33], P.stat[:, 0:ncol], axis=AX.X)
        em.op("vector", part, wait=[("SC", em.count("SC"))], inc="V")
        vsum = em.count("V")
        em.op("tensor",
              lambda te: te.matmul(P.pb[5][0:1, 0:1], P.onescol_f32[:],
                                   P.stat[:, 32:33], start=True, stop=True),
              wait=[("V", vsum)], inc="T")
        tsum = em.count("T")

        def swmath(v, R=R, Cc=Cc):
            v.tensor_scalar_mul(P.scr[0:1, 0:1], P.pb[5][0:1, 0:1],
                                1.0 / (R * Cc))
            v.drain()
            v.tensor_scalar_max(P.scr[0:1, 0:1], P.scr[0:1, 0:1], 1e-5)
            v.drain()
            v.reciprocal(P.scr[0:1, 1:2], P.scr[0:1, 0:1])
            v.drain()
            v.tensor_scalar_mul(P.scr[0:1, 2:3], P.scr[0:1, 1:2], -1.0)
            v.drain()
            return v.tensor_copy(P.scr[0:1, 3:4], P.scr[0:1, 0:1])
        em.op("vector", swmath, wait=[("T", tsum)], inc="V")
        vsw = em.count("V")
        em.op("tensor",
              lambda te: te.matmul(P.pb[5][:, 0:3], P.ones_f32[:],
                                   P.scr[0:1, 1:4], start=True, stop=True),
              wait=[("V", vsw)], inc="T")
        tbc = em.count("T")
        em.op("scalar",
              lambda s, wi=wi: s.copy(P.stat[:, 34 + 3 * 0:37], P.pb[5][:, 0:3]),
              wait=[("T", tbc)], inc="SC")
        scbc = em.count("SC")
        em.op("vector",
              lambda v, wi=wi: v.tensor_copy(P.stat[:, 40 + wi:41 + wi],
                                             P.stat[:, 36:37]),
              wait=[("SC", scbc)], inc="V")

        v_tgts = {}
        ks = [(rt, ch) for rt in range(nrt) for ch in range((Cc + 2047) // 2048)]
        for k, (rt, ch) in enumerate(ks):
            Cs = min(2048, Cc - 2048 * ch)
            slot = 2048 * (k % 2)
            wait = ([("V", em.count("V")), ("T", em.count("T"))] if k < 2
                    else [("V", v_tgts[k - 2])])
            lw = em.dma("sync", [(P.wstage[:, slot:slot + Cs],
                                  w_dram[128 * rt:128 * (rt + 1),
                                         2048 * ch:2048 * ch + Cs])],
                        inc="LW", wait=wait)

            def quant(v, Cs=Cs, slot=slot):
                v.tensor_scalar(P.arnB[:, 0:2 * Cs].bitcast(F32),
                                P.wstage[:, slot:slot + Cs],
                                P.stat[:, 34:35], 0.5,
                                op0=ALU.mult, op1=ALU.is_gt)
                v.tensor_scalar(P.wstage[:, slot:slot + Cs],
                                P.wstage[:, slot:slot + Cs],
                                P.stat[:, 35:36], 0.5,
                                op0=ALU.mult, op1=ALU.is_gt)
                v.drain()
                return v.tensor_sub(P.arnC[:, 0:Cs],
                                    P.arnB[:, 0:2 * Cs].bitcast(F32),
                                    P.wstage[:, slot:slot + Cs])
            qw = [("LW", lw), ("SC", scbc)]
            if t_prev is not None:
                qw.append(("T", t_prev))
            em.op("vector", quant, wait=qw, inc="V")
            v_tgts[k] = em.count("V")
            nb = Cs // 128
            for b0 in range(0, nb, 8):
                xslot = 1024 * (g % 2)

                def tr(te, b0=b0):
                    last = None
                    for j in range(8):
                        last = te.transpose(
                            P.pt[0][:, 128 * j:128 * (j + 1)],
                            P.arnC[:, 128 * (b0 + j):128 * (b0 + j + 1)],
                            P.ident[:])
                    return last
                tw = [("V", v_tgts[k])]
                if sc_copy:
                    tw.append(("SC", sc_copy[-1]))
                em.op("tensor", tr, wait=tw, inc="T")
                t_prev = em.count("T")
                cw = [("T", t_prev)]
                if g >= 2:
                    cw.append(("LG", lg_tgts[g - 2]))
                em.op("scalar",
                      lambda s, xslot=xslot: s.copy(
                          P.xbf[:, xslot:xslot + 1024], P.pt[0][:, 0:1024]),
                      wait=cw, inc="SC")
                sc_copy.append(em.count("SC"))
                em.dma("gpsimd",
                       [(dT[2048 * ch + 128 * b0:2048 * ch + 128 * b0 + 1024,
                            128 * rt:128 * (rt + 1)]
                         .rearrange("(c p) s -> p c s", p=128),
                         P.xbf[:, xslot:xslot + 1024]
                         .rearrange("p (c s) -> p c s", c=8))],
                       inc="LG", wait=[("SC", sc_copy[-1])])
                lg_tgts.append(em.count("LG"))
                g += 1
    lg_w = em.count("LG")
    if P.dbg("WST", (128, 64), F32) is not None:
        em.tap_barrier(em.dma("sync", [(P.dbg_tensors["WST"], P.stat[:])],
                              inc="ST", wait=[("V", em.count("V")),
                                              ("LG", lg_w)]))
    for nm, dT, sh in (("WGT", P.d_wgT, (D, 4096)), ("WUT", P.d_wuT, (D, 4096)),
                       ("WDT", P.d_wdT, (4096, D))):
        if P.dbg(nm, sh, BF16) is not None:
            em.dma("sync", [(P.dbg_tensors[nm], dT[:, :])], inc="ST",
                   wait=[("LG", lg_w)])
    return emit_l1_ffn_part2(P, lg_w)


def emit_l1_ffn_part2(P: Prog, lg_w):
    em = P.em

    for t in range(NT):
        w = [("LG", lg_w)] if t == 0 else []
        em.op("vector", lambda v, t=t: v.reduce_max(
            P.stat[:, 0 + t:1 + t], P.xs[:, D * t:D * (t + 1)],
            axis=AX.X, apply_absolute_value=True), wait=w, inc="V")
    vam = em.count("V")

    def sxmath(v):
        v.drain()
        v.tensor_scalar_max(P.stat[:, 0:16], P.stat[:, 0:16], 1e-5)
        v.drain()
        v.tensor_scalar_mul(P.stat[:, 16:32], P.stat[:, 0:16], 1.0 / 127.0)
        v.drain()
        v.reciprocal(P.stat[:, 0:16], P.stat[:, 16:32])
        v.drain()
        v.tensor_tensor(P.stat[:, 48:64], P.stat[:, 16:32],
                        P.stat[:, 40:41].broadcast_to([128, 16]), op=ALU.mult)
        v.drain()
        return v.tensor_tensor(P.stat[:, 16:32], P.stat[:, 16:32],
                               P.stat[:, 41:42].broadcast_to([128, 16]),
                               op=ALU.mult)
    em.op("vector", sxmath, wait=[("V", vam)], inc="V")
    v_sx = em.count("V")

    def xqT_ap(c):
        buf, off = (P.arnB, 2048 * c) if c < 4 else (P.arnD, 2048 * (c - 4))
        return buf[:, off:off + 2048]

    for t in range(NT):
        em.op("gpsimd", lambda g, t=t: g.tensor_tensor(
            P.wstage[:, 0:1024], P.xs[:, D * t:D * (t + 1)],
            P.stat[:, 0 + t:1 + t].broadcast_to([128, 1024]), op=ALU.mult),
            wait=[("V", v_sx), ("SC", em.count("SC")),
                  ("LG", em.count("LG"))], inc="GP")
        w = [("V", em.count("V")), ("GP", em.count("GP"))]
        if t > 0:
            w.append(("T", em.count("T")))
        emit_round(P, src=P.wstage[:, 0:1024],
                   ab=P.wstage[:, 1024:2048], c=P.wstage[:, 2048:3072],
                   sgn=P.wstage[:, 3072:4096], tscr=P.scr[:, 0:1024],
                   out_=P.arnC[:, 0:1024], wait_sc=w)
        vxq = em.count("V")

        def tr(te, t=t):
            last = None
            for j in range(8):
                last = te.transpose(P.pt[0][:, 128 * j:128 * (j + 1)],
                                    P.arnC[:, 128 * j:128 * (j + 1)],
                                    P.ident[:])
            return last
        em.op("tensor", tr, wait=[("V", vxq)], inc="T")
        ttr = em.count("T")

        def evac(v, t=t):
            v.tensor_copy(
                P.arnB[:, 0:8192].rearrange("p (c s) -> p c s", c=4)
                [:, :, 128 * t:128 * (t + 1)],
                P.pt[0][:, 0:512].rearrange("p (c s) -> p c s", c=4))
            return v.tensor_copy(
                P.arnD[:, 0:8192].rearrange("p (c s) -> p c s", c=4)
                [:, :, 128 * t:128 * (t + 1)],
                P.pt[0][:, 512:1024].rearrange("p (c s) -> p c s", c=4))
        em.op("vector", evac, wait=[("T", ttr)], inc="V")
    v_xq = em.count("V")
    if P.dbg("XQTB", (128, 8192), BF16) is not None:
        P.dbg("XQTD", (128, 8192), BF16)
        em.dma("sync", [(P.dbg_tensors["XQTB"], P.arnB[:, 0:8192])], inc="ST",
               wait=[("V", v_xq)])
        em.tap_barrier(em.dma("sync",
               [(P.dbg_tensors["XQTD"], P.arnD[:, 0:8192])],
               inc="ST", wait=[("V", v_xq)]))
    if P.dbg("SX", (128, 64), F32) is not None:
        em.dma("sync", [(P.dbg_tensors["SX"], P.stat[:])], inc="ST",
               wait=[("V", v_xq)])
    return emit_l1_ffn_part3(P, v_xq)




def emit_l1_ffn_part3(P: Prog, v_xq):
    em = P.em

    def xqT_ap(c):
        buf, off = (P.arnB, 2048 * c) if c < 4 else (P.arnD, 2048 * (c - 4))
        return buf[:, off:off + 2048]

    em.op("vector", lambda v: v.memset(P.stat[:, 0:16], 0.0),
          wait=[("V", v_xq)], inc="V")
    zm_v = []
    lg_sp = []
    anF0 = P.arnA.bitcast(F32)
    for j in range(8):
        lwg = em.dma("sync", [(P.wbf[:, 0:4096]
                               .rearrange("p (c s) -> p c s", c=8),
                               P.d_wgT[:, 512 * j:512 * (j + 1)]
                               .rearrange("(c p) s -> p c s", p=128)),
                              (P.wbf[:, 4096:8192]
                               .rearrange("p (c s) -> p c s", c=8),
                               P.d_wuT[:, 512 * j:512 * (j + 1)]
                               .rearrange("(c p) s -> p c s", p=128))],
                     inc="LW", wait=[("T", em.count("T")),
                                     ("V", em.count("V"))])
        for t in range(NT):
            k = 16 * j + t
            bg, bu = P.pb[2 * (t % 2)], P.pb[2 * (t % 2) + 1]
            w = [("V", zm_v[k - 2])] if k >= 2 else [("V", em.count("V"))]
            if t == 0:
                w.append(("LW", lwg))

            def gumm(te, t=t, bg=bg, bu=bu):
                last = None
                for ic in range(NCH):
                    lhsT = xqT_ap(ic)[:, 128 * t:128 * (t + 1)]
                    last = te.matmul(bg[:, 0:512], lhsT,
                                     P.wbf[:, 512 * ic:512 * (ic + 1)],
                                     start=(ic == 0), stop=(ic == NCH - 1))
                    last = te.matmul(bu[:, 0:512], lhsT,
                                     P.wbf[:, 4096 + 512 * ic:4096 + 512 * (ic + 1)],
                                     start=(ic == 0), stop=(ic == NCH - 1))
                return last
            em.op("tensor", gumm, wait=w, inc="T")
            tg = em.count("T")

            sslot = P.scr[:, 512 * (t % 2):512 * (t % 2) + 512]
            aslot = anF0[:, 512 * (k % 2):512 * (k % 2) + 512]

            def silu(s, t=t, bg=bg, sslot=sslot):
                return s.activation(out=sslot, in_=bg[:, 0:512],
                                    func=ACT.Silu,
                                    scale=P.stat[:, 48 + t:49 + t])
            em.op("scalar", silu, wait=[("T", tg)], inc="SC")
            ssl = em.count("SC")

            def zmul(v, t=t, bu=bu, sslot=sslot, aslot=aslot):
                v.scalar_tensor_tensor(
                    out=aslot,
                    in0=bu[:, 0:512], scalar=P.stat[:, 16 + t:17 + t],
                    in1=sslot, op0=ALU.mult, op1=ALU.mult)
                v.drain()
                v.reduce_max(P.scr[:, 1024:1025], aslot,
                             axis=AX.X, apply_absolute_value=True)
                v.drain()
                return v.tensor_tensor(P.stat[:, 0 + t:1 + t],
                                       P.stat[:, 0 + t:1 + t],
                                       P.scr[:, 1024:1025], op=ALU.max)
            zw = [("SC", ssl)]
            if k >= 2:
                zw.append(("LG", lg_sp[k - 2]))
            em.op("vector", zmul, wait=zw, inc="V")
            vz = em.count("V")
            zm_v.append(vz)
            em.dma("gpsimd",
                   [(P.d_zspill[128 * t:128 * (t + 1), 512 * j:512 * (j + 1)],
                     aslot)],
                   inc="LG", wait=[("V", vz)])
            lg_sp.append(em.count("LG"))
    lg_z = em.count("LG")
    if P.dbg("ZSP", (S, 4096), F32) is not None:
        em.tap_barrier(em.dma("sync", [(P.dbg_tensors["ZSP"], P.d_zspill[:, :])],
               inc="ST", wait=[("LG", lg_z)]))

    def szmath(v):
        v.drain()
        v.tensor_scalar_max(P.stat[:, 0:16], P.stat[:, 0:16], 1e-5)
        v.drain()
        v.tensor_scalar_mul(P.stat[:, 16:32], P.stat[:, 0:16], 1.0 / 127.0)
        v.drain()
        v.reciprocal(P.stat[:, 48:64], P.stat[:, 16:32])
        v.drain()
        return v.tensor_tensor(P.stat[:, 16:32], P.stat[:, 16:32],
                               P.stat[:, 42:43].broadcast_to([128, 16]),
                               op=ALU.mult)
    em.op("vector", szmath, inc="V")
    v_sz = em.count("V")
    if P.dbg("ZST", (128, 64), F32) is not None:
        em.tap_barrier(em.dma("sync", [(P.dbg_tensors["ZST"], P.stat[:])],
                              inc="ST", wait=[("V", v_sz)]))

    xbF = P.xbf.bitcast(F32)
    anF = P.arnA.bitcast(F32)
    zsets = [dict(ab=P.scr[:, 0:512], c=xbF[:, 0:512],
                  sgn=P.scr[:, 512:1024], tscr=xbF[:, 512:1024]),
             dict(ab=anF[:, 0:512], c=anF[:, 512:1024],
                  sgn=anF[:, 1024:1536], tscr=P.pb[4][:, 0:512])]
    zq_v = []
    zq_t = []
    zq_c = []
    q = 0
    for tq in range(4):
        for ti in range(4):
            t = 4 * tq + ti
            lz = em.dma("sync", [(P.wstage[:, 0:4096],
                                  P.d_zspill[128 * t:128 * (t + 1), :])],
                        inc="LW", wait=[("LG", lg_z), ("V", em.count("V")),
                                        ("T", em.count("T")),
                                        ("GP", em.count("GP")),
                                        ("SC", em.count("SC"))])
            for sl in range(8):
                srcw = P.wstage[:, 512 * sl:512 * (sl + 1)]
                em.op("gpsimd", lambda g_, srcw=srcw, t=t: g_.tensor_tensor(
                    srcw, srcw,
                    P.stat[:, 48 + t:49 + t].broadcast_to([128, 512]),
                    op=ALU.mult),
                    wait=[("LW", lz), ("V", v_sz)], inc="GP")
                wsc = [("GP", em.count("GP")),
                       ("V", zq_v[q - 2] if q >= 2 else em.count("V"))]
                wv = [("T", zq_t[q - 2])] if q >= 2 else []
                st = zsets[q % 2]
                emit_round(P, src=srcw, ab=st["ab"], c=st["c"],
                           sgn=st["sgn"], tscr=st["tscr"],
                           out_=P.arnC[:, 512 * (q % 2):512 * (q % 2) + 512],
                           wait_sc=wsc, wait_v=wv)
                zq_v.append(em.count("V"))

                def ztr(te, q=q):
                    last = None
                    for jj in range(4):
                        last = te.transpose(
                            P.pt[q % 2][:, 128 * jj:128 * (jj + 1)],
                            P.arnC[:, 512 * (q % 2) + 128 * jj:
                                    512 * (q % 2) + 128 * (jj + 1)],
                            P.ident[:])
                    return last
                tw = [("V", zq_v[-1])]
                if q >= 2:
                    tw.append(("SC", zq_c[q - 2]))
                em.op("tensor", ztr, wait=tw, inc="T")
                zq_t.append(em.count("T"))
                em.op("scalar",
                      lambda s_, ti=ti, sl=sl, q=q: s_.copy(
                          (P.arnB if ti < 2 else P.arnD)
                          [:, 4096 * (ti % 2) + 512 * sl:
                           4096 * (ti % 2) + 512 * (sl + 1)],
                          P.pt[q % 2][:, 0:512]),
                      wait=[("T", zq_t[-1])], inc="SC")
                zq_c.append(em.count("SC"))
                q += 1
        v_zq = em.count("V")

        def obank(ti, half):
            if ti < 3:
                return P.pb[2 * ti + half][:, 0:512]
            return P.pt[half].bitcast(F32)[:, 0:512]
        lwd = [em.dma("sync", [(P.xbf[:, 0:1024], P.d_wdT[0:128, :])],
                      inc="LW", wait=[("T", em.count("T")),
                                      ("SC", em.count("SC"))])]
        for c in range(32):
            if c + 1 < 32:
                h = 1024 * ((c + 1) % 2)
                lwd.append(em.dma(
                    "sync", [(P.xbf[:, h:h + 1024],
                              P.d_wdT[128 * (c + 1):128 * (c + 2), :])],
                    inc="LW", wait=[("T", em.count("T"))]))
            for ti in range(4):
                def dmm(te, c=c, ti=ti):
                    zq = (P.arnB if ti < 2 else P.arnD)
                    lhsT = zq[:, 4096 * (ti % 2):4096 * (ti % 2) + 4096] \
                        .rearrange("p (cc s) -> p cc s", cc=32)[:, c, :]
                    h = 1024 * (c % 2)
                    te.matmul(obank(ti, 0), lhsT, P.xbf[:, h:h + 512],
                              start=(c == 0), stop=(c == 31))
                    return te.matmul(obank(ti, 1), lhsT,
                                     P.xbf[:, h + 512:h + 1024],
                                     start=(c == 0), stop=(c == 31))
                em.op("tensor", dmm, wait=[("LW", lwd[c])] if ti == 0 else [],
                      inc="T")
        td = em.count("T")
        if tq == 0 and P.dbg("ZQT0B", (128, 8192), BF16) is not None:
            P.dbg("ZQT0D", (128, 8192), BF16)
            em.dma("sync", [(P.dbg_tensors["ZQT0B"], P.arnB[:, 0:8192])],
                   inc="ST", wait=[("T", td)])
            em.tap_barrier(em.dma("sync",
                   [(P.dbg_tensors["ZQT0D"], P.arnD[:, 0:8192])],
                   inc="ST", wait=[("T", td)]))
        for ti in range(4):
            t = 4 * tq + ti

            def yadd(v, t=t, ti=ti):
                v.scalar_tensor_tensor(
                    out=P.xs[:, D * t:D * t + 512],
                    in0=obank(ti, 0),
                    scalar=P.stat[:, 16 + t:17 + t],
                    in1=P.xs[:, D * t:D * t + 512],
                    op0=ALU.mult, op1=ALU.add)
                v.drain()
                return v.scalar_tensor_tensor(
                    out=P.xs[:, D * t + 512:D * (t + 1)],
                    in0=obank(ti, 1),
                    scalar=P.stat[:, 16 + t:17 + t],
                    in1=P.xs[:, D * t + 512:D * (t + 1)],
                    op0=ALU.mult, op1=ALU.add)
            em.op("vector", yadd, wait=[("T", td)], inc="V")
    if P.dbg("Y4", (128, NT * D), F32) is not None:
        em.tap_barrier(em.dma("sync", [(P.dbg_tensors["Y4"], P.xs[:])],
                              inc="ST", wait=[("V", em.count("V"))]))
    emit_ln(P, wait_v=em.count("V"))
